# version 07
# baseline (speedup 1.0000x reference)
import numpy as np
import ml_dtypes
import concourse.bass as bass
import concourse.mybir as mybir


F32 = mybir.dt.float32
BF16 = mybir.dt.bfloat16
F16 = mybir.dt.float16
I32 = mybir.dt.int32
I8 = mybir.dt.int8
AF = mybir.ActivationFunctionType
ALU = mybir.AluOpType
AX = mybir.AxisListType

ENGINES = ("pe", "act", "dve", "pool", "sp")
ENG_ATTR = {"pe": "tensor", "act": "scalar", "dve": "vector", "pool": "gpsimd", "sp": "sync"}


class Op:
    __slots__ = ("eng", "fn", "deps", "idx", "val", "needed", "tag", "sem", "dval")

    def __init__(self, eng, fn, deps, tag):
        self.eng = eng
        self.fn = fn
        self.deps = deps
        self.idx = -1
        self.val = None
        self.needed = False
        self.tag = tag
        self.sem = None
        self.dval = None


class DmaSem:
    __slots__ = ("name", "count", "handle", "queue", "last_op", "detached")

    def __init__(self, name):
        self.name = name
        self.count = 0
        self.handle = None
        self.queue = None
        self.last_op = None
        self.detached = False


class KB:
    def __init__(self, nc: bass.Bass):
        self.nc = nc
        self.ops = {e: [] for e in ENGINES}
        self.dma_sems = []
        self.ctx = None
        self.eng_sem = {}
        self.final_deps = []
        self.phase_deps = []
        self.last_reg = {e: None for e in ENGINES}

    def barrier(self):
        toks = []
        for e in ENGINES:
            if self.last_reg[e] is not None:
                toks.append(self.last_reg[e])
        for s in self.dma_sems:
            if s.last_op is not None and not s.detached:
                toks.append(s.last_op)
        self.phase_deps = toks

    def op(self, eng, fn, deps=(), tag="", detach=False):
        pd = [] if detach else self.phase_deps
        o = Op(eng, fn, [d for d in deps if d is not None] + pd, tag)
        for d in o.deps:
            d.needed = True
        self.ops[eng].append(o)
        if not detach:
            self.last_reg[eng] = o
        return o

    def dma_sem(self, name):
        s = DmaSem(name)
        self.dma_sems.append(s)
        return s

    def dma(self, q, sem, fn, deps=(), tag="", detach=False):
        pd = [] if detach else self.phase_deps
        o = Op(q, fn, [d for d in deps if d is not None] + pd, tag)
        for d in o.deps:
            d.needed = True
        if sem.queue is None:
            sem.queue = q
        assert sem.queue == q, f"DmaSem {sem.name} used from {sem.queue} and {q}"
        if detach:
            sem.detached = True
        o.sem = sem
        sem.count += 1
        sem.last_op = o
        o.dval = 16 * sem.count
        self.ops[q].append(o)
        if not detach:
            self.last_reg[q] = o
        return o

    def finish(self, deps):
        self.final_deps.extend([d for d in deps if d is not None])

    def emit(self, ctx):
        nc = self.nc
        e = ctx.enter_context
        for eng in ENGINES:
            self.eng_sem[eng] = e(nc.semaphore(f"sem_{eng}"))
        for s in self.dma_sems:
            s.handle = e(nc.semaphore(f"dsem_{s.name}"))
        for eng in ENGINES:
            v = 0
            for o in self.ops[eng]:
                if o.sem is None and o.needed:
                    v += 1
                    o.val = v
        block = e(nc.Block())

        def tok_sem_val(d):
            if d.sem is not None:
                return d.sem.handle, d.dval
            assert d.val is not None, f"dep on un-inc'd op {d.tag}"
            return self.eng_sem[d.eng], d.val

        for eng in ENGINES:
            getattr(block, ENG_ATTR[eng])(self._mkbody(eng, tok_sem_val))

    def _mkbody(self, eng_name, tok_sem_val):
        def body(engobj):
            water = {}
            for o in self.ops[eng_name]:
                for d in o.deps:
                    sem, val = tok_sem_val(d)
                    if water.get(id(sem), -1) < val:
                        engobj.wait_ge(sem, val)
                        water[id(sem)] = val
                if o.sem is not None and o.dval > 16:
                    if water.get(id(o.sem.handle), -1) < o.dval - 16:
                        engobj.wait_ge(o.sem.handle, o.dval - 16)
                        water[id(o.sem.handle)] = o.dval - 16
                try:
                    instr = o.fn(engobj)
                except Exception as _ex:
                    raise RuntimeError(f"op failed tag={o.tag} eng={eng_name}") from _ex
                if o.sem is not None:
                    instr.then_inc(o.sem.handle, 16)
                elif o.needed:
                    instr.then_inc(self.eng_sem[eng_name], 1)
            if eng_name == "sp":
                for d in self.final_deps:
                    sem, val = tok_sem_val(d)
                    if water.get(id(sem), -1) < val:
                        engobj.wait_ge(sem, val)
                        water[id(sem)] = val
        return body




def cdiv(a, b):
    return (a + b - 1) // b


class PsumPool:

    def __init__(self, nc, ctx):
        self.db = [ctx.enter_context(nc.psum_tensor(f"pdb{i}", [128, 1024], F32))
                   for i in range(4)]
        self.rtok = [None] * 4
        self.wtok = [None] * 4


class Consts:
    def __init__(self, kb, nc, ctx, ident_dram=None):
        e = ctx.enter_context
        self.ones_row_f32 = e(nc.sbuf_tensor("c_ones_row_f32", [1, 128], F32))
        self.ones_col_bf16 = e(nc.sbuf_tensor("c_ones_col_bf16", [128, 1], BF16))
        t1 = kb.op("pool", lambda g: g.memset(self.ones_row_f32[:], 1.0))
        t2 = kb.op("pool", lambda g: g.memset(self.ones_col_bf16[:], 1.0))
        self.tok = [t1, t2]
        self.ident_f32 = None
        self.ident_bf16 = None
        if ident_dram is not None:
            self.ident_f32 = e(nc.sbuf_tensor("c_ident_f32", [128, 128], F32))
            self.ident_bf16 = e(nc.sbuf_tensor("c_ident_bf16", [128, 128], BF16))
            s = kb.dma_sem("cid")
            s2 = kb.dma_sem("cid2")
            d1 = kb.dma("sp", s, lambda sy: sy.dma_start(out=self.ident_f32[:], in_=ident_dram[:]))
            d2 = kb.dma("pool", s2, lambda g: g.dma_start(out=self.ident_bf16[:], in_=ident_dram[:]))
            self.tok += [d1, d2]


def bcast_row(kb, consts, pool, db, row_ap, T, deps, tag="bc"):
    toks = []
    nt = cdiv(T, 512)
    for i in range(nt):
        n = min(512, T - 512 * i)
        dep = list(deps) + consts.tok
        if pool.rtok[db] is not None:
            dep.append(pool.rtok[db])
        t = kb.op("pe", lambda t_, i=i, n=n: t_.matmul(
            out=pool.db[db][:, 512 * i:512 * i + n], lhsT=consts.ones_row_f32[:],
            rhs=row_ap[:, 512 * i:512 * i + n], start=True, stop=True),
            deps=dep, tag=tag)
        toks.append(t)
    pool.wtok[db] = toks[-1]
    return toks


def emit_ln_fm(kb, nc, ctx, consts, pool, x_at, C, T, xln, eps, deps, name,
               rms_only=False, gamma=None, db_stats=(0, 1), db_ab=(2, 3),
               x_done=None):
    e = ctx.enter_context
    D = C * 128
    xb = e(nc.sbuf_tensor(f"{name}_xb", [128, T], BF16))
    sq = e(nc.sbuf_tensor(f"{name}_sq", [128, T], BF16))
    rows = e(nc.sbuf_tensor(f"{name}_rows", [1, 2 * T], F32))
    tmp = e(nc.sbuf_tensor(f"{name}_tmp", [128, T], F32))
    s_mean = rows[:, 0:T]
    s_var = rows[:, T:2 * T]
    s_rstd = rows[:, T:2 * T]
    s_b = tmp[0:1, 0:T]
    nt = cdiv(T, 512)
    db_sum, db_ssq = db_stats
    db_A, db_B = db_ab

    mm_last = []
    prev_xb = None
    prev_sq = None
    rd_xb = []
    rd_sq = []
    for c in range(C):
        xc, xdep = x_at(c)
        tb = None
        if not rms_only:
            tb = kb.op("dve", lambda v, xc=xc: v.tensor_copy(out=xb[:], in_=xc),
                       deps=list(deps) + xdep + ([prev_xb] if prev_xb else []) + rd_xb,
                       tag=f"{name}xb{c}")
        ts = kb.op("act", lambda s, xc=xc: s.activation(out=sq[:], in_=xc, func=AF.Square),
                   deps=list(deps) + xdep + ([prev_sq] if prev_sq else []) + rd_sq,
                   tag=f"{name}sq{c}")
        rd_xb = []
        rd_sq = []
        for i in range(nt):
            n = min(512, T - 512 * i)
            if not rms_only:
                dep = [tb] + consts.tok
                if c == 0 and pool.rtok[db_sum] is not None:
                    dep.append(pool.rtok[db_sum])
                t1 = kb.op("pe", lambda t_, i=i, c=c, n=n: t_.matmul(
                    out=pool.db[db_sum][0:1, 512 * i:512 * i + n], lhsT=consts.ones_col_bf16[:],
                    rhs=xb[:, 512 * i:512 * i + n], start=(c == 0), stop=(c == C - 1)),
                    deps=dep, tag=f"{name}mmS")
                rd_xb.append(t1)
                if c == C - 1:
                    mm_last.append(t1)
                    pool.wtok[db_sum] = t1
            dep = [ts] + consts.tok
            if c == 0 and pool.rtok[db_ssq] is not None:
                dep.append(pool.rtok[db_ssq])
            t2 = kb.op("pe", lambda t_, i=i, c=c, n=n: t_.matmul(
                out=pool.db[db_ssq][0:1, 512 * i:512 * i + n], lhsT=consts.ones_col_bf16[:],
                rhs=sq[:, 512 * i:512 * i + n], start=(c == 0), stop=(c == C - 1)),
                deps=dep, tag=f"{name}mmQ")
            rd_sq.append(t2)
            if c == C - 1:
                mm_last.append(t2)
                pool.wtok[db_ssq] = t2
        prev_xb = tb
        prev_sq = ts
        if x_done is not None:
            x_done(c, [t for t in (tb, ts) if t is not None])

    if not rms_only:
        tm = kb.op("dve", lambda v: v.tensor_single_scalar(
            out=s_mean[:], in_=pool.db[db_sum][0:1, :T], scalar=1.0 / D, op=ALU.mult),
            deps=mm_last, tag=f"{name}mean")
        pool.rtok[db_sum] = tm
        tv1 = kb.op("dve", lambda v: v.tensor_single_scalar(
            out=s_var[:], in_=pool.db[db_ssq][0:1, :T], scalar=1.0 / D, op=ALU.mult),
            deps=mm_last, tag=f"{name}v1")
        pool.rtok[db_ssq] = tv1
        tv2 = kb.op("dve", lambda v: v.scalar_tensor_tensor(
            out=s_b[:], in0=s_mean[:], scalar=1.0, in1=s_mean[:], op0=ALU.mult, op1=ALU.mult),
            deps=[tm], tag=f"{name}m2")
        tv = kb.op("dve", lambda v: v.scalar_tensor_tensor(
            out=s_var[:], in0=s_var[:], scalar=float(eps), in1=s_b[:],
            op0=ALU.add, op1=ALU.subtract), deps=[tv1, tv2], tag=f"{name}v3")
    else:
        tv = kb.op("dve", lambda v: v.tensor_scalar(
            out=s_var[:], in0=pool.db[db_ssq][0:1, :T], scalar1=1.0 / D, scalar2=float(eps),
            op0=ALU.mult, op1=ALU.add), deps=mm_last, tag=f"{name}v1")
        pool.rtok[db_ssq] = tv
    tl = kb.op("act", lambda s: s.activation(out=s_rstd[:], in_=s_var[:], func=AF.Ln),
               deps=[tv], tag=f"{name}ln")
    tr = kb.op("act", lambda s: s.activation(out=s_rstd[:], in_=s_rstd[:], func=AF.Exp, scale=-0.5),
               deps=[tl], tag=f"{name}exp")
    toks_A = bcast_row(kb, consts, pool, db_A, s_rstd, T, [tr], tag=f"{name}bA")
    toks_B = []
    if not rms_only:
        tb2 = kb.op("dve", lambda v: v.scalar_tensor_tensor(
            out=s_b[:], in0=s_mean[:], scalar=-1.0, in1=s_rstd[:], op0=ALU.mult, op1=ALU.mult),
            deps=[tr], tag=f"{name}negm")
        toks_B = bcast_row(kb, consts, pool, db_B, s_b, T, [tb2], tag=f"{name}bB")

    out_toks = []
    prev = None
    for c in range(C):
        xc, xdep = x_at(c)
        t1 = kb.op("dve", lambda v, xc=xc: v.tensor_tensor(
            out=tmp[:], in0=xc, in1=pool.db[db_A][:, :T], op=ALU.mult),
            deps=list(deps) + xdep + toks_A + toks_B + ([prev] if prev else []),
            tag=f"{name}o1_{c}")
        if x_done is not None:
            x_done(c, [t1])
        if not rms_only:
            t2 = kb.op("dve", lambda v, c=c: v.tensor_tensor(
                out=xln[:, c, :], in0=tmp[:], in1=pool.db[db_B][:, :T], op=ALU.add),
                deps=[t1] + toks_B, tag=f"{name}o2_{c}")
        else:
            t2 = kb.op("dve", lambda v, c=c: v.tensor_copy(out=xln[:, c, :], in_=tmp[:]),
                       deps=[t1], tag=f"{name}o2_{c}")
        if gamma is not None:
            t2 = kb.op("dve", lambda v, c=c: v.tensor_scalar(
                out=xln[:, c, :], in0=xln[:, c, :], scalar1=gamma[:, c:c + 1],
                scalar2=None, op0=ALU.mult), deps=[t2], tag=f"{name}og{c}")
        prev = t2
        out_toks.append(t2)
    pool.rtok[db_A] = out_toks[-1]
    if not rms_only:
        pool.rtok[db_B] = out_toks[-1]
    return out_toks


def emit_dequant(kb, nc, ctx, consts, pool, pk_dram, sc_dram, out_dram, O, W, name,
                 db_tp=0):
    e = ctx.enter_context
    IN = 8 * W
    G = IN // 128
    n_ot = O // 128
    assert O % 128 == 0 and IN % 128 == 0

    sc_sb = e(nc.sbuf_tensor(f"{name}_sc", [G, O], F32))
    s_sc = kb.dma_sem(f"{name}sc")
    t_sc = kb.dma("pool", s_sc, lambda g: g.dma_start(out=sc_sb[:], in_=sc_dram[:]),
                  tag=f"{name}scdma")
    pk = [e(nc.sbuf_tensor(f"{name}_pk{i}", [128, W], I32)) for i in range(2)]
    s_pk = [kb.dma_sem(f"{name}pk{i}") for i in range(2)]
    sct = e(nc.sbuf_tensor(f"{name}_sct", [128, G], F32))
    tm1 = e(nc.sbuf_tensor(f"{name}_t1", [128, W], I32))
    tm2 = e(nc.sbuf_tensor(f"{name}_t2", [128, W], I32))
    tu = e(nc.sbuf_tensor(f"{name}_tu", [128, W], F32))
    tq = e(nc.sbuf_tensor(f"{name}_tq", [128, W], F32))
    tsg = e(nc.sbuf_tensor(f"{name}_ts", [128, W], I32))
    val = e(nc.sbuf_tensor(f"{name}_val", [128, IN], F32))
    vb = e(nc.sbuf_tensor(f"{name}_vb", [128, IN], BF16))
    trs = e(nc.sbuf_tensor(f"{name}_trs", [128, IN], BF16))
    trs3 = trs[:].rearrange("p (c o) -> p c o", o=128)
    s_out = kb.dma_sem(f"{name}out")

    pk_tok = [None, None]
    prev_val = None
    prev_sct = None
    prev_trs = None
    out_dma_toks = []
    for ot in range(n_ot):
        sl = ot % 2
        dep = [pk_tok[sl]] if pk_tok[sl] else []
        td = kb.dma("sp", s_pk[sl], lambda sy, ot=ot, sl=sl: sy.dma_start(
            out=pk[sl][:], in_=pk_dram[128 * ot:128 * ot + 128, :]), deps=dep,
            tag=f"{name}pkd{ot}")
        dep = [t_sc] + consts.tok
        if pool.rtok[db_tp] is not None:
            dep.append(pool.rtok[db_tp])
        if prev_sct is not None:
            dep.append(prev_sct)
        ttp = kb.op("pe", lambda t_, ot=ot: t_.transpose(
            out=pool.db[db_tp][:, :G], in_=sc_sb[:, 128 * ot:128 * ot + 128],
            identity=consts.ident_f32[:G, :G]), deps=dep, tag=f"{name}sctp{ot}")
        pool.wtok[db_tp] = ttp
        tev = kb.op("act", lambda s, ot=ot: s.copy(out=sct[:], in_=pool.db[db_tp][:, :G]),
                    deps=[ttp] + ([prev_val] if prev_val else []), tag=f"{name}scev{ot}")
        pool.rtok[db_tp] = tev
        prev_sct = tev

        plane_toks = []
        pv = prev_val
        for j in range(8):
            t1 = kb.op("dve", lambda v, j=j, sl=sl: v.tensor_scalar(
                out=tm1[:], in0=pk[sl][:], scalar1=4 * j, scalar2=7,
                op0=ALU.logical_shift_right, op1=ALU.bitwise_and),
                deps=[td] + ([pv] if pv else []), tag=f"{name}p{j}a")
            t2 = kb.op("dve", lambda v: v.tensor_scalar(
                out=tm2[:], in0=tm1[:], scalar1=22, scalar2=0x3E000000,
                op0=ALU.logical_shift_left, op1=ALU.bitwise_or), deps=[t1], tag=f"{name}p{j}b")
            t3 = kb.op("dve", lambda v: v.tensor_scalar(
                out=tu[:], in0=tm2[:].bitcast(F32), scalar1=8.0, scalar2=-1.0,
                op0=ALU.mult, op1=ALU.add), deps=[t2], tag=f"{name}p{j}c")
            t4 = kb.op("dve", lambda v: v.tensor_single_scalar(
                out=tq[:], in_=tm2[:].bitcast(F32), scalar=4.0, op=ALU.mult),
                deps=[t2], tag=f"{name}p{j}d")
            t5 = kb.op("dve", lambda v: v.tensor_tensor(
                out=tq[:], in0=tu[:], in1=tq[:], op=ALU.min), deps=[t3, t4], tag=f"{name}p{j}e")
            t6 = kb.op("dve", lambda v, j=j, sl=sl: v.tensor_scalar(
                out=tsg[:], in0=pk[sl][:], scalar1=28 - 4 * j, scalar2=-0x80000000,
                op0=ALU.logical_shift_left, op1=ALU.bitwise_and),
                deps=[td] + ([pv] if pv else []), tag=f"{name}p{j}f")
            t7 = kb.op("dve", lambda v, j=j: v.tensor_tensor(
                out=val[:].bitcast(I32)[:, j::8], in0=tq[:].bitcast(I32), in1=tsg[:],
                op=ALU.bitwise_or), deps=[t5, t6], tag=f"{name}p{j}g")
            plane_toks.append(t7)
        pk_tok[sl] = plane_toks[-1]

        sc_toks = []
        pg = prev_trs
        for g in range(G):
            t = kb.op("dve", lambda v, g=g: v.tensor_scalar(
                out=vb[:, 128 * g:128 * g + 128], in0=val[:, 128 * g:128 * g + 128],
                scalar1=sct[:, g:g + 1], scalar2=None, op0=ALU.mult),
                deps=plane_toks + [tev] + ([pg] if pg else []), tag=f"{name}sc{g}")
            sc_toks.append(t)
        prev_val = sc_toks[-1]

        ev_toks = []
        ptp = None
        for c in range(IN // 128):
            dep = [sc_toks[c]] + consts.tok
            if ptp is not None:
                dep.append(ptp)
            ttp2 = kb.op("pe", lambda t_, c=c: t_.transpose(
                out=pool.db[db_tp][:, :64].bitcast(BF16), in_=vb[:, 128 * c:128 * c + 128],
                identity=consts.ident_bf16[:]), deps=dep + [pool.rtok[db_tp]], tag=f"{name}tp{c}")
            pool.wtok[db_tp] = ttp2
            tev2 = kb.op("act", lambda s, c=c: s.copy(
                out=trs3[:, c, :], in_=pool.db[db_tp][:, :64].bitcast(BF16)),
                deps=[ttp2] + out_dma_toks[-1:], tag=f"{name}tev{c}")
            pool.rtok[db_tp] = tev2
            ptp = tev2
            ev_toks.append(tev2)
        prev_trs = ev_toks[-1]

        tdo = kb.dma("pool", s_out, lambda g, ot=ot: g.dma_start(
            out=out_dram[:, 128 * ot:128 * ot + 128].rearrange("(c p) o -> p c o", p=128),
            in_=trs3[:]), deps=ev_toks, tag=f"{name}odma{ot}")
        out_dma_toks.append(tdo)
    return out_dma_toks


def emit_l1ffn(kb, nc, ctx, consts, pool, x_dram, wguT_dram, wdnT_dram, out_dram,
               T=1024, name="lf", w_natural=False, w_deps=()):
    e = ctx.enter_context
    C = 16
    NI = 44
    NT = cdiv(T, 512)
    x3 = x_dram.rearrange("(c p) t -> c p t", p=128)
    o3 = out_dram.rearrange("(c p) t -> c p t", p=128)

    xbuf = [e(nc.sbuf_tensor(f"{name}_xb{i}", [128, T], F32)) for i in range(2)]
    s_xb = [kb.dma_sem(f"{name}xb{i}") for i in range(2)]
    xbuf_state = [None, None]
    xbuf_rd = [[], []]

    def x_at(c):
        sl = c % 2
        cur = xbuf_state[sl]
        if cur is not None and cur[0] == c:
            return xbuf[sl][:], [cur[1]]
        dep = xbuf_rd[sl]
        t = kb.dma("sp", s_xb[sl], lambda sy, c=c, sl=sl: sy.dma_start(
            out=xbuf[sl][:], in_=x3[c, :, :]), deps=dep, tag=f"{name}xd{c}")
        xbuf_state[sl] = (c, t)
        xbuf_rd[sl] = []
        return xbuf[sl][:], [t]

    def x_used(c, toks):
        xbuf_rd[c % 2].extend(toks)

    xln = e(nc.sbuf_tensor(f"{name}_xln", [128, C, T], BF16))
    ln_toks = emit_ln_fm(kb, nc, ctx, consts, pool, x_at, C, T, xln, 1e-5, [],
                         name + "ln", x_done=x_used)

    sg = e(nc.sbuf_tensor(f"{name}_sg", [128, 4, T], BF16))
    act = e(nc.sbuf_tensor(f"{name}_act", [128, NI, T], BF16))
    strip = [e(nc.sbuf_tensor(f"{name}_st{i}", [128, 512], BF16)) for i in range(2)]
    s_st = [kb.dma_sem(f"{name}st{i}") for i in range(2)]
    strip_rd = [None, None]
    sg_rd = [None] * 4
    sg_wr = [None] * 4
    act_wr = {}

    si = 0
    for g in range(11):
        for part in range(2):
            col0 = 512 * g + (5632 if part else 0)
            mm_last = [None] * 4
            for hc in range(C):
                sl = si % 2
                si += 1
                dep = ([strip_rd[sl]] if strip_rd[sl] else []) + list(w_deps)
                if w_natural:
                    td = kb.dma("sp", s_st[sl], lambda sy, hc=hc, col0=col0, sl=sl: sy.dma_start_transpose(
                        out=strip[sl][:], in_=wguT_dram[col0:col0 + 512, 128 * hc:128 * hc + 128]),
                        deps=dep, tag=f"{name}gu_st")
                else:
                    td = kb.dma("sp", s_st[sl], lambda sy, hc=hc, col0=col0, sl=sl: sy.dma_start(
                        out=strip[sl][:], in_=wguT_dram[128 * hc:128 * hc + 128, col0:col0 + 512]),
                        deps=dep, tag=f"{name}gu_st")
                mms = []
                for b in range(4):
                    for tb in range(NT):
                        n = min(512, T - 512 * tb)
                        dep2 = [td, ln_toks[hc]]
                        if hc == 0 and pool.rtok[b] is not None:
                            dep2.append(pool.rtok[b])
                        t = kb.op("pe", lambda t_, hc=hc, b=b, tb=tb, n=n, sl=sl: t_.matmul(
                            out=pool.db[b][:, 512 * tb:512 * tb + n],
                            lhsT=strip[sl][:, 128 * b:128 * b + 128],
                            rhs=xln[:, hc, 512 * tb:512 * tb + n],
                            start=(hc == 0), stop=(hc == C - 1)), deps=dep2,
                            tag=f"{name}gu_mm")
                        mms.append(t)
                        if hc == C - 1:
                            mm_last[b] = t
                strip_rd[sl] = mms[-1]
            for b in range(4):
                pool.wtok[b] = mm_last[b]
                if part == 0:
                    dep = [mm_last[b]] + ([sg_rd[b]] if sg_rd[b] else [])
                    t = kb.op("act", lambda s, b=b: s.activation(
                        out=sg[:, b, :], in_=pool.db[b][:, :T], func=AF.Silu),
                        deps=dep, tag=f"{name}silu")
                    pool.rtok[b] = t
                    sg_wr[b] = t
                else:
                    ib = 4 * g + b
                    t = kb.op("dve", lambda v, b=b, ib=ib: v.tensor_tensor(
                        out=act[:, ib, :], in0=pool.db[b][:, :T], in1=sg[:, b, :],
                        op=ALU.mult), deps=[mm_last[b], sg_wr[b]], tag=f"{name}amul")
                    pool.rtok[b] = t
                    sg_rd[b] = t
                    act_wr[ib] = t

    y = e(nc.sbuf_tensor(f"{name}_y", [128, C, T], BF16))
    y_wr = [None] * C
    for grp in range(4):
        mm_last = [None] * 4
        for ic in range(NI):
            sl = si % 2
            si += 1
            dep = ([strip_rd[sl]] if strip_rd[sl] else []) + list(w_deps)
            if w_natural:
                td = kb.dma("sp", s_st[sl], lambda sy, ic=ic, grp=grp, sl=sl: sy.dma_start_transpose(
                    out=strip[sl][:], in_=wdnT_dram[512 * grp:512 * grp + 512, 128 * ic:128 * ic + 128]),
                    deps=dep, tag=f"{name}dn_st")
            else:
                td = kb.dma("sp", s_st[sl], lambda sy, ic=ic, grp=grp, sl=sl: sy.dma_start(
                    out=strip[sl][:], in_=wdnT_dram[128 * ic:128 * ic + 128, 512 * grp:512 * grp + 512]),
                    deps=dep, tag=f"{name}dn_st")
            mms = []
            for b in range(4):
                for tb in range(NT):
                    n = min(512, T - 512 * tb)
                    dep2 = [td, act_wr[ic]]
                    if ic == 0 and pool.rtok[b] is not None:
                        dep2.append(pool.rtok[b])
                    t = kb.op("pe", lambda t_, ic=ic, b=b, tb=tb, n=n, sl=sl: t_.matmul(
                        out=pool.db[b][:, 512 * tb:512 * tb + n],
                        lhsT=strip[sl][:, 128 * b:128 * b + 128],
                        rhs=act[:, ic, 512 * tb:512 * tb + n],
                        start=(ic == 0), stop=(ic == NI - 1)), deps=dep2,
                        tag=f"{name}dn_mm")
                    mms.append(t)
                    if ic == NI - 1:
                        mm_last[b] = t
            strip_rd[sl] = mms[-1]
        for b in range(4):
            ob = 4 * grp + b
            pool.wtok[b] = mm_last[b]
            t = kb.op("act", lambda s, b=b, ob=ob: s.copy(
                out=y[:, ob, :], in_=pool.db[b][:, :T]),
                deps=[mm_last[b]] + ([y_wr[ob]] if y_wr[ob] else []), tag=f"{name}yev")
            pool.rtok[b] = t
            y_wr[ob] = t

    sq2 = e(nc.sbuf_tensor(f"{name}_sq2", [128, T], BF16))
    rrows = e(nc.sbuf_tensor(f"{name}_rr", [1, T], F32))
    mm_last2 = []
    prev_sq = None
    rd_sq2 = []
    for c in range(C):
        ts = kb.op("act", lambda s, c=c: s.activation(out=sq2[:], in_=y[:, c, :], func=AF.Square),
                   deps=[y_wr[c]] + ([prev_sq] if prev_sq else []) + rd_sq2, tag=f"{name}rsq{c}")
        prev_sq = ts
        rd_sq2 = []
        for i in range(NT):
            n = min(512, T - 512 * i)
            dep = [ts] + consts.tok
            if c == 0 and pool.rtok[0] is not None:
                dep.append(pool.rtok[0])
            t2 = kb.op("pe", lambda t_, i=i, c=c, n=n: t_.matmul(
                out=pool.db[0][0:1, 512 * i:512 * i + n], lhsT=consts.ones_col_bf16[:],
                rhs=sq2[:, 512 * i:512 * i + n], start=(c == 0), stop=(c == C - 1)),
                deps=dep, tag=f"{name}rmm")
            rd_sq2.append(t2)
            if c == C - 1:
                mm_last2.append(t2)
    pool.wtok[0] = mm_last2[-1]
    tv = kb.op("dve", lambda v: v.tensor_scalar(
        out=rrows[:], in0=pool.db[0][0:1, :T], scalar1=1.0 / 2048.0, scalar2=1e-6,
        op0=ALU.mult, op1=ALU.add), deps=mm_last2, tag=f"{name}rv")
    pool.rtok[0] = tv
    tl = kb.op("act", lambda s: s.activation(out=rrows[:], in_=rrows[:], func=AF.Ln),
               deps=[tv], tag=f"{name}rln")
    tr = kb.op("act", lambda s: s.activation(out=rrows[:], in_=rrows[:], func=AF.Exp, scale=-0.5),
               deps=[tl], tag=f"{name}rexp")
    toks_R = bcast_row(kb, consts, pool, 1, rrows[:], T, [tr], tag=f"{name}bR")

    obuf = [e(nc.sbuf_tensor(f"{name}_ob{i}", [128, T], F32)) for i in range(2)]
    s_ob = [kb.dma_sem(f"{name}ob0"), kb.dma_sem(f"{name}ob1")]
    out_toks = []
    ob_rd = [None, None]
    prev_t1 = None
    for c in range(C):
        xc, xdep = x_at(c)
        sl = c % 2
        t1 = kb.op("dve", lambda v, c=c, sl=sl: v.tensor_tensor(
            out=obuf[sl][:], in0=y[:, c, :], in1=pool.db[1][:, :T], op=ALU.mult),
            deps=[y_wr[c]] + toks_R + ([ob_rd[sl]] if ob_rd[sl] else []), tag=f"{name}f1_{c}")
        t2 = kb.op("dve", lambda v, xc=xc, sl=sl: v.tensor_tensor(
            out=obuf[sl][:], in0=obuf[sl][:], in1=xc, op=ALU.add),
            deps=[t1] + xdep, tag=f"{name}f2_{c}")
        x_used(c, [t2])
        td = kb.dma("pool", s_ob[sl], lambda g, c=c, sl=sl: g.dma_start(
            out=o3[c, :, :], in_=obuf[sl][:]), deps=[t2], tag=f"{name}odma{c}")
        ob_rd[sl] = td
        out_toks.append(td)
    pool.rtok[1] = prev_t1
    return out_toks


class StripStream:

    def __init__(self, kb, nc, ctx, name, shape, dtype, nslots=2, queue="pool"):
        e = ctx.enter_context
        self.kb = kb
        self.queue = queue
        self.shape = list(shape)
        self.bufs = [e(nc.sbuf_tensor(f"{name}_s{i}", list(shape), dtype))
                     for i in range(nslots)]
        self.sems = [kb.dma_sem(f"{name}{i}") for i in range(nslots)]
        self.readers = [[] for _ in range(nslots)]
        self.i = 0
        self.n = nslots

    def load(self, src_fn, w=None, tag=""):
        sl = self.i % self.n
        self.i += 1
        w = self.shape[1] if w is None else w
        t = self.kb.dma(self.queue, self.sems[sl],
                        lambda g, sl=sl, w=w: g.dma_start(out=self.bufs[sl][:, :w],
                                                          in_=src_fn()),
                        deps=self.readers[sl], tag=tag)
        self.readers[sl] = []
        return self.bufs[sl], sl, t

    def read_by(self, sl, toks):
        self.readers[sl].extend(toks)


def emit_l0ffn(kb, nc, ctx, consts, pool, x_dram, w, out_dram, T=1024, name="mf",
               pump=None):
    e = ctx.enter_context
    C = 16
    NT = cdiv(T, 512)
    NTT = T // 128
    x3 = x_dram.rearrange("(c p) t -> c p t", p=128)
    o3 = out_dram.rearrange("(c p) t -> c p t", p=128)

    xbuf = [e(nc.sbuf_tensor(f"{name}_xb{i}", [128, T], F32)) for i in range(2)]
    s_xb = [kb.dma_sem(f"{name}xb{i}") for i in range(2)]
    xbuf_state = [None, None]
    xbuf_rd = [[], []]

    def x_at(c):
        sl = c % 2
        cur = xbuf_state[sl]
        if cur is not None and cur[0] == c:
            return xbuf[sl][:], [cur[1]]
        dep = xbuf_rd[sl]
        t = kb.dma("sp", s_xb[sl], lambda sy, c=c, sl=sl: sy.dma_start(
            out=xbuf[sl][:], in_=x3[c, :, :]), deps=dep, tag=f"{name}xd{c}")
        xbuf_state[sl] = (c, t)
        xbuf_rd[sl] = []
        return xbuf[sl][:], [t]

    def x_used(c, toks):
        xbuf_rd[c % 2].extend(toks)

    xln = e(nc.sbuf_tensor(f"{name}_xln", [128, C, T], BF16))
    ln_toks = emit_ln_fm(kb, nc, ctx, consts, pool, x_at, C, T, xln, 1e-5, [],
                         name + "ln", x_done=x_used)

    rw_sb = e(nc.sbuf_tensor(f"{name}_rw", [8, 2048], BF16))
    s_rw = kb.dma_sem(f"{name}rw")
    t_rw = kb.dma("pool", s_rw, lambda g: g.dma_start(out=rw_sb[:], in_=w["router_w"][:]))
    rwT = e(nc.sbuf_tensor(f"{name}_rwT", [128, C, 8], BF16))
    rwT_toks = []
    prev = None
    for c in range(C):
        dep = [t_rw] + consts.tok + [pool.wtok[0], pool.rtok[0]]
        tp = kb.op("pe", lambda t_, c=c: t_.transpose(
            out=pool.db[0][:, :4].bitcast(BF16), in_=rw_sb[:, 128 * c:128 * c + 128],
            identity=consts.ident_bf16[:8, :8]), deps=dep, tag=f"{name}rwt{c}")
        pool.wtok[0] = tp
        te = kb.op("act", lambda s, c=c: s.copy(
            out=rwT[:, c, :], in_=pool.db[0][:, :4].bitcast(BF16)[:, :8]),
            deps=[tp] + ([prev] if prev else []), tag=f"{name}rwe{c}")
        pool.rtok[0] = te
        prev = te
        rwT_toks.append(te)

    lg_mm = []
    for c in range(C):
        for i in range(NT):
            n = min(512, T - 512 * i)
            dep = [rwT_toks[c], ln_toks[c]]
            if c == 0 and pool.rtok[1] is not None:
                dep += [pool.rtok[1], pool.wtok[1]]
            t = kb.op("pe", lambda t_, c=c, i=i, n=n: t_.matmul(
                out=pool.db[1][0:8, 512 * i:512 * i + n], lhsT=rwT[:, c, :],
                rhs=xln[:, c, 512 * i:512 * i + n], start=(c == 0), stop=(c == C - 1)),
                deps=dep, tag=f"{name}lgmm")
            if c == C - 1:
                lg_mm.append(t)
    pool.wtok[1] = lg_mm[-1]
    lg_sb = e(nc.sbuf_tensor(f"{name}_lg", [8, T], F32))
    tlg = kb.op("act", lambda s: s.copy(out=lg_sb[:], in_=pool.db[1][0:8, :T]),
                deps=lg_mm, tag=f"{name}lgev")
    pool.rtok[1] = tlg

    cw = e(nc.sbuf_tensor(f"{name}_cw", [128, NTT, 8], F32))
    col = e(nc.sbuf_tensor(f"{name}_col", [128, 6], F32))
    ex = e(nc.sbuf_tensor(f"{name}_ex", [128, 8], F32))
    eq = e(nc.sbuf_tensor(f"{name}_eq", [128, 8], F32))
    cw_toks = []
    prev_ops = []
    prev_t2 = None
    for tt in range(NTT):
        dep0 = [tlg] + consts.tok + [pool.wtok[2], pool.rtok[2]]
        tp = kb.op("pe", lambda t_, tt=tt: t_.transpose(
            out=pool.db[2][:, :8], in_=lg_sb[:, 128 * tt:128 * tt + 128],
            identity=consts.ident_f32[:8, :8]), deps=dep0, tag=f"{name}cwt{tt}")
        pool.wtok[2] = tp
        t0 = kb.op("dve", lambda v: v.reduce_max(
            out=col[:, 0:1], in_=pool.db[2][:, :8], axis=AX.X),
            deps=[tp] + prev_ops, tag=f"{name}r0")
        t1 = kb.op("dve", lambda v: v.tensor_single_scalar(
            out=col[:, 1:2], in_=col[:, 0:1], scalar=-1.0, op=ALU.mult),
            deps=[t0] + ([prev_t2] if prev_t2 else []), tag=f"{name}r1")
        t2 = kb.op("act", lambda s: s.activation(
            out=ex[:], in_=pool.db[2][:, :8], func=AF.Exp, bias=col[:, 1:2]),
            deps=[t1] + prev_ops, tag=f"{name}r2")
        prev_t2 = t2
        pool.rtok[2] = t2
        t3 = kb.op("dve", lambda v: v.reduce_max(out=col[:, 2:3], in_=ex[:], axis=AX.X),
                   deps=[t2], tag=f"{name}r3")
        t4 = kb.op("dve", lambda v: v.tensor_scalar(
            out=eq[:], in0=ex[:], scalar1=col[:, 2:3], scalar2=None, op0=ALU.is_equal),
            deps=[t3], tag=f"{name}r4")
        t5 = kb.op("dve", lambda v: v.scalar_tensor_tensor(
            out=eq[:], in0=eq[:], scalar=-1e30, in1=ex[:], op0=ALU.mult, op1=ALU.add),
            deps=[t4], tag=f"{name}r5")
        t6 = kb.op("dve", lambda v: v.reduce_max(out=col[:, 3:4], in_=eq[:], axis=AX.X),
                   deps=[t5], tag=f"{name}r6")
        t7 = kb.op("dve", lambda v: v.tensor_tensor(
            out=col[:, 4:5], in0=col[:, 2:3], in1=col[:, 3:4], op=ALU.add),
            deps=[t6], tag=f"{name}r7")
        t8 = kb.op("dve", lambda v: v.reciprocal(out=col[:, 5:6], in_=col[:, 4:5]),
                   deps=[t7], tag=f"{name}r8")
        t9 = kb.op("dve", lambda v: v.tensor_scalar(
            out=eq[:], in0=ex[:], scalar1=col[:, 3:4], scalar2=None, op0=ALU.is_ge),
            deps=[t6], tag=f"{name}r9")
        ta = kb.op("dve", lambda v: v.tensor_tensor(out=ex[:], in0=ex[:], in1=eq[:],
                                                    op=ALU.mult), deps=[t9], tag=f"{name}ra")
        tb = kb.op("dve", lambda v, tt=tt: v.tensor_scalar(
            out=cw[:, tt, :], in0=ex[:], scalar1=col[:, 5:6], scalar2=None, op0=ALU.mult),
            deps=[ta, t8], tag=f"{name}rb")
        prev_ops = [tb]
        cw_toks.append(tb)

    sgw = e(nc.sbuf_tensor(f"{name}_sgw", [16, 128], BF16))
    s_sgw = kb.dma_sem(f"{name}sgw")
    t_sgw = kb.dma("pool", s_sgw, lambda g: g.dma_start(out=sgw[:], in_=w["shared_gate_w"][:]))
    sgwT = e(nc.sbuf_tensor(f"{name}_sgwT", [128, 16], BF16))
    dep = [t_sgw] + consts.tok + [pool.wtok[3], pool.rtok[3]]
    tp = kb.op("pe", lambda t_: t_.transpose(
        out=pool.db[3][:, :8].bitcast(BF16), in_=sgw[:],
        identity=consts.ident_bf16[:16, :16]), deps=dep, tag=f"{name}sgwt")
    pool.wtok[3] = tp
    te = kb.op("act", lambda s: s.copy(out=sgwT[:], in_=pool.db[3][:, :8].bitcast(BF16)[:, :16]),
               deps=[tp], tag=f"{name}sgwe")
    pool.rtok[3] = te
    sg_mm = []
    for c in range(C):
        for i in range(NT):
            n = min(512, T - 512 * i)
            dep = [te, ln_toks[c]]
            if c == 0:
                dep += [pool.rtok[3], pool.wtok[3]]
            t = kb.op("pe", lambda t_, c=c, i=i, n=n: t_.matmul(
                out=pool.db[3][0:1, 512 * i:512 * i + n], lhsT=sgwT[:, c:c + 1],
                rhs=xln[:, c, 512 * i:512 * i + n], start=(c == 0), stop=(c == C - 1)),
                deps=dep, tag=f"{name}sgmm")
            if c == C - 1:
                sg_mm.append(t)
    pool.wtok[3] = sg_mm[-1]
    rows2 = e(nc.sbuf_tensor(f"{name}_rows2", [1, 2 * T], F32))
    srow = rows2[:, 0:T]
    cmbrow = rows2[:, T:2 * T]
    t_sr = kb.op("act", lambda s: s.activation(out=srow[:], in_=pool.db[3][0:1, :T],
                                               func=AF.Sigmoid), deps=sg_mm, tag=f"{name}sig")
    pool.rtok[3] = t_sr
    sgb = e(nc.sbuf_tensor(f"{name}_sgb", [128, T], F32))
    tb_toks = bcast_row(kb, consts, pool, 3, srow, T, [t_sr], tag=f"{name}sgbc")
    t_sgb = kb.op("dve", lambda v: v.tensor_copy(out=sgb[:], in_=pool.db[3][:, :T]),
                  deps=tb_toks, tag=f"{name}sgbe")
    pool.rtok[3] = t_sgb

    acc = e(nc.sbuf_tensor(f"{name}_acc", [128, C, T], F32))
    act = e(nc.sbuf_tensor(f"{name}_act", [128, 12, T], BF16))
    sg = e(nc.sbuf_tensor(f"{name}_sg", [128, 4, T], BF16))
    tmp = e(nc.sbuf_tensor(f"{name}_tmp", [128, T], F32))
    strips = StripStream(kb, nc, ctx, f"{name}st", [128, 512], BF16, 2, "pool")
    sg_rd = [None] * 4
    sg_wr = [None] * 4
    acc_wr = [None] * C
    act_wr = {}
    act_rd = {}
    state = {"tmp_rd": [], "cmbb_rd": [], "cmbrow_rd": []}

    def gate_up_phase(nblk, wg_src, wu_src, scale_bcast=None):
        for g0 in range(0, nblk, 4):
            gl = min(4, nblk - g0)
            col0 = 128 * g0
            width = 128 * gl
            for part in range(2):
                if pump:
                    pump()
                mm_last = [None] * gl
                for hc in range(C):
                    src = (wg_src if part == 0 else wu_src)
                    buf, sl, td = strips.load(lambda src=src, hc=hc, col0=col0, width=width:
                                              src(hc, col0, width), w=width, tag=f"{name}gu_st")
                    mms = []
                    for b in range(gl):
                        for tb_ in range(NT):
                            n = min(512, T - 512 * tb_)
                            dep2 = [td, ln_toks[hc]]
                            if hc == 0 and pool.rtok[b] is not None:
                                dep2 += [pool.rtok[b], pool.wtok[b]]
                            t = kb.op("pe", lambda t_, hc=hc, b=b, tb_=tb_, n=n, buf=buf: t_.matmul(
                                out=pool.db[b][:, 512 * tb_:512 * tb_ + n],
                                lhsT=buf[:, 128 * b:128 * b + 128],
                                rhs=xln[:, hc, 512 * tb_:512 * tb_ + n],
                                start=(hc == 0), stop=(hc == C - 1)), deps=dep2,
                                tag=f"{name}gu_mm")
                            mms.append(t)
                            if hc == C - 1:
                                mm_last[b] = t
                    strips.read_by(sl, [mms[-1]])
                for b in range(gl):
                    pool.wtok[b] = mm_last[b]
                    if part == 0:
                        dep = [mm_last[b]] + ([sg_rd[b]] if sg_rd[b] else [])
                        t = kb.op("act", lambda s, b=b: s.activation(
                            out=sg[:, b, :], in_=pool.db[b][:, :T], func=AF.Silu),
                            deps=dep, tag=f"{name}silu")
                        pool.rtok[b] = t
                        sg_wr[b] = t
                    else:
                        ib = g0 + b
                        if scale_bcast is None:
                            t = kb.op("dve", lambda v, b=b, ib=ib: v.tensor_tensor(
                                out=act[:, ib, :], in0=pool.db[b][:, :T], in1=sg[:, b, :],
                                op=ALU.mult), deps=[mm_last[b], sg_wr[b]] +
                                ([act_rd.get(ib)] if act_rd.get(ib) else []), tag=f"{name}amul")
                        else:
                            sb_ap, sb_tok = scale_bcast
                            t1 = kb.op("dve", lambda v, b=b, sb_ap=sb_ap, sb_tok=sb_tok: v.tensor_tensor(
                                out=tmp[:], in0=pool.db[b][:, :T], in1=sb_ap,
                                op=ALU.mult), deps=[mm_last[b], sb_tok] + state["tmp_rd"],
                                tag=f"{name}amul1")
                            state["tmp_rd"] = []
                            state["cmbb_rd"].append(t1)
                            t = kb.op("dve", lambda v, b=b, ib=ib: v.tensor_tensor(
                                out=act[:, ib, :], in0=tmp[:], in1=sg[:, b, :],
                                op=ALU.mult), deps=[t1, sg_wr[b]] +
                                ([act_rd.get(ib)] if act_rd.get(ib) else []), tag=f"{name}amul2")
                            state["tmp_rd"] = [t]
                        pool.rtok[b] = t
                        sg_rd[b] = t
                        act_wr[ib] = t

    def down_phase(n_ic, dn_src, first, scale_sgb):
        for grp in range(4):
            if pump:
                pump()
            mm_last = [None] * 4
            for ic in range(n_ic):
                buf, sl, td = strips.load(lambda ic=ic, grp=grp: dn_src(ic, 512 * grp),
                                          tag=f"{name}dn_st")
                mms = []
                for b in range(4):
                    for tb_ in range(NT):
                        n = min(512, T - 512 * tb_)
                        dep2 = [td, act_wr[ic]]
                        if ic == 0 and pool.rtok[b] is not None:
                            dep2 += [pool.rtok[b], pool.wtok[b]]
                        t = kb.op("pe", lambda t_, ic=ic, b=b, tb_=tb_, n=n, buf=buf: t_.matmul(
                            out=pool.db[b][:, 512 * tb_:512 * tb_ + n],
                            lhsT=buf[:, 128 * b:128 * b + 128],
                            rhs=act[:, ic, 512 * tb_:512 * tb_ + n],
                            start=(ic == 0), stop=(ic == n_ic - 1)), deps=dep2,
                            tag=f"{name}dn_mm")
                        mms.append(t)
                        act_rd[ic] = t
                        if ic == n_ic - 1:
                            mm_last[b] = t
                strips.read_by(sl, [mms[-1]])
            for b in range(4):
                ob = 4 * grp + b
                pool.wtok[b] = mm_last[b]
                if scale_sgb:
                    t1 = kb.op("dve", lambda v, b=b: v.tensor_tensor(
                        out=tmp[:], in0=pool.db[b][:, :T], in1=sgb[:],
                        op=ALU.mult), deps=[mm_last[b], t_sgb] + state["tmp_rd"], tag=f"{name}dse1")
                    state["tmp_rd"] = []
                    if first:
                        t = kb.op("dve", lambda v, ob=ob: v.tensor_copy(
                            out=acc[:, ob, :], in_=tmp[:]), deps=[t1], tag=f"{name}dse2")
                    else:
                        t = kb.op("dve", lambda v, ob=ob: v.tensor_tensor(
                            out=acc[:, ob, :], in0=acc[:, ob, :], in1=tmp[:],
                            op=ALU.add), deps=[t1, acc_wr[ob]], tag=f"{name}dse2")
                    state["tmp_rd"] = [t]
                    pool.rtok[b] = t1
                else:
                    t = kb.op("dve", lambda v, ob=ob, b=b: v.tensor_tensor(
                        out=acc[:, ob, :], in0=acc[:, ob, :], in1=pool.db[b][:, :T],
                        op=ALU.add), deps=[mm_last[b], acc_wr[ob]], tag=f"{name}eacc")
                    pool.rtok[b] = t
                acc_wr[ob] = t

    def sw_src(which, base):
        def f(hc, col0, width):
            return w[which][128 * hc:128 * hc + 128, base + col0:base + col0 + width]
        return f

    splits = [(0, 12), (12, 12), (24, 12), (36, 8)]
    for pi, (b0, nb) in enumerate(splits):
        gate_up_phase(nb, sw_src("sw_gate", 128 * b0), sw_src("sw_up", 128 * b0))
        down_phase(nb, lambda ic, oc, b0=b0: w["sw_down"][128 * (b0 + ic):128 * (b0 + ic) + 128,
                                                          oc:oc + 512],
                   first=(pi == 0), scale_sgb=True)

    cmbb = e(nc.sbuf_tensor(f"{name}_cmbb", [128, T], F32))
    for ee in range(8):
        rowtoks = []
        prevc = None
        for tt in range(NTT):
            dep = [cw_toks[tt]] + consts.tok + [pool.wtok[3], pool.rtok[3]]
            tp2 = kb.op("pe", lambda t_, tt=tt, ee=ee: t_.transpose(
                out=pool.db[3][0:1, :128], in_=cw[:, tt, ee:ee + 1],
                identity=consts.ident_f32[:]), deps=dep, tag=f"{name}cbt")
            pool.wtok[3] = tp2
            tc = kb.op("act", lambda s, tt=tt: s.copy(
                out=cmbrow[:, 128 * tt:128 * tt + 128], in_=pool.db[3][0:1, :128]),
                deps=[tp2] + ([prevc] if prevc else []) + state["cmbrow_rd"], tag=f"{name}cbe")
            state["cmbrow_rd"] = []
            pool.rtok[3] = tc
            prevc = tc
            rowtoks.append(tc)
        bts = bcast_row(kb, consts, pool, 3, cmbrow, T, rowtoks, tag=f"{name}cbb")
        state["cmbrow_rd"] = list(bts)
        tcb = kb.op("dve", lambda v: v.tensor_copy(out=cmbb[:], in_=pool.db[3][:, :T]),
                    deps=bts + state["cmbb_rd"], tag=f"{name}cbev")
        state["cmbb_rd"] = []
        pool.rtok[3] = tcb

        def wg_src(hc, col0, width, ee=ee):
            return w["w_gate"][ee, 128 * hc:128 * hc + 128, col0:col0 + width]

        def wu_src(hc, col0, width, ee=ee):
            return w["w_up"][ee, 128 * hc:128 * hc + 128, col0:col0 + width]

        gate_up_phase(11, wg_src, wu_src, scale_bcast=(cmbb[:], tcb))
        down_phase(11, lambda ic, oc, ee=ee: w["w_down"][ee, 128 * ic:128 * ic + 128,
                                                         oc:oc + 512],
                   first=False, scale_sgb=False)

    obuf = e(nc.sbuf_tensor(f"{name}_obuf", [128, T], F32))
    s_ob = kb.dma_sem(f"{name}ob")
    out_toks = []
    ob_rd = None
    for c in range(C):
        xc, xdep = x_at(c)
        t2 = kb.op("dve", lambda v, xc=xc, c=c: v.tensor_tensor(
            out=obuf[:], in0=acc[:, c, :], in1=xc, op=ALU.add),
            deps=[acc_wr[c]] + xdep + ([ob_rd] if ob_rd else []), tag=f"{name}fin{c}")
        x_used(c, [t2])
        td = kb.dma("sp", s_ob, lambda g, c=c: g.dma_start(
            out=o3[c, :, :], in_=obuf[:]), deps=[t2], tag=f"{name}odma{c}")
        ob_rd = td
        out_toks.append(td)
    return out_toks


def emit_l1attn_lat(kb, nc, ctx, consts, pool, x_dram, wkvd_dram, gamma_dram,
                    xln_out, lat_out, T=1024, name="la", lat_shared=None):
    e = ctx.enter_context
    C = 16
    NT = cdiv(T, 512)
    x3 = x_dram.rearrange("(c p) t -> c p t", p=128)
    xo3 = xln_out.rearrange("(c p) t -> c p t", p=128)
    lo3 = lat_out.rearrange("(c p) t -> c p t", p=128) if lat_out is not None else None

    xbuf = [e(nc.sbuf_tensor(f"{name}_xb{i}", [128, T], F32)) for i in range(2)]
    s_xb = [kb.dma_sem(f"{name}xb{i}") for i in range(2)]
    xbuf_state = [None, None]
    xbuf_rd = [[], []]

    def x_at(c):
        sl = c % 2
        cur = xbuf_state[sl]
        if cur is not None and cur[0] == c:
            return xbuf[sl][:], [cur[1]]
        dep = xbuf_rd[sl]
        t = kb.dma("sp", s_xb[sl], lambda sy, c=c, sl=sl: sy.dma_start(
            out=xbuf[sl][:], in_=x3[c, :, :]), deps=dep, tag=f"{name}xd{c}")
        xbuf_state[sl] = (c, t)
        xbuf_rd[sl] = []
        return xbuf[sl][:], [t]

    def x_used(c, toks):
        xbuf_rd[c % 2].extend(toks)

    xln = e(nc.sbuf_tensor(f"{name}_xln", [128, C, T], BF16))
    ln_toks = emit_ln_fm(kb, nc, ctx, consts, pool, x_at, C, T, xln, 1e-5, [],
                         name + "ln", x_done=x_used)
    s_xo = kb.dma_sem(f"{name}xo")
    xo_toks = []
    for c in range(C):
        t = kb.dma("sp", s_xo, lambda sy, c=c: sy.dma_start(
            out=xo3[c, :, :], in_=xln[:, c, :]), deps=[ln_toks[c]], tag=f"{name}xod{c}")
        xo_toks.append(t)

    wst = StripStream(kb, nc, ctx, f"{name}wd", [128, 256], BF16, 2, "pool")
    mm_last = [None, None]
    for dc in range(C):
        buf, sl, td = wst.load(lambda dc=dc: wkvd_dram[128 * dc:128 * dc + 128, :],
                               tag=f"{name}wdst")
        mms = []
        for lc in range(2):
            for i in range(NT):
                n = min(512, T - 512 * i)
                dep = [td, ln_toks[dc]]
                if dc == 0 and pool.rtok[lc] is not None:
                    dep += [pool.rtok[lc], pool.wtok[lc]]
                t = kb.op("pe", lambda t_, dc=dc, lc=lc, i=i, n=n, buf=buf: t_.matmul(
                    out=pool.db[lc][:, 512 * i:512 * i + n],
                    lhsT=buf[:, 128 * lc:128 * lc + 128],
                    rhs=xln[:, dc, 512 * i:512 * i + n],
                    start=(dc == 0), stop=(dc == C - 1)), deps=dep, tag=f"{name}latmm")
                mms.append(t)
                if dc == C - 1:
                    mm_last[lc] = t
        wst.read_by(sl, [mms[-1]])
    lat_sb = e(nc.sbuf_tensor(f"{name}_lat", [128, 2, T], F32))
    ev = []
    for lc in range(2):
        pool.wtok[lc] = mm_last[lc]
        t = kb.op("act", lambda s, lc=lc: s.copy(out=lat_sb[:, lc, :], in_=pool.db[lc][:, :T]),
                  deps=[mm_last[lc]], tag=f"{name}latev")
        pool.rtok[lc] = t
        ev.append(t)

    gamma = e(nc.sbuf_tensor(f"{name}_gam", [128, 2], F32))
    s_g = kb.dma_sem(f"{name}g")
    tg = kb.dma("sp", s_g, lambda sy: sy.dma_start(out=gamma[:], in_=gamma_dram[:]))
    latn = e(nc.sbuf_tensor(f"{name}_latn", [128, 2, T], BF16))
    rn_toks = emit_ln_fm(kb, nc, ctx, consts, pool,
                         lambda c: (lat_sb[:, c, :], [ev[c], tg]), 2, T, latn, 1e-6,
                         [], name + "rn", rms_only=True, gamma=gamma)
    s_lo = kb.dma_sem(f"{name}lo")
    lo_toks = []
    for lc in range(2):
        if lat_shared is None:
            t = kb.dma("sp", s_lo, lambda sy, lc=lc: sy.dma_start(
                out=lo3[lc, :, :], in_=latn[:, lc, :]), deps=[rn_toks[lc]],
                tag=f"{name}lod{lc}")
        else:
            t = kb.dma("sp", s_lo, lambda sy, lc=lc: sy.dma_start(
                out=lat_shared[bass.ds(sy.partition_id() % 2 * 256 + 128 * lc, 128), :],
                in_=latn[:, lc, :]), deps=[rn_toks[lc]],
                tag=f"{name}lod{lc}")
        lo_toks.append(t)
    return xo_toks + lo_toks


def emit_l1attn_core(kb, nc, ctx, consts, pool, xlnT_dram, latseq_dram, x_dram,
                     wq_dram, wkvu_dram, wo_dram, m01_dram, out_dram,
                     T=1024, L=2048, name="at", latseq_shared=None, lat_deps=()):
    e = ctx.enter_context
    C = 16
    H = 16
    NT = cdiv(T, 512)
    NKB = L // 128
    x3 = x_dram.rearrange("(c p) t -> c p t", p=128)
    o3 = out_dram.rearrange("(c p) t -> c p t", p=128)

    xln = e(nc.sbuf_tensor(f"{name}_xln", [128, C, T], BF16))
    latn = e(nc.sbuf_tensor(f"{name}_latn", [128, 2, L], BF16))
    wu_sb = e(nc.sbuf_tensor(f"{name}_wu", [128, 2, 32, 128], BF16))
    s_in = kb.dma_sem(f"{name}in")
    t_xln = kb.dma("sp", s_in, lambda sy: sy.dma_start(
        out=xln[:], in_=xlnT_dram.rearrange("(c p) t -> p c t", p=128)))
    if latseq_shared is None:
        t_lat = kb.dma("sp", s_in, lambda sy: sy.dma_start(
            out=latn[:], in_=latseq_dram.rearrange("(c p) t -> p c t", p=128)),
            deps=list(lat_deps))
    else:
        t_la = [kb.dma("sp", s_in, lambda sy, r=r: sy.dma_start(
            out=latn[:, :, r * T:(r + 1) * T],
            in_=latseq_shared[r * 256:(r + 1) * 256, :].rearrange(
                "(c p) t -> p c t", p=128)),
            deps=list(lat_deps), tag=f"{name}lat{r}") for r in range(2)]
        t_lat = t_la[-1]
    s_wu = kb.dma_sem(f"{name}wu")
    t_wu = kb.dma("pool", s_wu, lambda g: g.dma_start(
        out=wu_sb[:], in_=wkvu_dram.rearrange("(l p) (c o) -> p l c o", p=128, o=128)))

    qs = e(nc.sbuf_tensor(f"{name}_qs", [128, H, T], BF16))
    wst = StripStream(kb, nc, ctx, f"{name}ws", [128, 512], BF16, 2, "pool")
    q_toks = [None] * H
    for og in range(4):
        mm_last = [None] * 4
        for dc in range(C):
            buf, sl, td = wst.load(lambda dc=dc, og=og: wq_dram[128 * dc:128 * dc + 128,
                                                               512 * og:512 * og + 512],
                                   tag=f"{name}qst")
            mms = []
            for b in range(4):
                for i in range(NT):
                    n = min(512, T - 512 * i)
                    dep = [td, t_xln]
                    if dc == 0 and pool.rtok[b] is not None:
                        dep += [pool.rtok[b], pool.wtok[b]]
                    t = kb.op("pe", lambda t_, dc=dc, b=b, i=i, n=n, buf=buf: t_.matmul(
                        out=pool.db[b][:, 512 * i:512 * i + n],
                        lhsT=buf[:, 128 * b:128 * b + 128],
                        rhs=xln[:, dc, 512 * i:512 * i + n],
                        start=(dc == 0), stop=(dc == C - 1)), deps=dep, tag=f"{name}qmm")
                    mms.append(t)
                    if dc == C - 1:
                        mm_last[b] = t
            wst.read_by(sl, [mms[-1]])
        for b in range(4):
            h = 4 * og + b
            pool.wtok[b] = mm_last[b]
            t = kb.op("act", lambda s, h=h, b=b: s.activation(
                out=qs[:, h, :], in_=pool.db[b][:, :T], func=AF.Copy,
                scale=0.08838834764831845), deps=[mm_last[b]], tag=f"{name}qev")
            pool.rtok[b] = t
            q_toks[h] = t

    m01 = xln
    t_m01 = kb.dma("sp", s_in, lambda sy: sy.dma_start(out=m01[:], in_=m01_dram[:]),
                   deps=q_toks, tag=f"{name}m01")

    kf = [e(nc.sbuf_tensor(f"{name}_kf{i}", [128, L], BF16)) for i in range(2)]
    vt = [e(nc.sbuf_tensor(f"{name}_vt{i}", [128, NKB, 128], BF16)) for i in range(2)]
    kf_rd = [[], []]
    vt_rd = [[], []]
    osb = e(nc.sbuf_tensor(f"{name}_osb", [128, H, T], BF16))
    pt = [e(nc.sbuf_tensor(f"{name}_pt{i}", [128, T], BF16)) for i in range(2)]
    pt_rd = [[], []]
    rowb = e(nc.sbuf_tensor(f"{name}_rowb", [1, T + 64], F32))
    sqv = e(nc.sbuf_tensor(f"{name}_sqv", [128, L], BF16))
    bcol = e(nc.sbuf_tensor(f"{name}_bcol", [128, 1], F32))
    rb = e(nc.sbuf_tensor(f"{name}_rb", [128, T], F32))
    st = {"sq_rd": [], "bcol_rd": [], "rb_rd": [], "rowb_rd": []}
    kv_toks = [None] * H
    osb_toks = [None] * H

    def emit_kv(h):
        sl = h % 2
        tk_ev = []
        nhalf = cdiv(L, 1024)
        for half in range(nhalf):
            w_half = min(1024, L - 1024 * half)
            mmk = []
            for lc in range(2):
                for i in range(cdiv(w_half, 512)):
                    n = min(512, w_half - 512 * i)
                    dep = [t_wu, t_lat]
                    if lc == 0 and i == 0:
                        dep += [pool.rtok[2], pool.wtok[2]]
                    t = kb.op("pe", lambda t_, lc=lc, i=i, n=n, h=h, half=half: t_.matmul(
                        out=pool.db[2][:, 512 * i:512 * i + n],
                        lhsT=wu_sb[:, lc, h, :],
                        rhs=latn[:, lc, 1024 * half + 512 * i:1024 * half + 512 * i + n],
                        start=(lc == 0), stop=(lc == 1)), deps=dep, tag=f"{name}kmm")
                    mmk.append(t)
            pool.wtok[2] = mmk[-1]
            te = kb.op("act", lambda s, half=half, sl=sl, w_half=w_half: s.copy(
                out=kf[sl][:, 1024 * half:1024 * half + w_half], in_=pool.db[2][:, :w_half]),
                deps=mmk + (kf_rd[sl] if half == 0 else []), tag=f"{name}kev")
            pool.rtok[2] = te
            tk_ev.append(te)
        kf_rd[sl] = []
        tv_ev = []
        prev = None
        for kb_ in range(NKB):
            mmv = []
            for lc in range(2):
                dep = [t_wu, t_lat]
                if lc == 0:
                    dep += [pool.rtok[3], pool.wtok[3]]
                t = kb.op("pe", lambda t_, lc=lc, kb_=kb_, h=h: t_.matmul(
                    out=pool.db[3][:, :128],
                    lhsT=latn[:, lc, 128 * kb_:128 * kb_ + 128],
                    rhs=wu_sb[:, lc, 16 + h, :],
                    start=(lc == 0), stop=(lc == 1)), deps=dep, tag=f"{name}vmm")
                mmv.append(t)
            pool.wtok[3] = mmv[-1]
            te = kb.op("act", lambda s, kb_=kb_, sl=sl: s.copy(
                out=vt[sl][:, kb_, :], in_=pool.db[3][:, :128]),
                deps=[mmv[-1]] + (vt_rd[sl] if kb_ == 0 else []) +
                ([prev] if prev else []), tag=f"{name}vev")
            pool.rtok[3] = te
            prev = te
            tv_ev.append(te)
        vt_rd[sl] = []
        kv_toks[h] = (tk_ev, tv_ev)

    def emit_norms(h):
        sl = h % 2
        tk_ev, _ = kv_toks[h]
        tq1 = kb.op("act", lambda s, h=h: s.activation(out=sqv[:, :T], in_=qs[:, h, :],
                                                       func=AF.Square),
                    deps=[q_toks[h]] + st["sq_rd"], tag=f"{name}nsq")
        st["sq_rd"] = []
        mmq = []
        for i in range(NT):
            n = min(512, T - 512 * i)
            dep = [tq1] + consts.tok
            if i == 0:
                dep += [pool.rtok[2], pool.wtok[2]]
            t = kb.op("pe", lambda t_, i=i, n=n: t_.matmul(
                out=pool.db[2][0:1, 512 * i:512 * i + n], lhsT=consts.ones_col_bf16[:],
                rhs=sqv[:, 512 * i:512 * i + n], start=True, stop=True),
                deps=dep, tag=f"{name}nqm")
            mmq.append(t)
        pool.wtok[2] = mmq[-1]
        t_mq = kb.op("dve", lambda v: v.reduce_max(out=rowb[0:1, T:T + 1],
                                                   in_=pool.db[2][0:1, :T], axis=AX.X),
                     deps=mmq + st["rowb_rd"], tag=f"{name}nqr")
        st["rowb_rd"] = []
        pool.rtok[2] = t_mq
        tk1 = kb.op("act", lambda s, sl=sl: s.activation(out=sqv[:, :L], in_=kf[sl][:, :],
                                                         func=AF.Square),
                    deps=tk_ev + mmq, tag=f"{name}nsk")
        kf_rd[sl].append(tk1)
        mx_toks = []
        nhalf = cdiv(L, 1024)
        for half in range(nhalf):
            w_half = min(1024, L - 1024 * half)
            mmk = []
            for i in range(cdiv(w_half, 512)):
                n = min(512, w_half - 512 * i)
                dep = [tk1] + consts.tok + [pool.rtok[2], pool.wtok[2]]
                t = kb.op("pe", lambda t_, i=i, n=n, half=half: t_.matmul(
                    out=pool.db[2][0:1, 512 * i:512 * i + n], lhsT=consts.ones_col_bf16[:],
                    rhs=sqv[:, 1024 * half + 512 * i:1024 * half + 512 * i + n],
                    start=True, stop=True), deps=dep, tag=f"{name}nkm")
                mmk.append(t)
            pool.wtok[2] = mmk[-1]
            t_mk = kb.op("dve", lambda v, half=half, w_half=w_half: v.reduce_max(
                out=rowb[0:1, T + 1 + half:T + 2 + half], in_=pool.db[2][0:1, :w_half],
                axis=AX.X), deps=mmk, tag=f"{name}nkr")
            pool.rtok[2] = t_mk
            mx_toks.append(t_mk)
        if nhalf > 1:
            t1 = kb.op("dve", lambda v: v.tensor_tensor(
                out=rowb[0:1, T + 1:T + 2], in0=rowb[0:1, T + 1:T + 2],
                in1=rowb[0:1, T + 2:T + 3], op=ALU.max), deps=mx_toks, tag=f"{name}nmx")
        else:
            t1 = mx_toks[-1]
        t2 = kb.op("dve", lambda v: v.tensor_tensor(
            out=rowb[0:1, T:T + 1], in0=rowb[0:1, T:T + 1],
            in1=rowb[0:1, T + 1:T + 2], op=ALU.mult), deps=[t1, t_mq], tag=f"{name}nmul")
        t3 = kb.op("act", lambda s: s.activation(out=rowb[0:1, T + 3:T + 4],
                                                 in_=rowb[0:1, T:T + 1], func=AF.Sqrt),
                   deps=[t2], tag=f"{name}nsqrt")
        t4 = kb.op("dve", lambda v: v.tensor_single_scalar(
            out=rowb[0:1, T + 4:T + 5], in_=rowb[0:1, T + 3:T + 4], scalar=-1.0,
            op=ALU.mult), deps=[t3], tag=f"{name}nneg")
        st["rowb_rd"].append(t4)
        dep = [t4] + consts.tok + [pool.rtok[3], pool.wtok[3]]
        t5 = kb.op("pe", lambda t_: t_.matmul(
            out=pool.db[3][:, 0:1], lhsT=consts.ones_row_f32[:],
            rhs=rowb[0:1, T + 4:T + 5], start=True, stop=True), deps=dep, tag=f"{name}nbc")
        pool.wtok[3] = t5
        t6 = kb.op("act", lambda s: s.copy(out=bcol[:], in_=pool.db[3][:, 0:1]),
                   deps=[t5] + st["bcol_rd"], tag=f"{name}nbe")
        st["bcol_rd"] = []
        pool.rtok[3] = t6
        return t6

    def emit_attn(h, t_bcol):
        sl = h % 2
        tk_ev, tv_ev = kv_toks[h]
        den_last = None
        av_last = []
        stmm_all = []
        avmm_all = []
        exp_toks = []
        prev_exp = [None, None]
        prev_mask = [None, None]
        for kb_ in range(NKB):
            psl = kb_ % 2
            dep = [tk_ev[(128 * kb_) // 1024], q_toks[h]]
            if prev_exp[psl] is not None:
                dep.append(prev_exp[psl])
            elif pool.rtok[psl] is not None:
                dep += [pool.rtok[psl], pool.wtok[psl]]
            st_mm = []
            for i in range(NT):
                n = min(512, T - 512 * i)
                t = kb.op("pe", lambda t_, kb_=kb_, i=i, n=n, sl=sl, psl=psl: t_.matmul(
                    out=pool.db[psl][:, 512 * i:512 * i + n],
                    lhsT=kf[sl][:, 128 * kb_:128 * kb_ + 128],
                    rhs=qs[:, h, 512 * i:512 * i + n], start=True, stop=True),
                    deps=dep, tag=f"{name}smm")
                st_mm.append(t)
            stmm_all.extend(st_mm)
            pool.wtok[psl] = st_mm[-1]
            dep_e = st_mm + [t_bcol] + pt_rd[psl]
            te = kb.op("act", lambda s, psl=psl: s.activation(
                out=pt[psl][:], in_=pool.db[psl][:, :T], func=AF.Exp, bias=bcol[:]),
                deps=dep_e, tag=f"{name}exp")
            pt_rd[psl] = []
            prev_exp[psl] = te
            pool.rtok[psl] = te
            exp_toks.append(te)
            tm = kb.op("dve", lambda v, kb_=kb_, psl=psl: v.tensor_tensor(
                out=pt[psl][:], in0=pt[psl][:], in1=m01[:, kb_, :], op=ALU.mult),
                deps=[te, t_m01], tag=f"{name}msk")
            for i in range(NT):
                n = min(512, T - 512 * i)
                dep_d = [tm] + consts.tok
                if kb_ == 0 and i == 0:
                    dep_d += [pool.rtok[2], pool.wtok[2]]
                t = kb.op("pe", lambda t_, i=i, n=n, kb_=kb_, psl=psl: t_.matmul(
                    out=pool.db[2][0:1, 512 * i:512 * i + n], lhsT=consts.ones_col_bf16[:],
                    rhs=pt[psl][:, 512 * i:512 * i + n], start=(kb_ == 0), stop=(kb_ == NKB - 1)),
                    deps=dep_d, tag=f"{name}dmm")
                pt_rd[psl].append(t)
                if kb_ == NKB - 1 and i == NT - 1:
                    den_last = t
            for i in range(NT):
                n = min(512, T - 512 * i)
                dep_a = [tm, tv_ev[kb_]]
                if kb_ == 0 and i == 0:
                    dep_a += [pool.rtok[3], pool.wtok[3]]
                t = kb.op("pe", lambda t_, i=i, n=n, kb_=kb_, sl=sl, psl=psl: t_.matmul(
                    out=pool.db[3][:, 512 * i:512 * i + n], lhsT=vt[sl][:, kb_, :],
                    rhs=pt[psl][:, 512 * i:512 * i + n], start=(kb_ == 0), stop=(kb_ == NKB - 1)),
                    deps=dep_a, tag=f"{name}avmm")
                pt_rd[psl].append(t)
                avmm_all.append(t)
                if kb_ == NKB - 1:
                    av_last.append(t)
        kf_rd[sl].extend(stmm_all)
        vt_rd[sl].extend(avmm_all)
        pool.wtok[2] = den_last
        pool.wtok[3] = av_last[-1]
        st["bcol_rd"].extend(exp_toks)
        t_rd = kb.op("dve", lambda v: v.reciprocal(out=rowb[0:1, :T], in_=pool.db[2][0:1, :T]),
                     deps=[den_last] + st["rowb_rd"], tag=f"{name}rden")
        st["rowb_rd"] = [t_rd]
        pool.rtok[2] = t_rd
        bt = bcast_row(kb, consts, pool, 0, rowb[0:1, :T], T, [t_rd], tag=f"{name}rbb")
        st["rowb_rd"].extend(bt)
        t_rb = kb.op("act", lambda s: s.copy(out=rb[:], in_=pool.db[0][:, :T]),
                     deps=bt + st["rb_rd"], tag=f"{name}rbe")
        st["rb_rd"] = []
        pool.rtok[0] = t_rb
        t_o = kb.op("dve", lambda v, h=h: v.tensor_tensor(
            out=osb[:, h, :], in0=pool.db[3][:, :T], in1=rb[:], op=ALU.mult),
            deps=av_last + [t_rb], tag=f"{name}oev")
        st["rb_rd"] = [t_o]
        pool.rtok[3] = t_o
        osb_toks[h] = t_o

    for h in range(H):
        emit_kv(h)
        tb = emit_norms(h)
        emit_attn(h, tb)

    xbuf = [e(nc.sbuf_tensor(f"{name}_xb{i}", [128, T], F32)) for i in range(2)]
    s_xb = [kb.dma_sem(f"{name}xb{i}") for i in range(2)]
    obuf = e(nc.sbuf_tensor(f"{name}_obuf", [128, T], F32))
    s_ob = kb.dma_sem(f"{name}ob")
    out_toks = []
    xb_rd = [[], []]
    ob_rd = None
    for og in range(4):
        mm_last = [None] * 4
        for dc in range(C):
            buf, sl, td = wst.load(lambda dc=dc, og=og: wo_dram[128 * dc:128 * dc + 128,
                                                               512 * og:512 * og + 512],
                                   tag=f"{name}ost")
            mms = []
            for b in range(4):
                for i in range(NT):
                    n = min(512, T - 512 * i)
                    dep = [td, osb_toks[dc]]
                    if dc == 0 and pool.rtok[b] is not None:
                        dep += [pool.rtok[b], pool.wtok[b]]
                    t = kb.op("pe", lambda t_, dc=dc, b=b, i=i, n=n, buf=buf: t_.matmul(
                        out=pool.db[b][:, 512 * i:512 * i + n],
                        lhsT=buf[:, 128 * b:128 * b + 128],
                        rhs=osb[:, dc, 512 * i:512 * i + n],
                        start=(dc == 0), stop=(dc == C - 1)), deps=dep, tag=f"{name}omm")
                    mms.append(t)
                    if dc == C - 1:
                        mm_last[b] = t
            wst.read_by(sl, [mms[-1]])
        for b in range(4):
            c = 4 * og + b
            pool.wtok[b] = mm_last[b]
            sl = c % 2
            tx = kb.dma("sp", s_xb[sl], lambda sy, c=c, sl=sl: sy.dma_start(
                out=xbuf[sl][:], in_=x3[c, :, :]), deps=xb_rd[sl], tag=f"{name}xd{c}")
            xb_rd[sl] = []
            t2 = kb.op("dve", lambda v, b=b, sl=sl: v.tensor_tensor(
                out=obuf[:], in0=pool.db[b][:, :T], in1=xbuf[sl][:], op=ALU.add),
                deps=[mm_last[b], tx] + ([ob_rd] if ob_rd else []), tag=f"{name}fin{c}")
            xb_rd[sl].append(t2)
            pool.rtok[b] = t2
            td2 = kb.dma("sp", s_ob, lambda g, c=c: g.dma_start(
                out=o3[c, :, :], in_=obuf[:]), deps=[t2], tag=f"{name}odma{c}")
            ob_rd = td2
            out_toks.append(td2)
    return out_toks


def emit_wtranspose(kb, nc, ctx, consts, pool, w_dram, out_dram, O, D, name,
                    scale_col=None, scale_tok=None, db=0):
    e = ctx.enter_context
    ND = D // 128
    strip = [e(nc.sbuf_tensor(f"{name}_w{i}", [128, D], BF16)) for i in range(2)]
    s_w = [kb.dma_sem(f"{name}w{i}") for i in range(2)]
    strip_rd = [[], []]
    trs = e(nc.sbuf_tensor(f"{name}_tr", [128, ND, 128], BF16))
    s_o = kb.dma_sem(f"{name}o")
    out_toks = []
    n_ot = cdiv(O, 128)
    prev_odma = None
    prev_ev = None
    for ot in range(n_ot):
        wo = min(128, O - 128 * ot)
        sl = ot % 2
        td = kb.dma("pool", s_w[sl], lambda g, ot=ot, wo=wo, sl=sl: g.dma_start(
            out=strip[sl][:wo, :], in_=w_dram[128 * ot:128 * ot + wo, :]),
            deps=strip_rd[sl], tag=f"{name}wd{ot}")
        strip_rd[sl] = []
        ev_toks = []
        for dc in range(ND):
            dep = [td] + consts.tok + [pool.rtok[db], pool.wtok[db]] + ([scale_tok] if scale_tok else [])
            tp = kb.op("pe", lambda t_, dc=dc, wo=wo, sl=sl: t_.transpose(
                out=pool.db[db][:, :64].bitcast(BF16)[:, :wo],
                in_=strip[sl][:wo, 128 * dc:128 * dc + 128],
                identity=consts.ident_bf16[:wo, :wo]), deps=dep, tag=f"{name}tp")
            pool.wtok[db] = tp
            strip_rd[sl].append(tp)
            if scale_col is not None:
                tev = kb.op("dve", lambda v, dc=dc, wo=wo: v.tensor_scalar(
                    out=trs[:, dc, :wo], in0=pool.db[db][:, :64].bitcast(BF16)[:, :wo],
                    scalar1=scale_col[:, dc:dc + 1], scalar2=None, op0=ALU.mult),
                    deps=[tp] + ([prev_odma] if prev_odma and dc == 0 else []) +
                    ([prev_ev] if prev_ev else []), tag=f"{name}te")
            else:
                tev = kb.op("act", lambda s, dc=dc, wo=wo: s.copy(
                    out=trs[:, dc, :wo], in_=pool.db[db][:, :64].bitcast(BF16)[:, :wo]),
                    deps=[tp] + ([prev_odma] if prev_odma and dc == 0 else []) +
                    ([prev_ev] if prev_ev else []), tag=f"{name}te")
            prev_ev = tev
            pool.rtok[db] = tev
            ev_toks.append(tev)
        tdo = kb.dma("sp", s_o, lambda g, ot=ot, wo=wo: g.dma_start(
            out=out_dram[:, 128 * ot:128 * ot + wo].rearrange("(dc p) o -> p dc o", p=128),
            in_=trs[:, :, :wo]), deps=ev_toks, tag=f"{name}odma{ot}")
        prev_odma = tdo
        out_toks.append(tdo)
    return out_toks


def emit_l0attn_a1(kb, nc, ctx, consts, pool, x_dram, winT_dram, cw_dram, cb_dram,
                   dtb_dram, a_dram, d_dram, y_dram, gate_dram, xbc_dram,
                   dacs_dram, dprod_dram, sout_dram, T=1024, name="ma",
                   triu_dram=None, idneg_dram=None, ltc_dram=None, sout_shared=None):
    e = ctx.enter_context
    C = 16
    TH = T + 3
    NC = T // 256
    NT = cdiv(T, 512)
    x3 = x_dram.rearrange("(c p) t -> c p t", p=128)

    xtk = e(nc.sbuf_tensor(f"{name}_xtk", [128, 2, 4352], BF16))
    _xtk_f32 = xtk[:].rearrange("p a b -> p (a b)").bitcast(F32)
    xbuf = [_xtk_f32[:, 0:TH], _xtk_f32[:, TH:2 * TH]]
    s_xb = [kb.dma_sem(f"{name}xb{i}") for i in range(2)]
    xbuf_state = [None, None]
    xbuf_rd = [[], []]

    def x_at_full(c):
        sl = c % 2
        cur = xbuf_state[sl]
        if cur is not None and cur[0] == c:
            return xbuf[sl][:], [cur[1]]
        dep = xbuf_rd[sl]
        t = kb.dma("sp", s_xb[sl], lambda sy, c=c, sl=sl: sy.dma_start(
            out=xbuf[sl][:], in_=x3[c, :, :]), deps=dep, tag=f"{name}xd{c}")
        xbuf_state[sl] = (c, t)
        xbuf_rd[sl] = []
        return xbuf[sl][:], [t]

    def x_used(c, toks):
        xbuf_rd[c % 2].extend(toks)

    xln_t = e(nc.sbuf_tensor(f"{name}_xln", [128, C, TH], BF16))
    xln = xln_t[:]
    ln_toks = emit_ln_fm(kb, nc, ctx, consts, pool,
                         lambda c: ((lambda ap, d: (ap[:, 3:TH], d))(*x_at_full(c))),
                         C, T, xln[:, :, 3:TH], 1e-5, [], name + "ln", x_done=x_used)
    lnh_toks = emit_ln_fm(kb, nc, ctx, consts, pool,
                          lambda c: ((lambda ap, d: (ap[:, 0:3], d))(*x_at_full(c))),
                          C, 3, xln[:, :, 0:3], 1e-5, [], name + "lh", x_done=x_used)
    ln_all = [None] * C
    for c in range(C):
        ln_all[c] = ln_toks[c]
    lnh_all = lnh_toks

    strips = StripStream(kb, nc, ctx, f"{name}st", [128, 512], BF16, 2, "pool")
    y_sb = e(nc.sbuf_tensor(f"{name}_ysb", [128, 2, 4096], BF16))
    halo_sb = y_sb[:].rearrange("p a b -> p (a b)")[0:3, 0:4352]
    xbc_pre = [e(nc.sbuf_tensor(f"{name}_xp{i}", [128, TH], BF16)) for i in range(2)]
    xbc_out = [e(nc.sbuf_tensor(f"{name}_xo{i}", [128, T], BF16)) for i in range(2)]
    tmp_cv = e(nc.sbuf_tensor(f"{name}_cv", [128, T], F32))
    cw_sb = e(nc.sbuf_tensor(f"{name}_cw", [128, 34, 4], F32))
    cb_sb = e(nc.sbuf_tensor(f"{name}_cb", [128, 34], F32))
    dtp_sb = e(nc.sbuf_tensor(f"{name}_dtp", [64, T], F32))
    s_c = kb.dma_sem(f"{name}c")
    t_cw = kb.dma("sp", s_c, lambda sy: sy.dma_start(out=cw_sb[:], in_=cw_dram[:]))
    t_cb = kb.dma("sp", s_c, lambda sy: sy.dma_start(out=cb_sb[:], in_=cb_dram[:]))
    s_xbc = kb.dma_sem(f"{name}xbc")
    xpre_rd = [[], []]
    xout_rd = [[], []]
    tmp_rd = []
    xbc_toks = [None] * 34
    dtp_tok = None
    halo_rd = []
    halo_all = []

    groups = [list(range(3 * g, min(3 * g + 3, 34))) for g in range(12)] + [[34]]
    for gi, grp in enumerate(groups):
        is_dtp = (grp == [34])
        col0 = 4096 + 128 * grp[0]
        width = 64 if is_dtp else 128 * len(grp)
        mm_last = [None] * len(grp)
        mm_halo = None
        for dc in range(C):
            buf, sl, td = strips.load(
                lambda dc=dc, col0=col0, width=width: winT_dram[128 * dc:128 * dc + 128,
                                                                col0:col0 + width],
                w=width, tag=f"{name}b1st")
            mms = []
            for b in range(len(grp)):
                wb = 64 if is_dtp else 128
                for tb_ in range(NT):
                    n = min(512, T - 512 * tb_)
                    dep2 = [td, ln_all[dc]]
                    if dc == 0 and pool.rtok[b] is not None:
                        dep2 += [pool.rtok[b], pool.wtok[b]]
                    t = kb.op("pe", lambda t_, dc=dc, b=b, tb_=tb_, n=n, buf=buf, wb=wb: t_.matmul(
                        out=pool.db[b][:wb, 512 * tb_:512 * tb_ + n],
                        lhsT=buf[:, 128 * b:128 * b + wb],
                        rhs=xln[:, dc, 3:3 + T][:, 512 * tb_:512 * tb_ + n],
                        start=(dc == 0), stop=(dc == C - 1)), deps=dep2, tag=f"{name}b1mm")
                    mms.append(t)
                    if dc == C - 1:
                        mm_last[b] = t
            if not is_dtp:
                dep2 = [td, lnh_all[dc]]
                if dc == 0 and pool.rtok[3] is not None:
                    dep2 += [pool.rtok[3], pool.wtok[3]]
                th = kb.op("pe", lambda t_, dc=dc, width=width, buf=buf: t_.matmul(
                    out=pool.db[3][0:3, :width], lhsT=xln[:, dc, 0:3],
                    rhs=buf[:, :width], start=(dc == 0), stop=(dc == C - 1)),
                    deps=dep2, tag=f"{name}hmm")
                mms.append(th)
                if dc == C - 1:
                    mm_halo = th
            strips.read_by(sl, [mms[-1], mms[-2] if len(mms) > 1 else mms[-1]])
        if is_dtp:
            pool.wtok[0] = mm_last[0]
            dtp_tok = kb.op("act", lambda s: s.copy(out=dtp_sb[:], in_=pool.db[0][:64, :T]),
                            deps=[mm_last[0]], tag=f"{name}dtpev")
            pool.rtok[0] = dtp_tok
            continue
        pool.wtok[3] = mm_halo
        t_he = kb.op("act", lambda s, col0=col0, width=width: s.copy(
            out=halo_sb[:, col0 - 4096:col0 - 4096 + width], in_=pool.db[3][0:3, :width]),
            deps=[mm_halo] + halo_rd, tag=f"{name}hev")
        halo_rd = []
        halo_all.append(t_he)
        pool.rtok[3] = t_he
        for b in range(len(grp)):
            blk = grp[b]
            sl2 = blk % 2
            pool.wtok[b] = mm_last[b]
            dep = [t_he] + consts.tok + [pool.rtok[3], pool.wtok[3]]
            tp = kb.op("pe", lambda t_, blk=blk: t_.transpose(
                out=pool.db[3][:, 64:66].bitcast(BF16)[:, :3],
                in_=halo_sb[:, 128 * blk:128 * blk + 128],
                identity=consts.ident_bf16[:3, :3]), deps=dep, tag=f"{name}htp")
            pool.wtok[3] = tp
            halo_rd.append(tp)
            halo_all.append(tp)
            t_hc = kb.op("act", lambda s, sl2=sl2: s.copy(
                out=xbc_pre[sl2][:, 0:3], in_=pool.db[3][:, 64:66].bitcast(BF16)[:, :3]),
                deps=[tp] + xpre_rd[sl2], tag=f"{name}hcp")
            pool.rtok[3] = t_hc
            t_mc = kb.op("act", lambda s, b=b, sl2=sl2: s.copy(
                out=xbc_pre[sl2][:, 3:TH], in_=pool.db[b][:, :T]),
                deps=[mm_last[b]] + xpre_rd[sl2], tag=f"{name}mcp")
            pool.rtok[b] = t_mc
            xpre_rd[sl2] = []
            t1 = kb.op("dve", lambda v, blk=blk, sl2=sl2: v.tensor_scalar(
                out=tmp_cv[:], in0=xbc_pre[sl2][:, 0:T], scalar1=cw_sb[:, blk, 0:1],
                scalar2=None, op0=ALU.mult), deps=[t_hc, t_mc, t_cw] + tmp_rd,
                tag=f"{name}cv0")
            for k_ in range(1, 4):
                t1 = kb.op("dve", lambda v, blk=blk, sl2=sl2, k_=k_: v.scalar_tensor_tensor(
                    out=tmp_cv[:], in0=xbc_pre[sl2][:, k_:k_ + T], scalar=cw_sb[:, blk, k_:k_ + 1],
                    in1=tmp_cv[:], op0=ALU.mult, op1=ALU.add), deps=[t1], tag=f"{name}cv{k_}")
            xpre_rd[sl2] = [t1]
            t_si = kb.op("act", lambda s, blk=blk, sl2=sl2: s.activation(
                out=xbc_out[sl2][:], in_=tmp_cv[:], func=AF.Silu, bias=cb_sb[:, blk:blk + 1]),
                deps=[t1, t_cb] + xout_rd[sl2], tag=f"{name}cvsi")
            tmp_rd = [t_si]
            td2 = kb.dma("pool", s_xbc, lambda g, blk=blk, sl2=sl2: g.dma_start(
                out=xbc_dram[128 * blk:128 * blk + 128, :], in_=xbc_out[sl2][:]),
                deps=[t_si], tag=f"{name}xbcd{blk}")
            xout_rd[sl2] = [td2]
            xbc_toks[blk] = td2

    NTB = T // 128
    xbc_ch = e(nc.sbuf_tensor(f"{name}_xch", [128, 34, 256], BF16))
    gbuf = xbc_ch[:].rearrange("p c t -> p (c t)")[:, 0:NTB * 512].rearrange(
        "p (a b) -> p a b", b=512)
    s_g = kb.dma_sem(f"{name}g")
    gate_toks = []
    gb_rd = []
    for oc in range(8):
        mm_last = [None] * NTB
        for dc in range(C):
            buf, sl, td = strips.load(
                lambda dc=dc, oc=oc: winT_dram[128 * dc:128 * dc + 128,
                                               512 * oc:512 * oc + 512],
                tag=f"{name}b2st")
            mms = []
            for tb_ in range(NTB):
                dbi, half = tb_ // 2, tb_ % 2
                dep2 = [td, ln_all[dc]]
                if dc == 0 and pool.rtok[dbi] is not None:
                    dep2 += [pool.rtok[dbi], pool.wtok[dbi]]
                t = kb.op("pe", lambda t_, dc=dc, tb_=tb_, dbi=dbi, half=half, buf=buf: t_.matmul(
                    out=pool.db[dbi][:, 512 * half:512 * half + 512],
                    lhsT=xln[:, dc, 3 + 128 * tb_:3 + 128 * tb_ + 128],
                    rhs=buf[:, :], start=(dc == 0), stop=(dc == C - 1)),
                    deps=dep2, tag=f"{name}b2mm")
                mms.append(t)
                if dc == C - 1:
                    mm_last[tb_] = t
            strips.read_by(sl, mms[-4:])
        ev = []
        for tb_ in range(NTB):
            dbi, half = tb_ // 2, tb_ % 2
            pool.wtok[dbi] = mm_last[tb_]
            t = kb.op("act", lambda s, tb_=tb_, dbi=dbi, half=half: s.copy(
                out=gbuf[:, tb_, :].squeeze(1) if False else gbuf[:, tb_, :],
                in_=pool.db[dbi][:, 512 * half:512 * half + 512]),
                deps=[mm_last[tb_]] + gb_rd, tag=f"{name}gev")
            pool.rtok[dbi] = t
            ev.append(t)
        gb_rd = []
        tdg = kb.dma("pool", s_g, lambda g, oc=oc: g.dma_start(
            out=gate_dram[:, 512 * oc:512 * oc + 512].rearrange("(tb p) o -> p tb o", p=128),
            in_=gbuf[:]), deps=ev, tag=f"{name}gd{oc}")
        gb_rd = [tdg]
        gate_toks.append(tdg)

    dtb_sb = e(nc.sbuf_tensor(f"{name}_dtb", [64, 1], F32))
    a_sb = e(nc.sbuf_tensor(f"{name}_a", [64, 1], F32))
    d_sb = e(nc.sbuf_tensor(f"{name}_d", [1, 64], F32))
    t_dtb = kb.dma("sp", s_c, lambda sy: sy.dma_start(out=dtb_sb[:], in_=dtb_dram[:]))
    t_a = kb.dma("sp", s_c, lambda sy: sy.dma_start(out=a_sb[:], in_=a_dram[:]))
    t_d = kb.dma("sp", s_c, lambda sy: sy.dma_start(out=d_sb[:], in_=d_dram[:]))
    dt_sb = e(nc.sbuf_tensor(f"{name}_dt", [64, T], F32))
    t64a = e(nc.sbuf_tensor(f"{name}_t64a", [64, T], F32))
    zt = t64a
    da_sb = dtp_sb
    t1 = kb.op("dve", lambda v: v.tensor_scalar(
        out=zt[:], in0=dtp_sb[:], scalar1=dtb_sb[:], scalar2=None, op0=ALU.add),
        deps=[dtp_tok, t_dtb], tag=f"{name}z")
    t2 = kb.op("act", lambda s: s.activation(out=dt_sb[:], in_=zt[:], func=AF.Abs),
               deps=[t1], tag=f"{name}abs")
    t3 = kb.op("act", lambda s: s.activation(out=dt_sb[:], in_=dt_sb[:], func=AF.Exp,
                                             scale=-1.0), deps=[t2], tag=f"{name}e")
    t4 = kb.op("dve", lambda v: v.tensor_single_scalar(out=dt_sb[:], in_=dt_sb[:],
                                                       scalar=1.0, op=ALU.add),
               deps=[t3], tag=f"{name}p1")
    t5 = kb.op("act", lambda s: s.activation(out=dt_sb[:], in_=dt_sb[:], func=AF.Ln),
               deps=[t4], tag=f"{name}ln1")
    t6 = kb.op("dve", lambda v: v.tensor_single_scalar(out=zt[:], in_=zt[:],
                                                       scalar=0.0, op=ALU.max),
               deps=[t5, t1], tag=f"{name}mx")
    t_dt = kb.op("dve", lambda v: v.tensor_tensor(out=dt_sb[:], in0=dt_sb[:], in1=zt[:],
                                                  op=ALU.add), deps=[t6], tag=f"{name}dt")
    t_da = kb.op("dve", lambda v: v.tensor_scalar(
        out=da_sb[:], in0=dt_sb[:], scalar1=a_sb[:], scalar2=None, op0=ALU.mult),
        deps=[t_dt, t_a], tag=f"{name}da")
    ones64 = e(nc.sbuf_tensor(f"{name}_o64", [64, 256], F32))
    t_o64 = kb.op("pool", lambda g: g.memset(ones64[:], 1.0))
    dacs_sb = e(nc.sbuf_tensor(f"{name}_dacs", [64, T], F32))
    dacs_toks = []
    prev = None
    for c in range(NC):
        t = kb.op("dve", lambda v, c=c: v.tensor_tensor_scan(
            out=dacs_sb[:, 256 * c:256 * c + 256], data0=da_sb[:, 256 * c:256 * c + 256],
            data1=ones64[:], initial=0.0, op0=ALU.add, op1=ALU.mult),
            deps=[t_da, t_o64] + ([prev] if prev else []), tag=f"{name}cs{c}")
        prev = t
        dacs_toks.append(t)
    dpr = e(nc.sbuf_tensor(f"{name}_dpr", [64, NC + 2], F32))
    lcs = dpr[:, NC:NC + 1]
    t_z = kb.op("pool", lambda g: g.memset(dpr[:, NC:NC + 2], 0.0))
    dp_toks = []
    prev = t_z
    for c in range(NC):
        ta_ = kb.op("act", lambda s, c=c: s.activation(out=dpr[:, c:c + 1], in_=lcs,
                                                       func=AF.Exp), deps=[prev], tag=f"{name}dp{c}")
        tb_ = kb.op("dve", lambda v, c=c: v.tensor_tensor(
            out=lcs, in0=lcs, in1=dacs_sb[:, 256 * c + 255:256 * c + 256], op=ALU.add),
            deps=[ta_, dacs_toks[c]], tag=f"{name}lc{c}")
        prev = tb_
        dp_toks.append(ta_)
    s_out2 = kb.dma_sem(f"{name}o2")
    t_dacs_d = kb.dma("sp", s_out2, lambda sy: sy.dma_start(out=dacs_dram[:], in_=dacs_sb[:]),
                      deps=dacs_toks, tag=f"{name}dacsd")
    t_dpr_d = kb.dma("sp", s_out2, lambda sy: sy.dma_start(out=dprod_dram[:], in_=dpr[:, :NC]),
                     deps=dp_toks + [prev], tag=f"{name}dprd")

    ncl = t64a
    tl1 = kb.op("act", lambda s: s.activation(out=ncl[:], in_=dt_sb[:], func=AF.Ln),
                deps=[t_dt, t_da] + dacs_toks, tag=f"{name}lndt")
    tl2 = kb.op("dve", lambda v: v.tensor_tensor(out=ncl[:], in0=ncl[:], in1=dacs_sb[:],
                                                 op=ALU.subtract),
                deps=[tl1] + dacs_toks, tag=f"{name}ncl")
    ncl_dram = nc.dram_tensor(f"{name}_ncld", [64, T], F32, kind="Internal").ap()
    t_ncld = kb.dma("sp", s_out2, lambda sy: sy.dma_start(out=ncl_dram[:], in_=ncl[:]),
                    deps=[tl2], tag=f"{name}ncld")

    s_all = e(nc.sbuf_tensor(f"{name}_S", [128, 64, 64], F32))
    s_bf = e(nc.sbuf_tensor(f"{name}_Sb", [128, 64, 64], BF16))
    t_s0 = kb.op("pool", lambda g: g.memset(s_all[:], 0.0))
    t_s0b = kb.op("pool", lambda g: g.memset(s_bf[:], 0.0), deps=[t_s0])
    cbts = e(nc.sbuf_tensor(f"{name}_cbts", [128, 2, 256], BF16))
    m_sb = e(nc.sbuf_tensor(f"{name}_m", [128, 2, 256], BF16))
    bdt = e(nc.sbuf_tensor(f"{name}_bd", [128, 2, 128], BF16))
    ytmp = e(nc.sbuf_tensor(f"{name}_yt", [128, 2, 64], F32))
    rowsa = e(nc.sbuf_tensor(f"{name}_ra", [1, 1024], F32))
    rowsb = e(nc.sbuf_tensor(f"{name}_rb", [1, 1024], F32))
    idneg = e(nc.sbuf_tensor(f"{name}_idn", [128, 128], BF16))
    ltc01 = e(nc.sbuf_tensor(f"{name}_ltc", [128, 128], BF16))
    s_tri2 = kb.dma_sem(f"{name}tri2")
    t_idn = kb.dma("sp", s_tri2, lambda sy: sy.dma_start(out=idneg[:], in_=idneg_dram[:]))
    t_ltc = kb.dma("sp", s_tri2, lambda sy: sy.dma_start(out=ltc01[:], in_=ltc_dram[:]))
    scl = e(nc.sbuf_tensor(f"{name}_scl", [128, 10, 64], F32))
    ones256 = e(nc.sbuf_tensor(f"{name}_o256", [1, 256], F32))
    t_o256 = kb.op("pool", lambda g: g.memset(ones256[:], 1.0))
    rows1 = e(nc.sbuf_tensor(f"{name}_r1", [1, 64], F32))
    s_xch = kb.dma_sem(f"{name}xch")
    s_y = kb.dma_sem(f"{name}y")
    s_ra = kb.dma_sem(f"{name}ra")
    s_rb2 = kb.dma_sem(f"{name}rb2")
    dep = [t_d] + consts.tok + [pool.rtok[3], pool.wtok[3]]
    tDb = kb.op("pe", lambda t_: t_.matmul(out=pool.db[3][:, :64], lhsT=consts.ones_row_f32[:],
                                           rhs=d_sb[:], start=True, stop=True),
                deps=dep, tag=f"{name}Dbc")
    pool.wtok[3] = tDb
    tDe = kb.op("act", lambda s: s.copy(out=scl[:, 5, :], in_=pool.db[3][:, :64]),
                deps=[tDb], tag=f"{name}Dev")
    pool.rtok[3] = tDe

    state = {"xch_rd": [], "xtk_rd": xbuf_rd[0] + xbuf_rd[1] + [xbuf_state[0][1], xbuf_state[1][1]],
             "scl_rd": [], "y_rd": list(halo_all), "r1_rd": [],
             "m_rd": [], "bd_rd": [], "yt_rd": [], "cbts_rd": [], "ra_rd": [],
             "rb_rd": [], "s_upd": None}
    y_dma_toks = []
    for c in range(NC):
        t_x = kb.dma("sp", s_xch, lambda sy, c=c: sy.dma_start(
            out=xbc_ch[:], in_=xbc_dram[:, 256 * c:256 * c + 256].rearrange(
                "(cc p) t -> p cc t", p=128)),
            deps=state["xch_rd"] + list(xbc_toks) + gate_toks,
            tag=f"{name}xchd{c}")
        state["xch_rd"] = []
        xtk_toks = {}
        prev_ev = None
        for cc in range(34):
            for sb in range(2):
                dep = [t_x] + consts.tok + [pool.rtok[0], pool.wtok[0]]
                tp = kb.op("pe", lambda t_, cc=cc, sb=sb: t_.transpose(
                    out=pool.db[0][:, :64].bitcast(BF16), in_=xbc_ch[:, cc, 128 * sb:128 * sb + 128],
                    identity=consts.ident_bf16[:]), deps=dep, tag=f"{name}xtp")
                pool.wtok[0] = tp
                state["xch_rd"].append(tp)
                te = kb.op("act", lambda s, cc=cc, sb=sb: s.copy(
                    out=xtk[:, sb, 128 * cc:128 * cc + 128],
                    in_=pool.db[0][:, :64].bitcast(BF16)),
                    deps=[tp] + state["xtk_rd"] + ([prev_ev] if prev_ev else []),
                    tag=f"{name}xte")
                state["xtk_rd"] = []
                pool.rtok[0] = te
                prev_ev = te
                xtk_toks[(cc, sb)] = te
        trans_toks = {}
        for (slot, srcten, srctok) in [(6, dacs_sb, dacs_toks[c]), (8, dt_sb, t_dt)]:
            for sb in range(2):
                dep = [srctok] + consts.tok + [pool.rtok[3], pool.wtok[3]]
                tp = kb.op("pe", lambda t_, c=c, sb=sb, srcten=srcten: t_.transpose(
                    out=pool.db[3][:, :64],
                    in_=srcten[:, 256 * c + 128 * sb:256 * c + 128 * sb + 128],
                    identity=consts.ident_f32[:64, :64]), deps=dep, tag=f"{name}tp{slot}")
                pool.wtok[3] = tp
                te = kb.op("act", lambda s, slot=slot, sb=sb: s.copy(
                    out=scl[:, slot + sb, :], in_=pool.db[3][:, :64]),
                    deps=[tp] + state["scl_rd"], tag=f"{name}te{slot}")
                state["scl_rd"] = []
                pool.rtok[3] = te
                trans_toks[(slot, sb)] = te
        ea_toks = []
        pe_ = None
        for sb in range(2):
            t = kb.op("act", lambda s, sb=sb: s.activation(
                out=scl[:, 0 + sb, :], in_=scl[:, 6 + sb, :], func=AF.Exp),
                deps=[trans_toks[(6, sb)]] + ([pe_] if pe_ else []), tag=f"{name}ea{sb}")
            pe_ = t
            ea_toks.append(t)
        dep = [dacs_toks[c]] + consts.tok + [pool.rtok[3], pool.wtok[3]]
        tpl = kb.op("pe", lambda t_, c=c: t_.transpose(
            out=pool.db[3][0:1, :64], in_=dacs_sb[:, 256 * c + 255:256 * c + 256],
            identity=consts.ident_f32[:64, :64]), deps=dep, tag=f"{name}lrtp")
        pool.wtok[3] = tpl
        tle = kb.op("act", lambda s: s.copy(out=rows1[:], in_=pool.db[3][0:1, :64]),
                    deps=[tpl] + state["r1_rd"], tag=f"{name}lre")
        state["r1_rd"] = []
        pool.rtok[3] = tle
        dep = [tle] + consts.tok + [pool.rtok[3], pool.wtok[3]]
        tlb = kb.op("pe", lambda t_: t_.matmul(
            out=pool.db[3][:, 512:576], lhsT=consts.ones_row_f32[:], rhs=rows1[:],
            start=True, stop=True), deps=dep, tag=f"{name}lbb")
        pool.wtok[3] = tlb
        dte_t = []
        for sb in range(2):
            t1_ = kb.op("dve", lambda v, sb=sb: v.tensor_tensor(
                out=scl[:, 2 + sb, :], in0=pool.db[3][:, 512:576], in1=scl[:, 6 + sb, :],
                op=ALU.subtract), deps=[tlb, trans_toks[(6, sb)]], tag=f"{name}dts{sb}")
            t2_ = kb.op("act", lambda s, sb=sb: s.activation(
                out=scl[:, 2 + sb, :], in_=scl[:, 2 + sb, :], func=AF.Exp),
                deps=[t1_, pe_], tag=f"{name}dte{sb}")
            t3_ = kb.op("dve", lambda v, sb=sb: v.tensor_tensor(
                out=scl[:, 2 + sb, :], in0=scl[:, 2 + sb, :], in1=scl[:, 8 + sb, :],
                op=ALU.mult), deps=[t2_, trans_toks[(8, sb)]], tag=f"{name}dtm{sb}")
            dte_t.append(t3_)
        pool.rtok[3] = dte_t[0]
        tce = kb.op("act", lambda s: s.activation(out=rows1[:], in_=rows1[:], func=AF.Exp),
                    deps=[tle, tlb], tag=f"{name}cde")
        state["r1_rd"].append(tce)
        dep = [tce] + consts.tok + [pool.rtok[3], pool.wtok[3]]
        tcb2 = kb.op("pe", lambda t_: t_.matmul(
            out=pool.db[3][:, 512:576], lhsT=consts.ones_row_f32[:], rhs=rows1[:],
            start=True, stop=True), deps=dep + dte_t, tag=f"{name}cdb")
        pool.wtok[3] = tcb2
        state["r1_rd"].append(tcb2)
        tc9 = kb.op("act", lambda s: s.copy(out=scl[:, 4, :], in_=pool.db[3][:, 512:576]),
                    deps=[tcb2], tag=f"{name}cd9")
        pool.rtok[3] = tc9
        cbt_mm = []
        for sb in range(2):
            dep = [t_x] + [pool.rtok[0], pool.wtok[0]]
            t = kb.op("pe", lambda t_, sb=sb: t_.matmul(
                out=pool.db[0][:, 512 * sb:512 * sb + 256],
                lhsT=xbc_ch[:, 32, 128 * sb:128 * sb + 128],
                rhs=xbc_ch[:, 33, :], start=True, stop=True), deps=dep, tag=f"{name}cbt{sb}")
            pool.wtok[0] = t
            cbt_mm.append(t)
        cbe = []
        for sb in range(2):
            t = kb.op("act", lambda s, sb=sb: s.copy(
                out=cbts[:, sb, :], in_=pool.db[0][:, 512 * sb:512 * sb + 256]),
                deps=[cbt_mm[sb]] + state["cbts_rd"], tag=f"{name}cbe{sb}")
            state["cbts_rd"] = []
            cbe.append(t)
        pool.rtok[0] = cbe[-1]

        if c == 0:
            sup = [t_s0b] * 64
        else:
            sup = state["s_upd"]
        new_sup = [None] * 64
        rd = state.setdefault("reg_rd", {k: [] for k in
                                         ["db1r0", "db1r1", "db2r0", "db2r1",
                                          "db3v0", "db3v1", "db3w"]})
        grp_toks = [None, None]
        y_wr_local = []
        for h in range(64):
            j2 = h % 4
            if j2 == 0:
                g0 = h
                ta_ = kb.dma("sp", s_ra, lambda sy, c=c, g0=g0: sy.dma_start(
                    out=rowsa[:].rearrange("o (h t) -> o h t", t=256),
                    in_=dacs_dram[g0:g0 + 4, 256 * c:256 * c + 256].unsqueeze(0)),
                    deps=state["ra_rd"] + [t_dacs_d], tag=f"{name}rga")
                tb2_ = kb.dma("sp", s_rb2, lambda sy, c=c, g0=g0: sy.dma_start(
                    out=rowsb[:].rearrange("o (h t) -> o h t", t=256),
                    in_=ncl_dram[g0:g0 + 4, 256 * c:256 * c + 256].unsqueeze(0)),
                    deps=state["rb_rd"] + [t_ncld], tag=f"{name}rgb")
                state["ra_rd"] = []
                state["rb_rd"] = []
                grp_toks = [ta_, tb2_]
            xh0 = 64 * (h % 2)
            cc_h = h // 2
            xh_t0 = xtk_toks[(cc_h, 0)]
            xh_t1 = xtk_toks[(cc_h, 1)]
            dep0 = [grp_toks[0], grp_toks[1], t_idn, t_ltc, t_o256] + consts.tok + rd["db1r0"]
            rd["db1r0"] = []
            e1 = kb.op("pe", lambda t_, j2=j2: t_.matmul(
                out=pool.db[1][:, 0:256], lhsT=consts.ones_row_f32[:],
                rhs=rowsa[:, 256 * j2:256 * j2 + 256], start=True, stop=False),
                deps=dep0, tag=f"{name}e1")
            e2 = kb.op("pe", lambda t_, j2=j2: t_.matmul(
                out=pool.db[1][:, 0:256], lhsT=rowsb[:, 256 * j2:256 * j2 + 128],
                rhs=ones256[:], start=False, stop=False), deps=dep0, tag=f"{name}e2")
            e3 = kb.op("pe", lambda t_: t_.matmul(
                out=pool.db[1][:, 0:128], lhsT=idneg[:], rhs=ltc01[:],
                start=False, stop=True), deps=dep0, tag=f"{name}e3")
            dep1 = [grp_toks[0], grp_toks[1], t_o256] + rd["db1r1"]
            rd["db1r1"] = []
            f1 = kb.op("pe", lambda t_, j2=j2: t_.matmul(
                out=pool.db[1][:, 512:640], lhsT=consts.ones_row_f32[:],
                rhs=rowsa[:, 256 * j2 + 128:256 * j2 + 256], start=True, stop=False),
                deps=dep1, tag=f"{name}f1")
            f2 = kb.op("pe", lambda t_, j2=j2: t_.matmul(
                out=pool.db[1][:, 512:640], lhsT=rowsb[:, 256 * j2 + 128:256 * j2 + 256],
                rhs=ones256[:, 0:128], start=False, stop=False), deps=dep1, tag=f"{name}f2")
            f3 = kb.op("pe", lambda t_: t_.matmul(
                out=pool.db[1][:, 512:640], lhsT=idneg[:], rhs=ltc01[:],
                start=False, stop=True), deps=dep1, tag=f"{name}f3")
            state["ra_rd"] += [e1, f1]
            state["rb_rd"] += [e2, f2]
            x1_ = kb.op("act", lambda s: s.activation(
                out=m_sb[:, 0, :], in_=pool.db[1][:, 0:256], func=AF.Exp),
                deps=[e1, e2, e3] + state["m_rd"], tag=f"{name}xm0")
            state["m_rd"] = []
            x2_ = kb.op("act", lambda s: s.activation(
                out=m_sb[:, 1, 128:256], in_=pool.db[1][:, 512:640], func=AF.Exp),
                deps=[f1, f2, f3, x1_], tag=f"{name}xm1")
            rd["db1r0"].append(x1_)
            rd["db1r1"].append(x2_)
            p1_ = kb.op("dve", lambda g: g.tensor_tensor(
                out=m_sb[:, 0, :], in0=m_sb[:, 0, :], in1=cbts[:, 0, :], op=ALU.mult),
                deps=[x1_] + cbe, tag=f"{name}mc0")
            p2_ = kb.op("dve", lambda g: g.tensor_tensor(
                out=m_sb[:, 1, 128:256], in0=m_sb[:, 1, 128:256], in1=cbts[:, 1, 128:256],
                op=ALU.mult), deps=[x2_, p1_] + cbe, tag=f"{name}mc1")
            b1_ = kb.op("dve", lambda v, h=h: v.tensor_scalar(
                out=bdt[:, 0, :], in0=xtk[:, 0, 4096:4224], scalar1=scl[:, 2, h:h + 1],
                scalar2=None, op0=ALU.mult), deps=[xtk_toks[(32, 0)], dte_t[0], p2_] +
                state["bd_rd"], tag=f"{name}bd0")
            state["bd_rd"] = []
            b2_ = kb.op("dve", lambda v, h=h: v.tensor_scalar(
                out=bdt[:, 1, :], in0=xtk[:, 1, 4096:4224], scalar1=scl[:, 3, h:h + 1],
                scalar2=None, op0=ALU.mult), deps=[xtk_toks[(32, 1)], dte_t[1], b1_],
                tag=f"{name}bd1")
            u0 = kb.op("pe", lambda t_, h=h, xh0=xh0, cc_h=cc_h: t_.matmul(
                out=pool.db[2][:, 0:64], lhsT=m_sb[:, 0, 0:128],
                rhs=xtk[:, 0, 128 * cc_h + xh0:128 * cc_h + xh0 + 64],
                start=True, stop=True), deps=[p1_, xh_t0] + rd["db2r0"], tag=f"{name}u0")
            rd["db2r0"] = []
            u1a = kb.op("pe", lambda t_, h=h, xh0=xh0, cc_h=cc_h: t_.matmul(
                out=pool.db[2][:, 512:576], lhsT=m_sb[:, 0, 128:256],
                rhs=xtk[:, 0, 128 * cc_h + xh0:128 * cc_h + xh0 + 64],
                start=True, stop=False), deps=[p1_, xh_t0] + rd["db2r1"], tag=f"{name}u1a")
            rd["db2r1"] = []
            u1b = kb.op("pe", lambda t_, h=h, xh0=xh0, cc_h=cc_h: t_.matmul(
                out=pool.db[2][:, 512:576], lhsT=m_sb[:, 1, 128:256],
                rhs=xtk[:, 1, 128 * cc_h + xh0:128 * cc_h + xh0 + 64],
                start=False, stop=True), deps=[p2_, xh_t1], tag=f"{name}u1b")
            state["m_rd"] += [u0, u1a, u1b]
            v0 = kb.op("pe", lambda t_, h=h: t_.matmul(
                out=pool.db[3][:, 0:64], lhsT=xbc_ch[:, 33, 0:128],
                rhs=s_bf[:, h, :], start=True, stop=True),
                deps=[t_x, sup[h]] + rd["db3v0"], tag=f"{name}v0")
            rd["db3v0"] = []
            v1 = kb.op("pe", lambda t_, h=h: t_.matmul(
                out=pool.db[3][:, 64:128], lhsT=xbc_ch[:, 33, 128:256],
                rhs=s_bf[:, h, :], start=True, stop=True),
                deps=[t_x, sup[h]] + rd["db3v1"], tag=f"{name}v1")
            rd["db3v1"] = []
            w1 = kb.op("pe", lambda t_, h=h, xh0=xh0, cc_h=cc_h: t_.matmul(
                out=pool.db[3][:, 512:576], lhsT=bdt[:, 0, :],
                rhs=xtk[:, 0, 128 * cc_h + xh0:128 * cc_h + xh0 + 64],
                start=True, stop=False), deps=[b1_, xh_t0] + rd["db3w"], tag=f"{name}w1")
            rd["db3w"] = []
            w2 = kb.op("pe", lambda t_, h=h, xh0=xh0, cc_h=cc_h: t_.matmul(
                out=pool.db[3][:, 512:576], lhsT=bdt[:, 1, :],
                rhs=xtk[:, 1, 128 * cc_h + xh0:128 * cc_h + xh0 + 64],
                start=False, stop=True), deps=[b2_, xh_t1], tag=f"{name}w2")
            state["bd_rd"] = [w1, w2]
            asm = []
            for qb, (vmm, offv, umm) in enumerate([(v0, 0, u0), (v1, 64, u1b)]):
                a1_ = kb.op("act", lambda s, h=h, qb=qb, offv=offv: s.activation(
                    out=ytmp[:, qb, :], in_=pool.db[3][:, offv:offv + 64], func=AF.Copy,
                    scale=scl[:, 0 + qb, h:h + 1]), deps=[vmm, ea_toks[qb]] +
                    state["yt_rd"], tag=f"{name}as{qb}a")
                state["yt_rd"] = []
                a2_ = kb.op("dve", lambda v, h=h, qb=qb: v.tensor_tensor(
                    out=ytmp[:, qb, :], in0=ytmp[:, qb, :],
                    in1=pool.db[2][:, 512 * qb:512 * qb + 64], op=ALU.add),
                    deps=[a1_, umm], tag=f"{name}as{qb}b")
                a3_ = kb.op("dve", lambda v, h=h, qb=qb, xh0=xh0, cc_h=cc_h: v.scalar_tensor_tensor(
                    out=y_sb[:, qb, 64 * h:64 * h + 64],
                    in0=xtk[:, qb, 128 * cc_h + xh0:128 * cc_h + xh0 + 64],
                    scalar=scl[:, 5, h:h + 1], in1=ytmp[:, qb, :], op0=ALU.mult,
                    op1=ALU.add), deps=[a2_, xh_t0 if qb == 0 else xh_t1, tDe] +
                    state["y_rd"], tag=f"{name}as{qb}c")
                state["y_rd"] = []
                asm.append((a1_, a2_, a3_))
                state["yt_rd"] = [a3_]
                y_wr_local.append(a3_)
            rd["db3v0"].append(asm[0][0])
            rd["db3v1"].append(asm[1][0])
            rd["db2r0"].append(asm[0][1])
            rd["db2r1"].append(asm[1][1])
            su0 = kb.op("dve", lambda v, h=h: v.scalar_tensor_tensor(
                out=s_all[:, h, :], in0=s_all[:, h, :], scalar=scl[:, 4, h:h + 1],
                in1=pool.db[3][:, 512:576], op0=ALU.mult, op1=ALU.add),
                deps=[w2, v0, v1, tc9, sup[h]], tag=f"{name}scan")
            su = kb.op("dve", lambda v, h=h: v.tensor_copy(
                out=s_bf[:, h, :], in_=s_all[:, h, :]), deps=[su0], tag=f"{name}scanb")
            rd["db3w"].append(su0)
            new_sup[h] = su
        state["s_upd"] = new_sup
        state["scl_rd"] = [asm[0][0], asm[1][0], su0, b1_, b2_]
        state["xtk_rd"] = [w2, asm[1][2]]
        state["xch_rd"].extend([v1, w2, su0])
        state["cbts_rd"] = [p1_, p2_]
        pool.rtok[1] = x2_
        pool.rtok[2] = asm[1][1]
        pool.rtok[3] = su0
        pool.wtok[1] = f3
        pool.wtok[2] = u1b
        pool.wtok[3] = w2
        tdy = kb.dma("sp", s_y, lambda sy, c=c: sy.dma_start(
            out=y_dram[256 * c:256 * c + 256, :].rearrange("(qb p) f -> p qb f", p=128),
            in_=y_sb[:]), deps=y_wr_local, tag=f"{name}yd{c}")
        state["y_rd"] = [tdy]
        y_dma_toks.append(tdy)

    if sout_shared is None:
        t_sd = kb.dma("sp", s_out2, lambda sy: sy.dma_start(
            out=sout_dram[:], in_=s_all[:].rearrange("p h q -> p (h q)")),
            deps=state["s_upd"], tag=f"{name}soutd")
    else:
        t_sd = kb.dma("sp", s_out2, lambda sy: sy.dma_start(
            out=sout_shared[bass.ds(sy.partition_id() % 2 * 128, 128), :],
            in_=s_all[:].rearrange("p h q -> p (h q)")),
            deps=state["s_upd"], tag=f"{name}soutd")
    return y_dma_toks + gate_toks + list(xbc_toks) + [t_dacs_d, t_dpr_d, t_sd]


def emit_l0attn_a2(kb, nc, ctx, consts, pool, x_dram, y_dram, gate_dram, xbc_dram,
                   dacs_dram, dprod_dram, sin_dram, woT_dram, nw_dram, out_dram,
                   T=1024, name="mb", sin_shared=None, sel_dram=None, sin_deps=()):
    e = ctx.enter_context
    C = 16
    NC = T // 256
    NT = cdiv(T, 512)
    x3 = x_dram.rearrange("(c p) t -> c p t", p=128)
    o3 = out_dram.rearrange("(c p) t -> c p t", p=128)

    sin_sb = e(nc.sbuf_tensor(f"{name}_sin", [128, 64, 64], BF16))
    dacs_sb = e(nc.sbuf_tensor(f"{name}_dacs", [64, T], F32))
    dpr_sb = e(nc.sbuf_tensor(f"{name}_dpr", [64, NC], F32))
    nw_sb = e(nc.sbuf_tensor(f"{name}_nw", [128, 32], F32))
    s_in = kb.dma_sem(f"{name}in")
    s_in2 = kb.dma_sem(f"{name}in2")
    if sin_shared is None:
        t_sin = kb.dma("pool", s_in2, lambda g: g.dma_start(
            out=sin_sb[:], in_=sin_dram.rearrange("p (h q) -> p h q", q=64)))
    else:
        e2 = ctx.enter_context
        sin_f32 = e2(nc.sbuf_tensor(f"{name}_sinf", [128, 4096], F32))
        sel_sb = e2(nc.sbuf_tensor(f"{name}_sel", [128, 1], F32))
        t_sf = kb.dma("sp", s_in2, lambda sy: sy.dma_start(
            out=sin_f32[:],
            in_=sin_shared[bass.ds((1 - sy.partition_id() % 2) * 128, 128), :]),
            deps=list(sin_deps), tag=f"{name}sinsh")
        t_se = kb.dma("sp", s_in2, lambda sy: sy.dma_start(out=sel_sb[:], in_=sel_dram[:]))
        t_sin = kb.op("dve", lambda v: v.tensor_scalar(
            out=sin_sb[:].rearrange("p h q -> p (h q)"), in0=sin_f32[:],
            scalar1=sel_sb[:], scalar2=None, op0=ALU.mult), deps=[t_sf, t_se],
            tag=f"{name}sinsel")
    t_dacs = kb.dma("sp", s_in, lambda sy: sy.dma_start(out=dacs_sb[:], in_=dacs_dram[:]))
    t_dpr = kb.dma("sp", s_in, lambda sy: sy.dma_start(out=dpr_sb[:], in_=dprod_dram[:]))
    t_nw = kb.dma("sp", s_in, lambda sy: sy.dma_start(out=nw_sb[:], in_=nw_dram[:]))

    ybuf = e(nc.sbuf_tensor(f"{name}_y", [128, 2, 4096], BF16))
    gbuf = e(nc.sbuf_tensor(f"{name}_g", [128, 2, 4096], BF16))
    cf = e(nc.sbuf_tensor(f"{name}_cf", [128, 256], BF16))
    scl = e(nc.sbuf_tensor(f"{name}_scl", [128, 4, 64], F32))
    ytmp = e(nc.sbuf_tensor(f"{name}_yt", [128, 64], F32))
    col = e(nc.sbuf_tensor(f"{name}_col", [128, 4], F32))
    rows1 = e(nc.sbuf_tensor(f"{name}_r1", [1, 64], F32))
    gnT = e(nc.sbuf_tensor(f"{name}_gnT", [128, 32, T], BF16))
    s_y = kb.dma_sem(f"{name}y")
    s_g2 = kb.dma_sem(f"{name}g2")
    s_cf = kb.dma_sem(f"{name}cf")
    st = {"y_rd": [], "g_rd": [], "cf_rd": [], "scl_rd": [], "r1_rd": [], "gnT_rd": [],
          "yt_rd": [], "col_rd": []}
    gnT_toks = {}
    for c in range(NC):
        t_y = kb.dma("sp", s_y, lambda sy, c=c: sy.dma_start(
            out=ybuf[:], in_=y_dram[256 * c:256 * c + 256, :].rearrange(
                "(qb p) f -> p qb f", p=128)), deps=st["y_rd"], tag=f"{name}yd{c}")
        st["y_rd"] = []
        t_cf = kb.dma("sp", s_cf, lambda sy, c=c: sy.dma_start(
            out=cf[:], in_=xbc_dram[4224:4352, 256 * c:256 * c + 256]),
            deps=st["cf_rd"], tag=f"{name}cfd{c}")
        st["cf_rd"] = []
        ea_toks = []
        for sb in range(2):
            dep = [t_dacs] + consts.tok + [pool.rtok[3], pool.wtok[3]]
            tp = kb.op("pe", lambda t_, c=c, sb=sb: t_.transpose(
                out=pool.db[3][:, :64], in_=dacs_sb[:, 256 * c + 128 * sb:256 * c + 128 * sb + 128],
                identity=consts.ident_f32[:64, :64]), deps=dep, tag=f"{name}dctp")
            pool.wtok[3] = tp
            te = kb.op("act", lambda s, sb=sb: s.activation(
                out=scl[:, sb, :], in_=pool.db[3][:, :64], func=AF.Exp),
                deps=[tp] + st["scl_rd"], tag=f"{name}ea{sb}")
            st["scl_rd"] = []
            pool.rtok[3] = te
            ea_toks.append(te)
        dep = [t_dpr] + consts.tok + [pool.rtok[3], pool.wtok[3]]
        tpd = kb.op("pe", lambda t_, c=c: t_.transpose(
            out=pool.db[3][0:1, :64], in_=dpr_sb[:, c:c + 1],
            identity=consts.ident_f32[:64, :64]), deps=dep, tag=f"{name}dptp")
        pool.wtok[3] = tpd
        tde = kb.op("act", lambda s: s.copy(out=rows1[:], in_=pool.db[3][0:1, :64]),
                    deps=[tpd] + st["r1_rd"], tag=f"{name}dpe")
        pool.rtok[3] = tde
        st["r1_rd"] = []
        dep = [tde] + consts.tok + [pool.rtok[3], pool.wtok[3]]
        tbc = kb.op("pe", lambda t_: t_.matmul(
            out=pool.db[3][:, 512:576], lhsT=consts.ones_row_f32[:], rhs=rows1[:],
            start=True, stop=True), deps=dep, tag=f"{name}dpb")
        pool.wtok[3] = tbc
        tdp = kb.op("act", lambda s: s.copy(out=scl[:, 2, :], in_=pool.db[3][:, 512:576]),
                    deps=[tbc], tag=f"{name}dpe2")
        pool.rtok[3] = tdp
        st["r1_rd"] = [tbc]
        corr_rd = []
        y_wr = []
        prev_a = []
        for h in range(64):
            for qb in range(2):
                dep = [t_cf, t_sin] + corr_rd
                corr_rd = []
                mm = kb.op("pe", lambda t_, h=h, qb=qb: t_.matmul(
                    out=pool.db[1][:, 0:64], lhsT=cf[:, 128 * qb:128 * qb + 128],
                    rhs=sin_sb[:, h, :], start=True, stop=True), deps=dep, tag=f"{name}cmm")
                a1_ = kb.op("dve", lambda v, h=h, qb=qb: v.tensor_scalar(
                    out=ytmp[:], in0=pool.db[1][:, 0:64], scalar1=scl[:, qb, h:h + 1],
                    scalar2=None, op0=ALU.mult), deps=[mm, ea_toks[qb]] + st["yt_rd"],
                    tag=f"{name}ca1")
                st["yt_rd"] = []
                corr_rd = [a1_]
                a2_ = kb.op("dve", lambda v, h=h, qb=qb: v.scalar_tensor_tensor(
                    out=ybuf[:, qb, 64 * h:64 * h + 64], in0=ytmp[:], scalar=scl[:, 2, h:h + 1],
                    in1=ybuf[:, qb, 64 * h:64 * h + 64], op0=ALU.mult, op1=ALU.add),
                    deps=[a1_, tdp, t_y] + prev_a, tag=f"{name}ca2")
                prev_a = []
                st["yt_rd"] = [a2_]
                y_wr.append(a2_)
        st["scl_rd"] = [a1_, a2_]
        pool.wtok[1] = mm
        pool.rtok[1] = a1_
        t_g = kb.dma("sp", s_g2, lambda sy, c=c: sy.dma_start(
            out=gbuf[:], in_=gate_dram[256 * c:256 * c + 256, :].rearrange(
                "(qb p) f -> p qb f", p=128)), deps=st["g_rd"], tag=f"{name}gd{c}")
        st["g_rd"] = []
        gn_toks = []
        prev_sq = None
        for qb in range(2):
            tsi = kb.op("act", lambda s, qb=qb: s.activation(
                out=gbuf[:, qb, :], in_=gbuf[:, qb, :], func=AF.Silu),
                deps=[t_g] + ([prev_sq] if prev_sq else []), tag=f"{name}si{qb}")
            tg1 = kb.op("dve", lambda v, qb=qb: v.tensor_tensor(
                out=gbuf[:, qb, :], in0=ybuf[:, qb, :], in1=gbuf[:, qb, :], op=ALU.mult),
                deps=[tsi] + y_wr, tag=f"{name}gm{qb}")
            tsq = kb.op("act", lambda s, qb=qb: s.activation(
                out=ybuf[:, qb, :], in_=gbuf[:, qb, :], func=AF.Square,
                accum_out=col[:, 0:1]), deps=[tg1] + st["col_rd"], tag=f"{name}sq{qb}")
            st["y_rd"].append(tsq)
            tms = kb.op("dve", lambda v: v.tensor_scalar(
                out=col[:, 1:2], in0=col[:, 0:1], scalar1=1.0 / 4096.0, scalar2=1e-5,
                op0=ALU.mult, op1=ALU.add), deps=[tsq], tag=f"{name}ms{qb}")
            tln = kb.op("act", lambda s: s.activation(out=col[:, 2:3], in_=col[:, 1:2],
                                                      func=AF.Ln), deps=[tms], tag=f"{name}l{qb}")
            tex = kb.op("act", lambda s: s.activation(out=col[:, 3:4], in_=col[:, 2:3],
                                                      func=AF.Exp, scale=-0.5),
                        deps=[tln], tag=f"{name}e{qb}")
            tgn = kb.op("dve", lambda v, qb=qb: v.tensor_scalar(
                out=gbuf[:, qb, :], in0=gbuf[:, qb, :], scalar1=col[:, 3:4],
                scalar2=None, op0=ALU.mult), deps=[tex, tg1], tag=f"{name}gn{qb}")
            st["col_rd"] = [tgn]
            prev_sq = tgn
            gn_toks.append(tgn)
        st["g_rd"] = []
        prev_ev = None
        for qb in range(2):
            for ic in range(32):
                dep = [gn_toks[qb]] + consts.tok + [pool.rtok[0], pool.wtok[0]]
                tp = kb.op("pe", lambda t_, qb=qb, ic=ic: t_.transpose(
                    out=pool.db[0][:, :64].bitcast(BF16), in_=gbuf[:, qb, 128 * ic:128 * ic + 128],
                    identity=consts.ident_bf16[:]), deps=dep, tag=f"{name}gtp")
                pool.wtok[0] = tp
                st["g_rd"].append(tp)
                te = kb.op("dve", lambda v, c=c, qb=qb, ic=ic: v.tensor_scalar(
                    out=gnT[:, ic, 256 * c + 128 * qb:256 * c + 128 * qb + 128],
                    in0=pool.db[0][:, :64].bitcast(BF16), scalar1=nw_sb[:, ic:ic + 1],
                    scalar2=None, op0=ALU.mult),
                    deps=[tp, t_nw] + st["gnT_rd"] + ([prev_ev] if prev_ev else []),
                    tag=f"{name}gte")
                st["gnT_rd"] = []
                pool.rtok[0] = te
                prev_ev = te
                gnT_toks[(c, ic)] = te

    wst = StripStream(kb, nc, ctx, f"{name}ws", [128, 512], BF16, 2, "pool")
    xbuf = [e(nc.sbuf_tensor(f"{name}_xb{i}", [128, T], F32)) for i in range(2)]
    s_xb = [kb.dma_sem(f"{name}xb{i}") for i in range(2)]
    obuf = e(nc.sbuf_tensor(f"{name}_ob", [128, T], F32))
    s_ob = kb.dma_sem(f"{name}ob")
    out_toks = []
    xb_rd = [[], []]
    ob_rd = None
    gnT_deps = list(gnT_toks.values())
    for og in range(4):
        mm_last = [None] * 4
        for ic in range(32):
            buf, sl, td = wst.load(lambda ic=ic, og=og: woT_dram[128 * ic:128 * ic + 128,
                                                                512 * og:512 * og + 512],
                                   tag=f"{name}ost")
            mms = []
            for b in range(4):
                for i in range(NT):
                    n = min(512, T - 512 * i)
                    dep = [td] + ([gnT_toks[(cc, ic)] for cc in range(NC)] if ic >= 0 else [])
                    if ic == 0 and pool.rtok[b] is not None:
                        dep += [pool.rtok[b], pool.wtok[b]]
                    t = kb.op("pe", lambda t_, ic=ic, b=b, i=i, n=n, buf=buf: t_.matmul(
                        out=pool.db[b][:, 512 * i:512 * i + n],
                        lhsT=buf[:, 128 * b:128 * b + 128],
                        rhs=gnT[:, ic, 512 * i:512 * i + n],
                        start=(ic == 0), stop=(ic == 31)), deps=dep, tag=f"{name}omm")
                    mms.append(t)
                    if ic == 31:
                        mm_last[b] = t
            wst.read_by(sl, [mms[-1]])
        for b in range(4):
            cc2 = 4 * og + b
            pool.wtok[b] = mm_last[b]
            sl = cc2 % 2
            tx = kb.dma("sp", s_xb[sl], lambda sy, cc2=cc2, sl=sl: sy.dma_start(
                out=xbuf[sl][:], in_=x3[cc2, :, :]), deps=xb_rd[sl], tag=f"{name}xd{cc2}")
            xb_rd[sl] = []
            t2 = kb.op("dve", lambda v, b=b, sl=sl: v.tensor_tensor(
                out=obuf[:], in0=pool.db[b][:, :T], in1=xbuf[sl][:], op=ALU.add),
                deps=[mm_last[b], tx] + ([ob_rd] if ob_rd else []), tag=f"{name}fin{cc2}")
            xb_rd[sl].append(t2)
            pool.rtok[b] = t2
            td2 = kb.dma("sp", s_ob, lambda g, cc2=cc2: g.dma_start(
                out=o3[cc2, :, :], in_=obuf[:]), deps=[t2], tag=f"{name}odma{cc2}")
            ob_rd = td2
            out_toks.append(td2)
    return out_toks


class FlagSync:

    def __init__(self, kb, nc, ctx, flg_dram, nonce_dram, nsync):
        e = ctx.enter_context
        self.kb, self.nc = kb, nc
        self._parc = {}
        self.flg = flg_dram
        self.nsync = nsync
        self.non_sb = e(nc.sbuf_tensor("fs_non", [1, nsync * 16], I32))
        self.fsb = [e(nc.sbuf_tensor(f"fs_f{i}", [1, 16], I32)) for i in range(nsync)]
        self.s_n = kb.dma_sem("fsn")
        self.t_n = kb.dma("sp", self.s_n, lambda sy: sy.dma_start(
            out=self.non_sb[:], in_=nonce_dram[:]), detach=True)
        self.s_w = kb.dma_sem("fsw")
        self.psem = [e(nc.semaphore(f"fspoll{i}")) for i in range(nsync)]

    def _par(self, sy):
        if "v" not in self._parc:
            self._parc["v"] = sy.partition_id() % 2
        return self._parc["v"]

    def write(self, s_idx, deps):
        return self.kb.dma("sp", self.s_w, lambda sy, s_idx=s_idx: sy.dma_start(
            out=self.flg[bass.ds(self._par(sy), 1), 16 * s_idx:16 * s_idx + 16],
            in_=self.non_sb[:, 16 * s_idx:16 * s_idx + 16]),
            deps=list(deps) + [self.t_n], tag=f"flgw{s_idx}", detach=True)

    def wait(self, s_idx, deps=()):
        psem = self.psem[s_idx]
        fsb = self.fsb[s_idx]

        def fn(s):
            par = s.partition_id() % 2
            nslot = 1 - par
            cnt = s.alloc_register(f"plc{s_idx}")
            fval = s.alloc_register(f"plf{s_idx}")
            nval = s.alloc_register(f"pln{s_idx}")
            cond = s.alloc_register(f"plq{s_idx}")
            s.reg_load(nval, self.non_sb[0:1, 16 * s_idx:16 * s_idx + 1])
            s.reg_mov(cnt, 0)
            s.reg_mov(fval, 0)
            s.reg_alu(cond, fval, nval, ALU.not_equal)
            with s.While(cond):
                s.dma_start(out=fsb[:], in_=self.flg[
                    bass.ds(nslot, 1), 16 * s_idx:16 * s_idx + 16]).then_inc(psem, 16)
                s.reg_add(cnt, cnt, 16)
                s.wait_ge(psem, cnt)
                s.reg_load(fval, fsb[0:1, 0:1])
                s.reg_alu(cond, fval, nval, ALU.not_equal)
            return s.engine_nop() if hasattr(s, "engine_nop") else s.nop()

        return self.kb.op("act", fn, deps=list(deps) + [self.t_n], tag=f"flgp{s_idx}",
                          detach=True)


def dequant_scratch(nc, ctx, name="dqs"):
    e = ctx.enter_context
    WC = 128
    return {"tm1": e(nc.sbuf_tensor(f"{name}_t1", [128, WC], I32)),
            "tm2": e(nc.sbuf_tensor(f"{name}_t2", [128, WC], I32)),
            "tu": e(nc.sbuf_tensor(f"{name}_tu", [128, WC], F32)),
            "tq": e(nc.sbuf_tensor(f"{name}_tq", [128, WC], F32)),
            "tsg": e(nc.sbuf_tensor(f"{name}_ts", [128, WC], I32)),
            "val": e(nc.sbuf_tensor(f"{name}_val", [128, 8 * WC], F32)),
            "vb": e(nc.sbuf_tensor(f"{name}_vb", [128, 8 * WC], BF16)),
            "vb_rd": [], "val_rd": []}


def emit_dequant_nat(kb, nc, ctx, pk_dram, scT_dram, out_sh, O_half, W,
                     name, sliced=False, detach=False, scratch=None):
    e = ctx.enter_context
    DT = detach
    IN = 8 * W
    G = IN // 128
    n_ot = O_half // 128
    WC = 128
    nwc = cdiv(W, WC)
    out_loc = nc.dram_tensor(f"{name}_loc", [O_half, IN], BF16, kind="Internal").ap()

    pk = [e(nc.sbuf_tensor(f"{name}_pk{i}", [128, W], I32)) for i in range(2)]
    s_pk = [kb.dma_sem(f"{name}pk{i}") for i in range(2)]
    sct = [e(nc.sbuf_tensor(f"{name}_sct{i}", [128, G], F32)) for i in range(2)]
    s_sct = [kb.dma_sem(f"{name}sct{i}") for i in range(2)]
    if scratch is None:
        scratch = dequant_scratch(nc, ctx, name + "s")
    tm1, tm2, tu, tq, tsg = (scratch[k] for k in ("tm1", "tm2", "tu", "tq", "tsg"))
    val, vb = scratch["val"], scratch["vb"]
    s_out = kb.dma_sem(f"{name}out")

    st = {"pk_tok": [[], []], "sct_rd": [[], []]}
    row_toks = [[] for _ in range(n_ot)]

    def emit_ot(ot):
        sl = ot % 2
        td = kb.dma("sp", s_pk[sl], lambda sy, ot=ot, sl=sl: sy.dma_start(
            out=pk[sl][:], in_=pk_dram[128 * ot:128 * ot + 128, :]),
            deps=st["pk_tok"][sl], tag=f"{name}pkd{ot}", detach=DT)
        st["pk_tok"][sl] = []
        tsc = kb.dma("pool", s_sct[sl], lambda g, ot=ot, sl=sl: g.dma_start(
            out=sct[sl][:], in_=scT_dram[128 * ot:128 * ot + 128, :]),
            deps=st["sct_rd"][sl], tag=f"{name}sctd{ot}", detach=DT)
        st["sct_rd"][sl] = []
        val_rd = scratch["val_rd"]
        vb_rd = scratch["vb_rd"]
        for wc in range(nwc):
            w0 = WC * wc
            wn = min(WC, W - w0)
            plane_toks = []
            pv = vb_rd + val_rd
            val_rd = []
            for j in range(8):
                t1 = kb.op("dve", lambda v, j=j, sl=sl, w0=w0, wn=wn: v.tensor_scalar(
                    out=tm1[:, :wn], in0=pk[sl][:, w0:w0 + wn], scalar1=4 * j, scalar2=7,
                    op0=ALU.logical_shift_right, op1=ALU.bitwise_and),
                    deps=[td] + pv, tag=f"{name}p{j}a", detach=DT)
                pv = []
                t2 = kb.op("dve", lambda v, wn=wn: v.tensor_scalar(
                    out=tm2[:, :wn], in0=tm1[:, :wn], scalar1=22, scalar2=0x3E000000,
                    op0=ALU.logical_shift_left, op1=ALU.bitwise_or), deps=[t1],
                    tag=f"{name}p{j}b", detach=DT)
                t3 = kb.op("dve", lambda v, wn=wn: v.tensor_scalar(
                    out=tu[:, :wn], in0=tm2[:, :wn].bitcast(F32), scalar1=8.0, scalar2=-1.0,
                    op0=ALU.mult, op1=ALU.add), deps=[t2], tag=f"{name}p{j}c", detach=DT)
                t4 = kb.op("act", lambda s, wn=wn: s.activation(
                    out=tq[:, :wn], in_=tm2[:, :wn].bitcast(F32), func=AF.Copy,
                    scale=4.0), deps=[t2], tag=f"{name}p{j}d", detach=DT)
                t5 = kb.op("dve", lambda v, wn=wn: v.tensor_tensor(
                    out=tq[:, :wn], in0=tu[:, :wn], in1=tq[:, :wn], op=ALU.min),
                    deps=[t3, t4], tag=f"{name}p{j}e", detach=DT)
                t6 = kb.op("dve", lambda v, j=j, sl=sl, w0=w0, wn=wn: v.tensor_scalar(
                    out=tsg[:, :wn], in0=pk[sl][:, w0:w0 + wn], scalar1=28 - 4 * j,
                    scalar2=-0x80000000, op0=ALU.logical_shift_left, op1=ALU.bitwise_and),
                    deps=[td], tag=f"{name}p{j}f", detach=DT)
                t7 = kb.op("dve", lambda v, j=j, wn=wn: v.tensor_tensor(
                    out=val[:].bitcast(I32)[:, j:8 * wn:8], in0=tq[:, :wn].bitcast(I32),
                    in1=tsg[:, :wn], op=ALU.bitwise_or), deps=[t5, t6],
                    tag=f"{name}p{j}g", detach=DT)
                plane_toks.append(t7)
            st["pk_tok"][sl] = [plane_toks[-1]]
            sc_toks = []
            g0 = (8 * w0) // 128
            for g in range(8 * wn // 128):
                t = kb.op("act", lambda s, g=g, g0=g0, sl=sl: s.activation(
                    out=vb[:, 128 * g:128 * g + 128], in_=val[:, 128 * g:128 * g + 128],
                    func=AF.Copy, scale=sct[sl][:, g0 + g:g0 + g + 1]),
                    deps=plane_toks + [tsc] + vb_rd, tag=f"{name}sc{g}", detach=DT)
                vb_rd = []
                sc_toks.append(t)
                val_rd.append(t)
                st["sct_rd"][sl].append(t)
            tdo = kb.dma("sp", s_out, lambda g, ot=ot, w0=w0, wn=wn: g.dma_start(
                out=out_loc[128 * ot:128 * ot + 128, 8 * w0:8 * w0 + 8 * wn],
                in_=vb[:, :8 * wn]), deps=sc_toks, tag=f"{name}odma{ot}_{wc}", detach=DT)
            vb_rd = [tdo]
            row_toks[ot].append(tdo)
        scratch["vb_rd"] = vb_rd
        scratch["val_rd"] = val_rd

    def emit_bulks():
        nsplit = 4 if O_half >= 4096 else 2
        Oq = O_half // nsplit
        opb = n_ot // nsplit
        bulk_toks = []
        for k in range(nsplit):
            s_bk = kb.dma_sem(f"{name}bk{k}")
            dep = [t for ot in range(k * opb, (k + 1) * opb) for t in row_toks[ot]]
            t = kb.dma("pool", s_bk, lambda gg, k=k: gg.dma_start(
                out=out_sh[bass.ds(gg.partition_id() % 2 * O_half + k * Oq, Oq), :],
                in_=out_loc[k * Oq:(k + 1) * Oq, :]), deps=dep,
                tag=f"{name}bulk{k}", detach=DT)
            bulk_toks.append(t)
        return bulk_toks

    if sliced:
        return [lambda ot=ot: emit_ot(ot) for ot in range(n_ot)], emit_bulks
    for ot in range(n_ot):
        emit_ot(ot)
    return emit_bulks()


from contextlib import ExitStack
from concourse.bass_utils import run_bass_kernel_spmd

_T = 1024
_NCORES = 8


def _build_fused():
    nc = bass.Bass("TRN2", target_bir_lowering=False)
    kb = KB(nc)
    F = F32
    d = {}
    d["x"] = nc.dram_tensor("x", [2048, _T + 3], F, kind="ExternalInput").ap()
    d["winT"] = nc.dram_tensor("winT", [2048, 8512], F, kind="ExternalInput").ap()
    d["cw"] = nc.dram_tensor("cw", [128, 34, 4], F, kind="ExternalInput").ap()
    d["cb"] = nc.dram_tensor("cb", [128, 34], F, kind="ExternalInput").ap()
    d["dtb"] = nc.dram_tensor("dtb", [64, 1], F, kind="ExternalInput").ap()
    d["a"] = nc.dram_tensor("a", [64, 1], F, kind="ExternalInput").ap()
    d["dd"] = nc.dram_tensor("dd", [1, 64], F, kind="ExternalInput").ap()
    d["idf"] = nc.dram_tensor("idf", [128, 128], F, kind="ExternalInput").ap()
    d["idneg"] = nc.dram_tensor("idneg", [128, 128], BF16, kind="ExternalInput").ap()
    d["ltc"] = nc.dram_tensor("ltc", [128, 128], BF16, kind="ExternalInput").ap()
    d["woT"] = nc.dram_tensor("woT", [4096, 2048], F, kind="ExternalInput").ap()
    d["nw"] = nc.dram_tensor("nw", [128, 32], F, kind="ExternalInput").ap()
    d["sel"] = nc.dram_tensor("sel", [128, 1], F, kind="ExternalInput").ap()
    d["nonce"] = nc.dram_tensor("nonce", [1, 48], I32, kind="ExternalInput").ap()
    w = {
        "router_w": nc.dram_tensor("router_w", [8, 2048], F, kind="ExternalInput").ap(),
        "w_gate": nc.dram_tensor("w_gate", [8, 2048, 1408], F, kind="ExternalInput").ap(),
        "w_up": nc.dram_tensor("w_up", [8, 2048, 1408], F, kind="ExternalInput").ap(),
        "w_down": nc.dram_tensor("w_down", [8, 1408, 2048], F, kind="ExternalInput").ap(),
        "sw_gate": nc.dram_tensor("sw_gate", [2048, 5632], F, kind="ExternalInput").ap(),
        "sw_up": nc.dram_tensor("sw_up", [2048, 5632], F, kind="ExternalInput").ap(),
        "sw_down": nc.dram_tensor("sw_down", [5632, 2048], F, kind="ExternalInput").ap(),
        "shared_gate_w": nc.dram_tensor("shared_gate_w", [16, 128], F, kind="ExternalInput").ap(),
    }
    d["wkvd"] = nc.dram_tensor("wkvd", [2048, 256], F, kind="ExternalInput").ap()
    d["gam"] = nc.dram_tensor("gam", [128, 2], F, kind="ExternalInput").ap()
    d["wq"] = nc.dram_tensor("wq", [2048, 2048], F, kind="ExternalInput").ap()
    d["wkvu"] = nc.dram_tensor("wkvu", [256, 4096], F, kind="ExternalInput").ap()
    d["wo"] = nc.dram_tensor("wo", [2048, 2048], F, kind="ExternalInput").ap()
    d["m01"] = nc.dram_tensor("m01", [128, 16, _T], BF16, kind="ExternalInput").ap()
    d["gup"] = nc.dram_tensor("gup", [5632, 256], I32, kind="ExternalInput").ap()
    d["gsc"] = nc.dram_tensor("gsc", [5632, 16], F16, kind="ExternalInput").ap()
    d["dnp"] = nc.dram_tensor("dnp", [1024, 704], I32, kind="ExternalInput").ap()
    d["dsc"] = nc.dram_tensor("dsc", [1024, 44], F16, kind="ExternalInput").ap()
    outT = nc.dram_tensor("outT", [2048, _T], F, kind="ExternalOutput").ap()
    yd = nc.dram_tensor("i_y", [_T, 4096], BF16, kind="Internal").ap()
    gd = nc.dram_tensor("i_gate", [_T, 4096], BF16, kind="Internal").ap()
    xbcd = nc.dram_tensor("i_xbc", [4352, _T], BF16, kind="Internal").ap()
    dacsd = nc.dram_tensor("i_dacs", [64, _T], F, kind="Internal").ap()
    dprodd = nc.dram_tensor("i_dprod", [64, 4], F, kind="Internal").ap()
    x1d = nc.dram_tensor("i_x1", [2048, _T], F, kind="Internal").ap()
    x2d = nc.dram_tensor("i_x2", [2048, _T], F, kind="Internal").ap()
    x3d = nc.dram_tensor("i_x3", [2048, _T], F, kind="Internal").ap()
    xlnd = nc.dram_tensor("i_xln", [2048, _T], BF16, kind="Internal").ap()
    s_sh = nc.dram_tensor("s_sh", [256, 4096], F, kind="Internal", addr_space="Shared").ap()
    lat_sh = nc.dram_tensor("lat_sh", [512, _T], BF16, kind="Internal", addr_space="Shared").ap()
    wgu_nat = nc.dram_tensor("wgu_nat", [11264, 2048], BF16, kind="Internal", addr_space="Shared").ap()
    wdn_nat = nc.dram_tensor("wdn_nat", [2048, 5632], BF16, kind="Internal", addr_space="Shared").ap()
    flgd = nc.dram_tensor("flgd", [2, 48], I32, kind="Internal", addr_space="Shared").ap()

    with ExitStack() as octx:
        pool = PsumPool(nc, octx)
        consts = Consts(kb, nc, octx, ident_dram=d["idf"])
        flags = FlagSync(kb, nc, octx, flgd, d["nonce"], 3)
        with ExitStack() as p1:
            a1 = emit_l0attn_a1(kb, nc, p1, consts, pool, d["x"], d["winT"], d["cw"],
                                d["cb"], d["dtb"], d["a"], d["dd"], yd, gd, xbcd,
                                dacsd, dprodd, None, T=_T, idneg_dram=d["idneg"],
                                ltc_dram=d["ltc"], sout_shared=s_sh)
            f0 = flags.write(0, [a1[-1]])
        kb.barrier()
        with ExitStack() as p2:
            w0 = flags.wait(0)
            a2 = emit_l0attn_a2(kb, nc, p2, consts, pool, d["x"][:, 3:_T + 3], yd, gd,
                                xbcd, dacsd, dprodd, None, d["woT"], d["nw"], x1d,
                                T=_T, sin_shared=s_sh, sel_dram=d["sel"], sin_deps=[w0])
        kb.barrier()
        with ExitStack() as p3:
            dqs = dequant_scratch(nc, p3)
            sl1, fin1 = emit_dequant_nat(kb, nc, p3, d["gup"], d["gsc"],
                                         wgu_nat, 5632, 256, "dg",
                                         sliced=True, detach=True, scratch=dqs)
            sl2, fin2 = emit_dequant_nat(kb, nc, p3, d["dnp"], d["dsc"],
                                         wdn_nat, 1024, 704, "dd",
                                         sliced=True, detach=True, scratch=dqs)
            sit = iter(sl1 + sl2)
            pcnt = [0]

            def pump():
                pcnt[0] += 1
                if pcnt[0] % 9 < 4:
                    s = next(sit, None)
                    if s is not None:
                        s()
            mf = emit_l0ffn(kb, nc, p3, consts, pool, x1d, w, x2d, T=_T, pump=pump)
            for s in sit:
                s()
            dqb = fin1() + fin2()
            f1 = flags.write(1, dqb)
            kb.op("sp", lambda sy: sy.nop(), deps=dqb, tag="dqjoin")
        kb.barrier()
        with ExitStack() as p4:
            la = emit_l1attn_lat(kb, nc, p4, consts, pool, x2d, d["wkvd"], d["gam"],
                                 xlnd, None, T=_T, lat_shared=lat_sh)
            f2 = flags.write(2, la)
        kb.barrier()
        with ExitStack() as p5:
            w2 = flags.wait(2)
            at = emit_l1attn_core(kb, nc, p5, consts, pool, xlnd, None, x2d,
                                  d["wq"], d["wkvu"], d["wo"], d["m01"], x3d,
                                  T=_T, L=2048, latseq_shared=lat_sh, lat_deps=[w2])
        kb.barrier()
        with ExitStack() as p6:
            w1 = flags.wait(1)
            d2 = emit_l1ffn(kb, nc, p6, consts, pool, x3d, wgu_nat, wdn_nat, outT,
                            T=_T, w_natural=True, w_deps=[w1])
            kb.finish(d2)
        kb.emit(octx)
    return nc


def _make_m01(q0):
    kb_i = np.arange(16)[None, :, None]
    k_i = np.arange(128)[:, None, None]
    q_i = np.arange(_T)[None, None, :]
    m = ((kb_i * 128 + k_i) <= (q0 + q_i)).astype(np.float32)
    return m.astype(ml_dtypes.bfloat16)


def kernel(**inp):
    cores = list(range(_NCORES))
    x = np.asarray(inp["L0_attn_hidden_states"])
    xt = np.ascontiguousarray(x.transpose(0, 2, 1))
    ident = np.eye(128, dtype=np.float32)
    idneg = (-1e30 * np.eye(128, dtype=np.float32)).astype(ml_dtypes.bfloat16)
    qq = np.arange(128)
    ltc = (qq[None, :] < qq[:, None]).astype(np.float32).astype(ml_dtypes.bfloat16)
    nonce = np.random.randint(1, 2 ** 31 - 1, size=(1, 48)).astype(np.int32)

    com = {
        "winT": np.ascontiguousarray(np.asarray(inp["L0_attn_in_proj_w"]).T),
        "cw": np.ascontiguousarray(np.asarray(inp["L0_attn_conv_w"]).reshape(34, 128, 4).transpose(1, 0, 2)),
        "cb": np.ascontiguousarray(np.asarray(inp["L0_attn_conv_b"]).reshape(34, 128).T),
        "dtb": np.asarray(inp["L0_attn_dt_bias"]).reshape(64, 1),
        "a": np.asarray(inp["L0_attn_A"]).reshape(64, 1),
        "dd": np.asarray(inp["L0_attn_D"]).reshape(1, 64),
        "idf": ident, "idneg": idneg, "ltc": ltc, "nonce": nonce,
        "woT": np.ascontiguousarray(np.asarray(inp["L0_attn_out_proj_w"]).T),
        "nw": np.ascontiguousarray(np.asarray(inp["L0_attn_norm_w"]).reshape(32, 128).T),
        "wkvd": np.asarray(inp["L1_attn_Wkv_down"]),
        "gam": np.ascontiguousarray(np.asarray(inp["L1_attn_kv_norm_w"]).reshape(2, 128).T),
        "wq": np.asarray(inp["L1_attn_Wq"]),
        "wkvu": np.asarray(inp["L1_attn_Wkv_up"]),
        "wo": np.asarray(inp["L1_attn_Wo"]),
    }
    for kk in ["router_w", "w_gate", "w_up", "w_down", "sw_gate", "sw_up", "sw_down"]:
        com[kk] = np.asarray(inp["L0_ffn_" + kk])
    com["shared_gate_w"] = np.asarray(inp["L0_ffn_shared_gate_w"]).reshape(16, 128)
    gup = np.asarray(inp["L1_ffn_gate_up_packed"])
    gsc = np.asarray(inp["L1_ffn_gate_up_scales"])
    dnp = np.asarray(inp["L1_ffn_down_packed"])
    dsc = np.asarray(inp["L1_ffn_down_scales"])

    nc = _build_fused()
    in_maps = []
    for k in cores:
        b, half = k // 2, k % 2
        halo = (np.zeros((2048, 3), np.float32) if half == 0 else xt[b][:, 1021:1024])
        m = dict(com)
        m["x"] = np.ascontiguousarray(
            np.concatenate([halo, xt[b][:, 1024 * half:1024 * half + 1024]], axis=1))
        m["sel"] = np.full((128, 1), float(half), np.float32)
        m["m01"] = _make_m01(half * _T)
        m["gup"] = np.ascontiguousarray(gup[half * 5632:(half + 1) * 5632])
        m["gsc"] = np.ascontiguousarray(gsc[:, half * 5632:(half + 1) * 5632].T)
        m["dnp"] = np.ascontiguousarray(dnp[half * 1024:(half + 1) * 1024])
        m["dsc"] = np.ascontiguousarray(dsc[:, half * 1024:(half + 1) * 1024].T)
        in_maps.append(m)
    res = run_bass_kernel_spmd(nc, in_maps, core_ids=cores)
    out = np.concatenate([res.results[k]["outT"].T for k in cores], axis=0)
    return np.ascontiguousarray(out.reshape(4, 2048, 2048)).astype(np.float32)


# version 08 (final)
# speedup vs baseline: 1.1376x; 1.1376x over previous
import numpy as np
import ml_dtypes
import concourse.bass as bass
import concourse.mybir as mybir


F32 = mybir.dt.float32
BF16 = mybir.dt.bfloat16
F16 = mybir.dt.float16
I32 = mybir.dt.int32
I8 = mybir.dt.int8
AF = mybir.ActivationFunctionType
ALU = mybir.AluOpType
AX = mybir.AxisListType

ENGINES = ("pe", "act", "dve", "pool", "sp")
ENG_ATTR = {"pe": "tensor", "act": "scalar", "dve": "vector", "pool": "gpsimd", "sp": "sync"}


class Op:
    __slots__ = ("eng", "fn", "deps", "idx", "val", "needed", "tag", "sem", "dval")

    def __init__(self, eng, fn, deps, tag):
        self.eng = eng
        self.fn = fn
        self.deps = deps
        self.idx = -1
        self.val = None
        self.needed = False
        self.tag = tag
        self.sem = None
        self.dval = None


class DmaSem:
    __slots__ = ("name", "count", "handle", "queue", "last_op", "detached")

    def __init__(self, name):
        self.name = name
        self.count = 0
        self.handle = None
        self.queue = None
        self.last_op = None
        self.detached = False


class KB:
    def __init__(self, nc: bass.Bass):
        self.nc = nc
        self.ops = {e: [] for e in ENGINES}
        self.dma_sems = []
        self.ctx = None
        self.eng_sem = {}
        self.final_deps = []
        self.phase_deps = []
        self.last_reg = {e: None for e in ENGINES}

    def barrier(self):
        toks = []
        for e in ENGINES:
            if self.last_reg[e] is not None:
                toks.append(self.last_reg[e])
        for s in self.dma_sems:
            if s.last_op is not None and not s.detached:
                toks.append(s.last_op)
        self.phase_deps = toks

    def op(self, eng, fn, deps=(), tag="", detach=False):
        pd = [] if detach else self.phase_deps
        o = Op(eng, fn, [d for d in deps if d is not None] + pd, tag)
        for d in o.deps:
            d.needed = True
        self.ops[eng].append(o)
        if not detach:
            self.last_reg[eng] = o
        return o

    def dma_sem(self, name):
        s = DmaSem(name)
        self.dma_sems.append(s)
        return s

    def dma(self, q, sem, fn, deps=(), tag="", detach=False):
        pd = [] if detach else self.phase_deps
        o = Op(q, fn, [d for d in deps if d is not None] + pd, tag)
        for d in o.deps:
            d.needed = True
        if sem.queue is None:
            sem.queue = q
        assert sem.queue == q, f"DmaSem {sem.name} used from {sem.queue} and {q}"
        if detach:
            sem.detached = True
        o.sem = sem
        sem.count += 1
        sem.last_op = o
        o.dval = 16 * sem.count
        self.ops[q].append(o)
        if not detach:
            self.last_reg[q] = o
        return o

    def finish(self, deps):
        self.final_deps.extend([d for d in deps if d is not None])

    def emit(self, ctx):
        nc = self.nc
        e = ctx.enter_context
        for eng in ENGINES:
            self.eng_sem[eng] = e(nc.semaphore(f"sem_{eng}"))
        for s in self.dma_sems:
            s.handle = e(nc.semaphore(f"dsem_{s.name}"))
        for eng in ENGINES:
            v = 0
            for o in self.ops[eng]:
                if o.sem is None and o.needed:
                    v += 1
                    o.val = v
        block = e(nc.Block())

        def tok_sem_val(d):
            if d.sem is not None:
                return d.sem.handle, d.dval
            assert d.val is not None, f"dep on un-inc'd op {d.tag}"
            return self.eng_sem[d.eng], d.val

        for eng in ENGINES:
            getattr(block, ENG_ATTR[eng])(self._mkbody(eng, tok_sem_val))

    def _mkbody(self, eng_name, tok_sem_val):
        def body(engobj):
            water = {}
            for o in self.ops[eng_name]:
                for d in o.deps:
                    sem, val = tok_sem_val(d)
                    if water.get(id(sem), -1) < val:
                        engobj.wait_ge(sem, val)
                        water[id(sem)] = val
                if o.sem is not None and o.dval > 16:
                    if water.get(id(o.sem.handle), -1) < o.dval - 16:
                        engobj.wait_ge(o.sem.handle, o.dval - 16)
                        water[id(o.sem.handle)] = o.dval - 16
                try:
                    instr = o.fn(engobj)
                except Exception as _ex:
                    raise RuntimeError(f"op failed tag={o.tag} eng={eng_name}") from _ex
                if o.sem is not None:
                    instr.then_inc(o.sem.handle, 16)
                elif o.needed:
                    instr.then_inc(self.eng_sem[eng_name], 1)
            if eng_name == "sp":
                for d in self.final_deps:
                    sem, val = tok_sem_val(d)
                    if water.get(id(sem), -1) < val:
                        engobj.wait_ge(sem, val)
                        water[id(sem)] = val
        return body




def cdiv(a, b):
    return (a + b - 1) // b


class PsumPool:

    def __init__(self, nc, ctx):
        self.db = [ctx.enter_context(nc.psum_tensor(f"pdb{i}", [128, 1024], F32))
                   for i in range(4)]
        self.rtok = [None] * 4
        self.wtok = [None] * 4


class Consts:
    def __init__(self, kb, nc, ctx, ident_dram=None):
        e = ctx.enter_context
        self.ones_row_f32 = e(nc.sbuf_tensor("c_ones_row_f32", [1, 128], F32))
        self.ones_col_bf16 = e(nc.sbuf_tensor("c_ones_col_bf16", [128, 1], BF16))
        t1 = kb.op("pool", lambda g: g.memset(self.ones_row_f32[:], 1.0))
        t2 = kb.op("pool", lambda g: g.memset(self.ones_col_bf16[:], 1.0))
        self.tok = [t1, t2]
        self.ident_f32 = None
        self.ident_bf16 = None
        if ident_dram is not None:
            self.ident_f32 = e(nc.sbuf_tensor("c_ident_f32", [128, 128], F32))
            self.ident_bf16 = e(nc.sbuf_tensor("c_ident_bf16", [128, 128], BF16))
            s = kb.dma_sem("cid")
            s2 = kb.dma_sem("cid2")
            d1 = kb.dma("sp", s, lambda sy: sy.dma_start(out=self.ident_f32[:], in_=ident_dram[:]))
            d2 = kb.dma("pool", s2, lambda g: g.dma_start(out=self.ident_bf16[:], in_=ident_dram[:]))
            self.tok += [d1, d2]


def bcast_row(kb, consts, pool, db, row_ap, T, deps, tag="bc"):
    toks = []
    nt = cdiv(T, 512)
    for i in range(nt):
        n = min(512, T - 512 * i)
        dep = list(deps) + consts.tok
        if pool.rtok[db] is not None:
            dep.append(pool.rtok[db])
        t = kb.op("pe", lambda t_, i=i, n=n: t_.matmul(
            out=pool.db[db][:, 512 * i:512 * i + n], lhsT=consts.ones_row_f32[:],
            rhs=row_ap[:, 512 * i:512 * i + n], start=True, stop=True),
            deps=dep, tag=tag)
        toks.append(t)
    pool.wtok[db] = toks[-1]
    return toks


def emit_ln_fm(kb, nc, ctx, consts, pool, x_at, C, T, xln, eps, deps, name,
               rms_only=False, gamma=None, db_stats=(0, 1), db_ab=(2, 3),
               x_done=None):
    e = ctx.enter_context
    D = C * 128
    xb = e(nc.sbuf_tensor(f"{name}_xb", [128, T], BF16))
    sq = e(nc.sbuf_tensor(f"{name}_sq", [128, T], BF16))
    rows = e(nc.sbuf_tensor(f"{name}_rows", [1, 2 * T], F32))
    tmp = e(nc.sbuf_tensor(f"{name}_tmp", [128, T], F32))
    s_mean = rows[:, 0:T]
    s_var = rows[:, T:2 * T]
    s_rstd = rows[:, T:2 * T]
    s_b = tmp[0:1, 0:T]
    nt = cdiv(T, 512)
    db_sum, db_ssq = db_stats
    db_A, db_B = db_ab

    mm_last = []
    prev_xb = None
    prev_sq = None
    rd_xb = []
    rd_sq = []
    for c in range(C):
        xc, xdep = x_at(c)
        tb = None
        if not rms_only:
            tb = kb.op("dve", lambda v, xc=xc: v.tensor_copy(out=xb[:], in_=xc),
                       deps=list(deps) + xdep + ([prev_xb] if prev_xb else []) + rd_xb,
                       tag=f"{name}xb{c}")
        ts = kb.op("act", lambda s, xc=xc: s.activation(out=sq[:], in_=xc, func=AF.Square),
                   deps=list(deps) + xdep + ([prev_sq] if prev_sq else []) + rd_sq,
                   tag=f"{name}sq{c}")
        rd_xb = []
        rd_sq = []
        for i in range(nt):
            n = min(512, T - 512 * i)
            if not rms_only:
                dep = [tb] + consts.tok
                if c == 0 and pool.rtok[db_sum] is not None:
                    dep.append(pool.rtok[db_sum])
                t1 = kb.op("pe", lambda t_, i=i, c=c, n=n: t_.matmul(
                    out=pool.db[db_sum][0:1, 512 * i:512 * i + n], lhsT=consts.ones_col_bf16[:],
                    rhs=xb[:, 512 * i:512 * i + n], start=(c == 0), stop=(c == C - 1)),
                    deps=dep, tag=f"{name}mmS")
                rd_xb.append(t1)
                if c == C - 1:
                    mm_last.append(t1)
                    pool.wtok[db_sum] = t1
            dep = [ts] + consts.tok
            if c == 0 and pool.rtok[db_ssq] is not None:
                dep.append(pool.rtok[db_ssq])
            t2 = kb.op("pe", lambda t_, i=i, c=c, n=n: t_.matmul(
                out=pool.db[db_ssq][0:1, 512 * i:512 * i + n], lhsT=consts.ones_col_bf16[:],
                rhs=sq[:, 512 * i:512 * i + n], start=(c == 0), stop=(c == C - 1)),
                deps=dep, tag=f"{name}mmQ")
            rd_sq.append(t2)
            if c == C - 1:
                mm_last.append(t2)
                pool.wtok[db_ssq] = t2
        prev_xb = tb
        prev_sq = ts
        if x_done is not None:
            x_done(c, [t for t in (tb, ts) if t is not None])

    if not rms_only:
        tm = kb.op("dve", lambda v: v.tensor_single_scalar(
            out=s_mean[:], in_=pool.db[db_sum][0:1, :T], scalar=1.0 / D, op=ALU.mult),
            deps=mm_last, tag=f"{name}mean")
        pool.rtok[db_sum] = tm
        tv1 = kb.op("dve", lambda v: v.tensor_single_scalar(
            out=s_var[:], in_=pool.db[db_ssq][0:1, :T], scalar=1.0 / D, op=ALU.mult),
            deps=mm_last, tag=f"{name}v1")
        pool.rtok[db_ssq] = tv1
        tv2 = kb.op("dve", lambda v: v.scalar_tensor_tensor(
            out=s_b[:], in0=s_mean[:], scalar=1.0, in1=s_mean[:], op0=ALU.mult, op1=ALU.mult),
            deps=[tm], tag=f"{name}m2")
        tv = kb.op("dve", lambda v: v.scalar_tensor_tensor(
            out=s_var[:], in0=s_var[:], scalar=float(eps), in1=s_b[:],
            op0=ALU.add, op1=ALU.subtract), deps=[tv1, tv2], tag=f"{name}v3")
    else:
        tv = kb.op("dve", lambda v: v.tensor_scalar(
            out=s_var[:], in0=pool.db[db_ssq][0:1, :T], scalar1=1.0 / D, scalar2=float(eps),
            op0=ALU.mult, op1=ALU.add), deps=mm_last, tag=f"{name}v1")
        pool.rtok[db_ssq] = tv
    tl = kb.op("act", lambda s: s.activation(out=s_rstd[:], in_=s_var[:], func=AF.Ln),
               deps=[tv], tag=f"{name}ln")
    tr = kb.op("act", lambda s: s.activation(out=s_rstd[:], in_=s_rstd[:], func=AF.Exp, scale=-0.5),
               deps=[tl], tag=f"{name}exp")
    toks_A = bcast_row(kb, consts, pool, db_A, s_rstd, T, [tr], tag=f"{name}bA")
    toks_B = []
    if not rms_only:
        tb2 = kb.op("dve", lambda v: v.scalar_tensor_tensor(
            out=s_b[:], in0=s_mean[:], scalar=-1.0, in1=s_rstd[:], op0=ALU.mult, op1=ALU.mult),
            deps=[tr], tag=f"{name}negm")
        toks_B = bcast_row(kb, consts, pool, db_B, s_b, T, [tb2], tag=f"{name}bB")

    out_toks = []
    prev = None
    for c in range(C):
        xc, xdep = x_at(c)
        t1 = kb.op("dve", lambda v, xc=xc: v.tensor_tensor(
            out=tmp[:], in0=xc, in1=pool.db[db_A][:, :T], op=ALU.mult),
            deps=list(deps) + xdep + toks_A + toks_B + ([prev] if prev else []),
            tag=f"{name}o1_{c}")
        if x_done is not None:
            x_done(c, [t1])
        if not rms_only:
            t2 = kb.op("dve", lambda v, c=c: v.tensor_tensor(
                out=xln[:, c, :], in0=tmp[:], in1=pool.db[db_B][:, :T], op=ALU.add),
                deps=[t1] + toks_B, tag=f"{name}o2_{c}")
        else:
            t2 = kb.op("dve", lambda v, c=c: v.tensor_copy(out=xln[:, c, :], in_=tmp[:]),
                       deps=[t1], tag=f"{name}o2_{c}")
        if gamma is not None:
            t2 = kb.op("dve", lambda v, c=c: v.tensor_scalar(
                out=xln[:, c, :], in0=xln[:, c, :], scalar1=gamma[:, c:c + 1],
                scalar2=None, op0=ALU.mult), deps=[t2], tag=f"{name}og{c}")
        prev = t2
        out_toks.append(t2)
    pool.rtok[db_A] = out_toks[-1]
    if not rms_only:
        pool.rtok[db_B] = out_toks[-1]
    return out_toks


def emit_dequant(kb, nc, ctx, consts, pool, pk_dram, sc_dram, out_dram, O, W, name,
                 db_tp=0):
    e = ctx.enter_context
    IN = 8 * W
    G = IN // 128
    n_ot = O // 128
    assert O % 128 == 0 and IN % 128 == 0

    sc_sb = e(nc.sbuf_tensor(f"{name}_sc", [G, O], F32))
    s_sc = kb.dma_sem(f"{name}sc")
    t_sc = kb.dma("pool", s_sc, lambda g: g.dma_start(out=sc_sb[:], in_=sc_dram[:]),
                  tag=f"{name}scdma")
    pk = [e(nc.sbuf_tensor(f"{name}_pk{i}", [128, W], I32)) for i in range(2)]
    s_pk = [kb.dma_sem(f"{name}pk{i}") for i in range(2)]
    sct = e(nc.sbuf_tensor(f"{name}_sct", [128, G], F32))
    tm1 = e(nc.sbuf_tensor(f"{name}_t1", [128, W], I32))
    tm2 = e(nc.sbuf_tensor(f"{name}_t2", [128, W], I32))
    tu = e(nc.sbuf_tensor(f"{name}_tu", [128, W], F32))
    tq = e(nc.sbuf_tensor(f"{name}_tq", [128, W], F32))
    tsg = e(nc.sbuf_tensor(f"{name}_ts", [128, W], I32))
    val = e(nc.sbuf_tensor(f"{name}_val", [128, IN], F32))
    vb = e(nc.sbuf_tensor(f"{name}_vb", [128, IN], BF16))
    trs = e(nc.sbuf_tensor(f"{name}_trs", [128, IN], BF16))
    trs3 = trs[:].rearrange("p (c o) -> p c o", o=128)
    s_out = kb.dma_sem(f"{name}out")

    pk_tok = [None, None]
    prev_val = None
    prev_sct = None
    prev_trs = None
    out_dma_toks = []
    for ot in range(n_ot):
        sl = ot % 2
        dep = [pk_tok[sl]] if pk_tok[sl] else []
        td = kb.dma("sp", s_pk[sl], lambda sy, ot=ot, sl=sl: sy.dma_start(
            out=pk[sl][:], in_=pk_dram[128 * ot:128 * ot + 128, :]), deps=dep,
            tag=f"{name}pkd{ot}")
        dep = [t_sc] + consts.tok
        if pool.rtok[db_tp] is not None:
            dep.append(pool.rtok[db_tp])
        if prev_sct is not None:
            dep.append(prev_sct)
        ttp = kb.op("pe", lambda t_, ot=ot: t_.transpose(
            out=pool.db[db_tp][:, :G], in_=sc_sb[:, 128 * ot:128 * ot + 128],
            identity=consts.ident_f32[:G, :G]), deps=dep, tag=f"{name}sctp{ot}")
        pool.wtok[db_tp] = ttp
        tev = kb.op("act", lambda s, ot=ot: s.copy(out=sct[:], in_=pool.db[db_tp][:, :G]),
                    deps=[ttp] + ([prev_val] if prev_val else []), tag=f"{name}scev{ot}")
        pool.rtok[db_tp] = tev
        prev_sct = tev

        plane_toks = []
        pv = prev_val
        for j in range(8):
            t1 = kb.op("dve", lambda v, j=j, sl=sl: v.tensor_scalar(
                out=tm1[:], in0=pk[sl][:], scalar1=4 * j, scalar2=7,
                op0=ALU.logical_shift_right, op1=ALU.bitwise_and),
                deps=[td] + ([pv] if pv else []), tag=f"{name}p{j}a")
            t2 = kb.op("dve", lambda v: v.tensor_scalar(
                out=tm2[:], in0=tm1[:], scalar1=22, scalar2=0x3E000000,
                op0=ALU.logical_shift_left, op1=ALU.bitwise_or), deps=[t1], tag=f"{name}p{j}b")
            t3 = kb.op("dve", lambda v: v.tensor_scalar(
                out=tu[:], in0=tm2[:].bitcast(F32), scalar1=8.0, scalar2=-1.0,
                op0=ALU.mult, op1=ALU.add), deps=[t2], tag=f"{name}p{j}c")
            t4 = kb.op("dve", lambda v: v.tensor_single_scalar(
                out=tq[:], in_=tm2[:].bitcast(F32), scalar=4.0, op=ALU.mult),
                deps=[t2], tag=f"{name}p{j}d")
            t5 = kb.op("dve", lambda v: v.tensor_tensor(
                out=tq[:], in0=tu[:], in1=tq[:], op=ALU.min), deps=[t3, t4], tag=f"{name}p{j}e")
            t6 = kb.op("dve", lambda v, j=j, sl=sl: v.tensor_scalar(
                out=tsg[:], in0=pk[sl][:], scalar1=28 - 4 * j, scalar2=-0x80000000,
                op0=ALU.logical_shift_left, op1=ALU.bitwise_and),
                deps=[td] + ([pv] if pv else []), tag=f"{name}p{j}f")
            t7 = kb.op("dve", lambda v, j=j: v.tensor_tensor(
                out=val[:].bitcast(I32)[:, j::8], in0=tq[:].bitcast(I32), in1=tsg[:],
                op=ALU.bitwise_or), deps=[t5, t6], tag=f"{name}p{j}g")
            plane_toks.append(t7)
        pk_tok[sl] = plane_toks[-1]

        sc_toks = []
        pg = prev_trs
        for g in range(G):
            t = kb.op("dve", lambda v, g=g: v.tensor_scalar(
                out=vb[:, 128 * g:128 * g + 128], in0=val[:, 128 * g:128 * g + 128],
                scalar1=sct[:, g:g + 1], scalar2=None, op0=ALU.mult),
                deps=plane_toks + [tev] + ([pg] if pg else []), tag=f"{name}sc{g}")
            sc_toks.append(t)
        prev_val = sc_toks[-1]

        ev_toks = []
        ptp = None
        for c in range(IN // 128):
            dep = [sc_toks[c]] + consts.tok
            if ptp is not None:
                dep.append(ptp)
            ttp2 = kb.op("pe", lambda t_, c=c: t_.transpose(
                out=pool.db[db_tp][:, :64].bitcast(BF16), in_=vb[:, 128 * c:128 * c + 128],
                identity=consts.ident_bf16[:]), deps=dep + [pool.rtok[db_tp]], tag=f"{name}tp{c}")
            pool.wtok[db_tp] = ttp2
            tev2 = kb.op("act", lambda s, c=c: s.copy(
                out=trs3[:, c, :], in_=pool.db[db_tp][:, :64].bitcast(BF16)),
                deps=[ttp2] + out_dma_toks[-1:], tag=f"{name}tev{c}")
            pool.rtok[db_tp] = tev2
            ptp = tev2
            ev_toks.append(tev2)
        prev_trs = ev_toks[-1]

        tdo = kb.dma("pool", s_out, lambda g, ot=ot: g.dma_start(
            out=out_dram[:, 128 * ot:128 * ot + 128].rearrange("(c p) o -> p c o", p=128),
            in_=trs3[:]), deps=ev_toks, tag=f"{name}odma{ot}")
        out_dma_toks.append(tdo)
    return out_dma_toks


def emit_l1ffn(kb, nc, ctx, consts, pool, x_dram, wguT_dram, wdnT_dram, out_dram,
               T=1024, name="lf", w_natural=False, w_deps=()):
    e = ctx.enter_context
    C = 16
    NI = 44
    NT = cdiv(T, 512)
    x3 = x_dram.rearrange("(c p) t -> c p t", p=128)
    o3 = out_dram.rearrange("(c p) t -> c p t", p=128)

    xbuf = [e(nc.sbuf_tensor(f"{name}_xb{i}", [128, T], F32)) for i in range(2)]
    s_xb = [kb.dma_sem(f"{name}xb{i}") for i in range(2)]
    xbuf_state = [None, None]
    xbuf_rd = [[], []]

    def x_at(c):
        sl = c % 2
        cur = xbuf_state[sl]
        if cur is not None and cur[0] == c:
            return xbuf[sl][:], [cur[1]]
        dep = xbuf_rd[sl]
        t = kb.dma("sp", s_xb[sl], lambda sy, c=c, sl=sl: sy.dma_start(
            out=xbuf[sl][:], in_=x3[c, :, :]), deps=dep, tag=f"{name}xd{c}")
        xbuf_state[sl] = (c, t)
        xbuf_rd[sl] = []
        return xbuf[sl][:], [t]

    def x_used(c, toks):
        xbuf_rd[c % 2].extend(toks)

    xln = e(nc.sbuf_tensor(f"{name}_xln", [128, C, T], BF16))
    ln_toks = emit_ln_fm(kb, nc, ctx, consts, pool, x_at, C, T, xln, 1e-5, [],
                         name + "ln", x_done=x_used)

    sg = e(nc.sbuf_tensor(f"{name}_sg", [128, 4, T], BF16))
    act = e(nc.sbuf_tensor(f"{name}_act", [128, NI, T], BF16))
    strip = [e(nc.sbuf_tensor(f"{name}_st{i}", [128, 512], BF16)) for i in range(2)]
    s_st = [kb.dma_sem(f"{name}st{i}") for i in range(2)]
    strip_rd = [None, None]
    sg_rd = [None] * 4
    sg_wr = [None] * 4
    act_wr = {}

    si = 0
    for g in range(11):
        for part in range(2):
            col0 = 512 * g + (5632 if part else 0)
            mm_last = [None] * 4
            for hc in range(C):
                sl = si % 2
                si += 1
                dep = ([strip_rd[sl]] if strip_rd[sl] else []) + list(w_deps)
                if w_natural:
                    td = kb.dma("sp", s_st[sl], lambda sy, hc=hc, col0=col0, sl=sl: sy.dma_start_transpose(
                        out=strip[sl][:], in_=wguT_dram[col0:col0 + 512, 128 * hc:128 * hc + 128]),
                        deps=dep, tag=f"{name}gu_st")
                else:
                    td = kb.dma("sp", s_st[sl], lambda sy, hc=hc, col0=col0, sl=sl: sy.dma_start(
                        out=strip[sl][:], in_=wguT_dram[128 * hc:128 * hc + 128, col0:col0 + 512]),
                        deps=dep, tag=f"{name}gu_st")
                mms = []
                for b in range(4):
                    for tb in range(NT):
                        n = min(512, T - 512 * tb)
                        dep2 = [td, ln_toks[hc]]
                        if hc == 0 and pool.rtok[b] is not None:
                            dep2.append(pool.rtok[b])
                        t = kb.op("pe", lambda t_, hc=hc, b=b, tb=tb, n=n, sl=sl: t_.matmul(
                            out=pool.db[b][:, 512 * tb:512 * tb + n],
                            lhsT=strip[sl][:, 128 * b:128 * b + 128],
                            rhs=xln[:, hc, 512 * tb:512 * tb + n],
                            start=(hc == 0), stop=(hc == C - 1)), deps=dep2,
                            tag=f"{name}gu_mm")
                        mms.append(t)
                        if hc == C - 1:
                            mm_last[b] = t
                strip_rd[sl] = mms[-1]
            for b in range(4):
                pool.wtok[b] = mm_last[b]
                if part == 0:
                    dep = [mm_last[b]] + ([sg_rd[b]] if sg_rd[b] else [])
                    t = kb.op("act", lambda s, b=b: s.activation(
                        out=sg[:, b, :], in_=pool.db[b][:, :T], func=AF.Silu),
                        deps=dep, tag=f"{name}silu")
                    pool.rtok[b] = t
                    sg_wr[b] = t
                else:
                    ib = 4 * g + b
                    t = kb.op("dve", lambda v, b=b, ib=ib: v.tensor_tensor(
                        out=act[:, ib, :], in0=pool.db[b][:, :T], in1=sg[:, b, :],
                        op=ALU.mult), deps=[mm_last[b], sg_wr[b]], tag=f"{name}amul")
                    pool.rtok[b] = t
                    sg_rd[b] = t
                    act_wr[ib] = t

    y = e(nc.sbuf_tensor(f"{name}_y", [128, C, T], BF16))
    y_wr = [None] * C
    for grp in range(4):
        mm_last = [None] * 4
        for ic in range(NI):
            sl = si % 2
            si += 1
            dep = ([strip_rd[sl]] if strip_rd[sl] else []) + list(w_deps)
            if w_natural:
                td = kb.dma("sp", s_st[sl], lambda sy, ic=ic, grp=grp, sl=sl: sy.dma_start_transpose(
                    out=strip[sl][:], in_=wdnT_dram[512 * grp:512 * grp + 512, 128 * ic:128 * ic + 128]),
                    deps=dep, tag=f"{name}dn_st")
            else:
                td = kb.dma("sp", s_st[sl], lambda sy, ic=ic, grp=grp, sl=sl: sy.dma_start(
                    out=strip[sl][:], in_=wdnT_dram[128 * ic:128 * ic + 128, 512 * grp:512 * grp + 512]),
                    deps=dep, tag=f"{name}dn_st")
            mms = []
            for b in range(4):
                for tb in range(NT):
                    n = min(512, T - 512 * tb)
                    dep2 = [td, act_wr[ic]]
                    if ic == 0 and pool.rtok[b] is not None:
                        dep2.append(pool.rtok[b])
                    t = kb.op("pe", lambda t_, ic=ic, b=b, tb=tb, n=n, sl=sl: t_.matmul(
                        out=pool.db[b][:, 512 * tb:512 * tb + n],
                        lhsT=strip[sl][:, 128 * b:128 * b + 128],
                        rhs=act[:, ic, 512 * tb:512 * tb + n],
                        start=(ic == 0), stop=(ic == NI - 1)), deps=dep2,
                        tag=f"{name}dn_mm")
                    mms.append(t)
                    if ic == NI - 1:
                        mm_last[b] = t
            strip_rd[sl] = mms[-1]
        for b in range(4):
            ob = 4 * grp + b
            pool.wtok[b] = mm_last[b]
            t = kb.op("act", lambda s, b=b, ob=ob: s.copy(
                out=y[:, ob, :], in_=pool.db[b][:, :T]),
                deps=[mm_last[b]] + ([y_wr[ob]] if y_wr[ob] else []), tag=f"{name}yev")
            pool.rtok[b] = t
            y_wr[ob] = t

    sq2 = e(nc.sbuf_tensor(f"{name}_sq2", [128, T], BF16))
    rrows = e(nc.sbuf_tensor(f"{name}_rr", [1, T], F32))
    mm_last2 = []
    prev_sq = None
    rd_sq2 = []
    for c in range(C):
        ts = kb.op("act", lambda s, c=c: s.activation(out=sq2[:], in_=y[:, c, :], func=AF.Square),
                   deps=[y_wr[c]] + ([prev_sq] if prev_sq else []) + rd_sq2, tag=f"{name}rsq{c}")
        prev_sq = ts
        rd_sq2 = []
        for i in range(NT):
            n = min(512, T - 512 * i)
            dep = [ts] + consts.tok
            if c == 0 and pool.rtok[0] is not None:
                dep.append(pool.rtok[0])
            t2 = kb.op("pe", lambda t_, i=i, c=c, n=n: t_.matmul(
                out=pool.db[0][0:1, 512 * i:512 * i + n], lhsT=consts.ones_col_bf16[:],
                rhs=sq2[:, 512 * i:512 * i + n], start=(c == 0), stop=(c == C - 1)),
                deps=dep, tag=f"{name}rmm")
            rd_sq2.append(t2)
            if c == C - 1:
                mm_last2.append(t2)
    pool.wtok[0] = mm_last2[-1]
    tv = kb.op("dve", lambda v: v.tensor_scalar(
        out=rrows[:], in0=pool.db[0][0:1, :T], scalar1=1.0 / 2048.0, scalar2=1e-6,
        op0=ALU.mult, op1=ALU.add), deps=mm_last2, tag=f"{name}rv")
    pool.rtok[0] = tv
    tl = kb.op("act", lambda s: s.activation(out=rrows[:], in_=rrows[:], func=AF.Ln),
               deps=[tv], tag=f"{name}rln")
    tr = kb.op("act", lambda s: s.activation(out=rrows[:], in_=rrows[:], func=AF.Exp, scale=-0.5),
               deps=[tl], tag=f"{name}rexp")
    toks_R = bcast_row(kb, consts, pool, 1, rrows[:], T, [tr], tag=f"{name}bR")

    obuf = [e(nc.sbuf_tensor(f"{name}_ob{i}", [128, T], F32)) for i in range(2)]
    s_ob = [kb.dma_sem(f"{name}ob0"), kb.dma_sem(f"{name}ob1")]
    out_toks = []
    ob_rd = [None, None]
    prev_t1 = None
    for c in range(C):
        xc, xdep = x_at(c)
        sl = c % 2
        t1 = kb.op("dve", lambda v, c=c, sl=sl: v.tensor_tensor(
            out=obuf[sl][:], in0=y[:, c, :], in1=pool.db[1][:, :T], op=ALU.mult),
            deps=[y_wr[c]] + toks_R + ([ob_rd[sl]] if ob_rd[sl] else []), tag=f"{name}f1_{c}")
        t2 = kb.op("dve", lambda v, xc=xc, sl=sl: v.tensor_tensor(
            out=obuf[sl][:], in0=obuf[sl][:], in1=xc, op=ALU.add),
            deps=[t1] + xdep, tag=f"{name}f2_{c}")
        x_used(c, [t2])
        td = kb.dma("pool", s_ob[sl], lambda g, c=c, sl=sl: g.dma_start(
            out=o3[c, :, :], in_=obuf[sl][:]), deps=[t2], tag=f"{name}odma{c}")
        ob_rd[sl] = td
        out_toks.append(td)
    pool.rtok[1] = prev_t1
    return out_toks


class StripStream:

    def __init__(self, kb, nc, ctx, name, shape, dtype, nslots=2, queue="pool"):
        e = ctx.enter_context
        self.kb = kb
        self.queue = queue
        self.shape = list(shape)
        self.bufs = [e(nc.sbuf_tensor(f"{name}_s{i}", list(shape), dtype))
                     for i in range(nslots)]
        self.sems = [kb.dma_sem(f"{name}{i}") for i in range(nslots)]
        self.readers = [[] for _ in range(nslots)]
        self.i = 0
        self.n = nslots

    def load(self, src_fn, w=None, tag=""):
        sl = self.i % self.n
        self.i += 1
        w = self.shape[1] if w is None else w
        t = self.kb.dma(self.queue, self.sems[sl],
                        lambda g, sl=sl, w=w: g.dma_start(out=self.bufs[sl][:, :w],
                                                          in_=src_fn()),
                        deps=self.readers[sl], tag=tag)
        self.readers[sl] = []
        return self.bufs[sl], sl, t

    def read_by(self, sl, toks):
        self.readers[sl].extend(toks)


def emit_l0ffn(kb, nc, ctx, consts, pool, x_dram, w, out_dram, T=1024, name="mf",
               pump=None):
    e = ctx.enter_context
    C = 16
    NT = cdiv(T, 512)
    NTT = T // 128
    x3 = x_dram.rearrange("(c p) t -> c p t", p=128)
    o3 = out_dram.rearrange("(c p) t -> c p t", p=128)

    xbuf = [e(nc.sbuf_tensor(f"{name}_xb{i}", [128, T], F32)) for i in range(2)]
    s_xb = [kb.dma_sem(f"{name}xb{i}") for i in range(2)]
    xbuf_state = [None, None]
    xbuf_rd = [[], []]

    def x_at(c):
        sl = c % 2
        cur = xbuf_state[sl]
        if cur is not None and cur[0] == c:
            return xbuf[sl][:], [cur[1]]
        dep = xbuf_rd[sl]
        t = kb.dma("sp", s_xb[sl], lambda sy, c=c, sl=sl: sy.dma_start(
            out=xbuf[sl][:], in_=x3[c, :, :]), deps=dep, tag=f"{name}xd{c}")
        xbuf_state[sl] = (c, t)
        xbuf_rd[sl] = []
        return xbuf[sl][:], [t]

    def x_used(c, toks):
        xbuf_rd[c % 2].extend(toks)

    xln = e(nc.sbuf_tensor(f"{name}_xln", [128, C, T], BF16))
    ln_toks = emit_ln_fm(kb, nc, ctx, consts, pool, x_at, C, T, xln, 1e-5, [],
                         name + "ln", x_done=x_used)

    rw_sb = e(nc.sbuf_tensor(f"{name}_rw", [8, 2048], BF16))
    s_rw = kb.dma_sem(f"{name}rw")
    t_rw = kb.dma("pool", s_rw, lambda g: g.dma_start(out=rw_sb[:], in_=w["router_w"][:]))
    rwT = e(nc.sbuf_tensor(f"{name}_rwT", [128, C, 8], BF16))
    rwT_toks = []
    prev = None
    for c in range(C):
        dep = [t_rw] + consts.tok + [pool.wtok[0], pool.rtok[0]]
        tp = kb.op("pe", lambda t_, c=c: t_.transpose(
            out=pool.db[0][:, :4].bitcast(BF16), in_=rw_sb[:, 128 * c:128 * c + 128],
            identity=consts.ident_bf16[:8, :8]), deps=dep, tag=f"{name}rwt{c}")
        pool.wtok[0] = tp
        te = kb.op("act", lambda s, c=c: s.copy(
            out=rwT[:, c, :], in_=pool.db[0][:, :4].bitcast(BF16)[:, :8]),
            deps=[tp] + ([prev] if prev else []), tag=f"{name}rwe{c}")
        pool.rtok[0] = te
        prev = te
        rwT_toks.append(te)

    lg_mm = []
    for c in range(C):
        for i in range(NT):
            n = min(512, T - 512 * i)
            dep = [rwT_toks[c], ln_toks[c]]
            if c == 0 and pool.rtok[1] is not None:
                dep += [pool.rtok[1], pool.wtok[1]]
            t = kb.op("pe", lambda t_, c=c, i=i, n=n: t_.matmul(
                out=pool.db[1][0:8, 512 * i:512 * i + n], lhsT=rwT[:, c, :],
                rhs=xln[:, c, 512 * i:512 * i + n], start=(c == 0), stop=(c == C - 1)),
                deps=dep, tag=f"{name}lgmm")
            if c == C - 1:
                lg_mm.append(t)
    pool.wtok[1] = lg_mm[-1]
    lg_sb = e(nc.sbuf_tensor(f"{name}_lg", [8, T], F32))
    tlg = kb.op("act", lambda s: s.copy(out=lg_sb[:], in_=pool.db[1][0:8, :T]),
                deps=lg_mm, tag=f"{name}lgev")
    pool.rtok[1] = tlg

    cw = e(nc.sbuf_tensor(f"{name}_cw", [128, NTT, 8], F32))
    col = e(nc.sbuf_tensor(f"{name}_col", [128, 6], F32))
    ex = e(nc.sbuf_tensor(f"{name}_ex", [128, 8], F32))
    eq = e(nc.sbuf_tensor(f"{name}_eq", [128, 8], F32))
    cw_toks = []
    prev_ops = []
    prev_t2 = None
    for tt in range(NTT):
        dep0 = [tlg] + consts.tok + [pool.wtok[2], pool.rtok[2]]
        tp = kb.op("pe", lambda t_, tt=tt: t_.transpose(
            out=pool.db[2][:, :8], in_=lg_sb[:, 128 * tt:128 * tt + 128],
            identity=consts.ident_f32[:8, :8]), deps=dep0, tag=f"{name}cwt{tt}")
        pool.wtok[2] = tp
        t0 = kb.op("dve", lambda v: v.reduce_max(
            out=col[:, 0:1], in_=pool.db[2][:, :8], axis=AX.X),
            deps=[tp] + prev_ops, tag=f"{name}r0")
        t1 = kb.op("dve", lambda v: v.tensor_single_scalar(
            out=col[:, 1:2], in_=col[:, 0:1], scalar=-1.0, op=ALU.mult),
            deps=[t0] + ([prev_t2] if prev_t2 else []), tag=f"{name}r1")
        t2 = kb.op("act", lambda s: s.activation(
            out=ex[:], in_=pool.db[2][:, :8], func=AF.Exp, bias=col[:, 1:2]),
            deps=[t1] + prev_ops, tag=f"{name}r2")
        prev_t2 = t2
        pool.rtok[2] = t2
        t3 = kb.op("dve", lambda v: v.reduce_max(out=col[:, 2:3], in_=ex[:], axis=AX.X),
                   deps=[t2], tag=f"{name}r3")
        t4 = kb.op("dve", lambda v: v.tensor_scalar(
            out=eq[:], in0=ex[:], scalar1=col[:, 2:3], scalar2=None, op0=ALU.is_equal),
            deps=[t3], tag=f"{name}r4")
        t5 = kb.op("dve", lambda v: v.scalar_tensor_tensor(
            out=eq[:], in0=eq[:], scalar=-1e30, in1=ex[:], op0=ALU.mult, op1=ALU.add),
            deps=[t4], tag=f"{name}r5")
        t6 = kb.op("dve", lambda v: v.reduce_max(out=col[:, 3:4], in_=eq[:], axis=AX.X),
                   deps=[t5], tag=f"{name}r6")
        t7 = kb.op("dve", lambda v: v.tensor_tensor(
            out=col[:, 4:5], in0=col[:, 2:3], in1=col[:, 3:4], op=ALU.add),
            deps=[t6], tag=f"{name}r7")
        t8 = kb.op("dve", lambda v: v.reciprocal(out=col[:, 5:6], in_=col[:, 4:5]),
                   deps=[t7], tag=f"{name}r8")
        t9 = kb.op("dve", lambda v: v.tensor_scalar(
            out=eq[:], in0=ex[:], scalar1=col[:, 3:4], scalar2=None, op0=ALU.is_ge),
            deps=[t6], tag=f"{name}r9")
        ta = kb.op("dve", lambda v: v.tensor_tensor(out=ex[:], in0=ex[:], in1=eq[:],
                                                    op=ALU.mult), deps=[t9], tag=f"{name}ra")
        tb = kb.op("dve", lambda v, tt=tt: v.tensor_scalar(
            out=cw[:, tt, :], in0=ex[:], scalar1=col[:, 5:6], scalar2=None, op0=ALU.mult),
            deps=[ta, t8], tag=f"{name}rb")
        prev_ops = [tb]
        cw_toks.append(tb)

    sgw = e(nc.sbuf_tensor(f"{name}_sgw", [16, 128], BF16))
    s_sgw = kb.dma_sem(f"{name}sgw")
    t_sgw = kb.dma("pool", s_sgw, lambda g: g.dma_start(out=sgw[:], in_=w["shared_gate_w"][:]))
    sgwT = e(nc.sbuf_tensor(f"{name}_sgwT", [128, 16], BF16))
    dep = [t_sgw] + consts.tok + [pool.wtok[3], pool.rtok[3]]
    tp = kb.op("pe", lambda t_: t_.transpose(
        out=pool.db[3][:, :8].bitcast(BF16), in_=sgw[:],
        identity=consts.ident_bf16[:16, :16]), deps=dep, tag=f"{name}sgwt")
    pool.wtok[3] = tp
    te = kb.op("act", lambda s: s.copy(out=sgwT[:], in_=pool.db[3][:, :8].bitcast(BF16)[:, :16]),
               deps=[tp], tag=f"{name}sgwe")
    pool.rtok[3] = te
    sg_mm = []
    for c in range(C):
        for i in range(NT):
            n = min(512, T - 512 * i)
            dep = [te, ln_toks[c]]
            if c == 0:
                dep += [pool.rtok[3], pool.wtok[3]]
            t = kb.op("pe", lambda t_, c=c, i=i, n=n: t_.matmul(
                out=pool.db[3][0:1, 512 * i:512 * i + n], lhsT=sgwT[:, c:c + 1],
                rhs=xln[:, c, 512 * i:512 * i + n], start=(c == 0), stop=(c == C - 1)),
                deps=dep, tag=f"{name}sgmm")
            if c == C - 1:
                sg_mm.append(t)
    pool.wtok[3] = sg_mm[-1]
    rows2 = e(nc.sbuf_tensor(f"{name}_rows2", [1, 2 * T], F32))
    srow = rows2[:, 0:T]
    cmbrow = rows2[:, T:2 * T]
    t_sr = kb.op("act", lambda s: s.activation(out=srow[:], in_=pool.db[3][0:1, :T],
                                               func=AF.Sigmoid), deps=sg_mm, tag=f"{name}sig")
    pool.rtok[3] = t_sr
    sgb = e(nc.sbuf_tensor(f"{name}_sgb", [128, T], F32))
    tb_toks = bcast_row(kb, consts, pool, 3, srow, T, [t_sr], tag=f"{name}sgbc")
    t_sgb = kb.op("dve", lambda v: v.tensor_copy(out=sgb[:], in_=pool.db[3][:, :T]),
                  deps=tb_toks, tag=f"{name}sgbe")
    pool.rtok[3] = t_sgb

    acc = e(nc.sbuf_tensor(f"{name}_acc", [128, C, T], F32))
    act = e(nc.sbuf_tensor(f"{name}_act", [128, 12, T], BF16))
    sg = e(nc.sbuf_tensor(f"{name}_sg", [128, 4, T], BF16))
    tmp = e(nc.sbuf_tensor(f"{name}_tmp", [128, T], F32))
    strips = StripStream(kb, nc, ctx, f"{name}st", [128, 512], BF16, 2, "pool")
    sg_rd = [None] * 4
    sg_wr = [None] * 4
    acc_wr = [None] * C
    act_wr = {}
    act_rd = {}
    state = {"tmp_rd": [], "cmbb_rd": [], "cmbrow_rd": []}

    def gate_up_phase(nblk, wg_src, wu_src, scale_bcast=None):
        for g0 in range(0, nblk, 4):
            gl = min(4, nblk - g0)
            col0 = 128 * g0
            width = 128 * gl
            for part in range(2):
                if pump:
                    pump()
                mm_last = [None] * gl
                for hc in range(C):
                    src = (wg_src if part == 0 else wu_src)
                    buf, sl, td = strips.load(lambda src=src, hc=hc, col0=col0, width=width:
                                              src(hc, col0, width), w=width, tag=f"{name}gu_st")
                    mms = []
                    for b in range(gl):
                        for tb_ in range(NT):
                            n = min(512, T - 512 * tb_)
                            dep2 = [td, ln_toks[hc]]
                            if hc == 0 and pool.rtok[b] is not None:
                                dep2 += [pool.rtok[b], pool.wtok[b]]
                            t = kb.op("pe", lambda t_, hc=hc, b=b, tb_=tb_, n=n, buf=buf: t_.matmul(
                                out=pool.db[b][:, 512 * tb_:512 * tb_ + n],
                                lhsT=buf[:, 128 * b:128 * b + 128],
                                rhs=xln[:, hc, 512 * tb_:512 * tb_ + n],
                                start=(hc == 0), stop=(hc == C - 1)), deps=dep2,
                                tag=f"{name}gu_mm")
                            mms.append(t)
                            if hc == C - 1:
                                mm_last[b] = t
                    strips.read_by(sl, [mms[-1]])
                for b in range(gl):
                    pool.wtok[b] = mm_last[b]
                    if part == 0:
                        dep = [mm_last[b]] + ([sg_rd[b]] if sg_rd[b] else [])
                        t = kb.op("act", lambda s, b=b: s.activation(
                            out=sg[:, b, :], in_=pool.db[b][:, :T], func=AF.Silu),
                            deps=dep, tag=f"{name}silu")
                        pool.rtok[b] = t
                        sg_wr[b] = t
                    else:
                        ib = g0 + b
                        if scale_bcast is None:
                            t = kb.op("dve", lambda v, b=b, ib=ib: v.tensor_tensor(
                                out=act[:, ib, :], in0=pool.db[b][:, :T], in1=sg[:, b, :],
                                op=ALU.mult), deps=[mm_last[b], sg_wr[b]] +
                                ([act_rd.get(ib)] if act_rd.get(ib) else []), tag=f"{name}amul")
                        else:
                            sb_ap, sb_tok = scale_bcast
                            t1 = kb.op("dve", lambda v, b=b, sb_ap=sb_ap, sb_tok=sb_tok: v.tensor_tensor(
                                out=tmp[:], in0=pool.db[b][:, :T], in1=sb_ap,
                                op=ALU.mult), deps=[mm_last[b], sb_tok] + state["tmp_rd"],
                                tag=f"{name}amul1")
                            state["tmp_rd"] = []
                            state["cmbb_rd"].append(t1)
                            t = kb.op("dve", lambda v, b=b, ib=ib: v.tensor_tensor(
                                out=act[:, ib, :], in0=tmp[:], in1=sg[:, b, :],
                                op=ALU.mult), deps=[t1, sg_wr[b]] +
                                ([act_rd.get(ib)] if act_rd.get(ib) else []), tag=f"{name}amul2")
                            state["tmp_rd"] = [t]
                        pool.rtok[b] = t
                        sg_rd[b] = t
                        act_wr[ib] = t

    def down_phase(n_ic, dn_src, first, scale_sgb):
        for grp in range(4):
            if pump:
                pump()
            mm_last = [None] * 4
            for ic in range(n_ic):
                buf, sl, td = strips.load(lambda ic=ic, grp=grp: dn_src(ic, 512 * grp),
                                          tag=f"{name}dn_st")
                mms = []
                for b in range(4):
                    for tb_ in range(NT):
                        n = min(512, T - 512 * tb_)
                        dep2 = [td, act_wr[ic]]
                        if ic == 0 and pool.rtok[b] is not None:
                            dep2 += [pool.rtok[b], pool.wtok[b]]
                        t = kb.op("pe", lambda t_, ic=ic, b=b, tb_=tb_, n=n, buf=buf: t_.matmul(
                            out=pool.db[b][:, 512 * tb_:512 * tb_ + n],
                            lhsT=buf[:, 128 * b:128 * b + 128],
                            rhs=act[:, ic, 512 * tb_:512 * tb_ + n],
                            start=(ic == 0), stop=(ic == n_ic - 1)), deps=dep2,
                            tag=f"{name}dn_mm")
                        mms.append(t)
                        act_rd[ic] = t
                        if ic == n_ic - 1:
                            mm_last[b] = t
                strips.read_by(sl, [mms[-1]])
            for b in range(4):
                ob = 4 * grp + b
                pool.wtok[b] = mm_last[b]
                if scale_sgb:
                    t1 = kb.op("dve", lambda v, b=b: v.tensor_tensor(
                        out=tmp[:], in0=pool.db[b][:, :T], in1=sgb[:],
                        op=ALU.mult), deps=[mm_last[b], t_sgb] + state["tmp_rd"], tag=f"{name}dse1")
                    state["tmp_rd"] = []
                    if first:
                        t = kb.op("dve", lambda v, ob=ob: v.tensor_copy(
                            out=acc[:, ob, :], in_=tmp[:]), deps=[t1], tag=f"{name}dse2")
                    else:
                        t = kb.op("dve", lambda v, ob=ob: v.tensor_tensor(
                            out=acc[:, ob, :], in0=acc[:, ob, :], in1=tmp[:],
                            op=ALU.add), deps=[t1, acc_wr[ob]], tag=f"{name}dse2")
                    state["tmp_rd"] = [t]
                    pool.rtok[b] = t1
                else:
                    t = kb.op("dve", lambda v, ob=ob, b=b: v.tensor_tensor(
                        out=acc[:, ob, :], in0=acc[:, ob, :], in1=pool.db[b][:, :T],
                        op=ALU.add), deps=[mm_last[b], acc_wr[ob]], tag=f"{name}eacc")
                    pool.rtok[b] = t
                acc_wr[ob] = t

    def sw_src(which, base):
        def f(hc, col0, width):
            return w[which][128 * hc:128 * hc + 128, base + col0:base + col0 + width]
        return f

    splits = [(0, 12), (12, 12), (24, 12), (36, 8)]
    for pi, (b0, nb) in enumerate(splits):
        gate_up_phase(nb, sw_src("sw_gate", 128 * b0), sw_src("sw_up", 128 * b0))
        down_phase(nb, lambda ic, oc, b0=b0: w["sw_down"][128 * (b0 + ic):128 * (b0 + ic) + 128,
                                                          oc:oc + 512],
                   first=(pi == 0), scale_sgb=True)

    cmbb = e(nc.sbuf_tensor(f"{name}_cmbb", [128, T], F32))
    for ee in range(8):
        rowtoks = []
        prevc = None
        for tt in range(NTT):
            dep = [cw_toks[tt]] + consts.tok + [pool.wtok[3], pool.rtok[3]]
            tp2 = kb.op("pe", lambda t_, tt=tt, ee=ee: t_.transpose(
                out=pool.db[3][0:1, :128], in_=cw[:, tt, ee:ee + 1],
                identity=consts.ident_f32[:]), deps=dep, tag=f"{name}cbt")
            pool.wtok[3] = tp2
            tc = kb.op("act", lambda s, tt=tt: s.copy(
                out=cmbrow[:, 128 * tt:128 * tt + 128], in_=pool.db[3][0:1, :128]),
                deps=[tp2] + ([prevc] if prevc else []) + state["cmbrow_rd"], tag=f"{name}cbe")
            state["cmbrow_rd"] = []
            pool.rtok[3] = tc
            prevc = tc
            rowtoks.append(tc)
        bts = bcast_row(kb, consts, pool, 3, cmbrow, T, rowtoks, tag=f"{name}cbb")
        state["cmbrow_rd"] = list(bts)
        tcb = kb.op("dve", lambda v: v.tensor_copy(out=cmbb[:], in_=pool.db[3][:, :T]),
                    deps=bts + state["cmbb_rd"], tag=f"{name}cbev")
        state["cmbb_rd"] = []
        pool.rtok[3] = tcb

        def wg_src(hc, col0, width, ee=ee):
            return w["w_gate"][ee, 128 * hc:128 * hc + 128, col0:col0 + width]

        def wu_src(hc, col0, width, ee=ee):
            return w["w_up"][ee, 128 * hc:128 * hc + 128, col0:col0 + width]

        gate_up_phase(11, wg_src, wu_src, scale_bcast=(cmbb[:], tcb))
        down_phase(11, lambda ic, oc, ee=ee: w["w_down"][ee, 128 * ic:128 * ic + 128,
                                                         oc:oc + 512],
                   first=False, scale_sgb=False)

    obuf = e(nc.sbuf_tensor(f"{name}_obuf", [128, T], F32))
    s_ob = kb.dma_sem(f"{name}ob")
    out_toks = []
    ob_rd = None
    for c in range(C):
        xc, xdep = x_at(c)
        t2 = kb.op("dve", lambda v, xc=xc, c=c: v.tensor_tensor(
            out=obuf[:], in0=acc[:, c, :], in1=xc, op=ALU.add),
            deps=[acc_wr[c]] + xdep + ([ob_rd] if ob_rd else []), tag=f"{name}fin{c}")
        x_used(c, [t2])
        td = kb.dma("sp", s_ob, lambda g, c=c: g.dma_start(
            out=o3[c, :, :], in_=obuf[:]), deps=[t2], tag=f"{name}odma{c}")
        ob_rd = td
        out_toks.append(td)
    return out_toks


def emit_l1attn_lat(kb, nc, ctx, consts, pool, x_dram, wkvd_dram, gamma_dram,
                    xln_out, lat_out, T=1024, name="la", lat_shared=None):
    e = ctx.enter_context
    C = 16
    NT = cdiv(T, 512)
    x3 = x_dram.rearrange("(c p) t -> c p t", p=128)
    xo3 = xln_out.rearrange("(c p) t -> c p t", p=128)
    lo3 = lat_out.rearrange("(c p) t -> c p t", p=128) if lat_out is not None else None

    xbuf = [e(nc.sbuf_tensor(f"{name}_xb{i}", [128, T], F32)) for i in range(2)]
    s_xb = [kb.dma_sem(f"{name}xb{i}") for i in range(2)]
    xbuf_state = [None, None]
    xbuf_rd = [[], []]

    def x_at(c):
        sl = c % 2
        cur = xbuf_state[sl]
        if cur is not None and cur[0] == c:
            return xbuf[sl][:], [cur[1]]
        dep = xbuf_rd[sl]
        t = kb.dma("sp", s_xb[sl], lambda sy, c=c, sl=sl: sy.dma_start(
            out=xbuf[sl][:], in_=x3[c, :, :]), deps=dep, tag=f"{name}xd{c}")
        xbuf_state[sl] = (c, t)
        xbuf_rd[sl] = []
        return xbuf[sl][:], [t]

    def x_used(c, toks):
        xbuf_rd[c % 2].extend(toks)

    xln = e(nc.sbuf_tensor(f"{name}_xln", [128, C, T], BF16))
    ln_toks = emit_ln_fm(kb, nc, ctx, consts, pool, x_at, C, T, xln, 1e-5, [],
                         name + "ln", x_done=x_used)
    s_xo = kb.dma_sem(f"{name}xo")
    xo_toks = []
    for c in range(C):
        t = kb.dma("sp", s_xo, lambda sy, c=c: sy.dma_start(
            out=xo3[c, :, :], in_=xln[:, c, :]), deps=[ln_toks[c]], tag=f"{name}xod{c}")
        xo_toks.append(t)

    wst = StripStream(kb, nc, ctx, f"{name}wd", [128, 256], BF16, 2, "pool")
    mm_last = [None, None]
    for dc in range(C):
        buf, sl, td = wst.load(lambda dc=dc: wkvd_dram[128 * dc:128 * dc + 128, :],
                               tag=f"{name}wdst")
        mms = []
        for lc in range(2):
            for i in range(NT):
                n = min(512, T - 512 * i)
                dep = [td, ln_toks[dc]]
                if dc == 0 and pool.rtok[lc] is not None:
                    dep += [pool.rtok[lc], pool.wtok[lc]]
                t = kb.op("pe", lambda t_, dc=dc, lc=lc, i=i, n=n, buf=buf: t_.matmul(
                    out=pool.db[lc][:, 512 * i:512 * i + n],
                    lhsT=buf[:, 128 * lc:128 * lc + 128],
                    rhs=xln[:, dc, 512 * i:512 * i + n],
                    start=(dc == 0), stop=(dc == C - 1)), deps=dep, tag=f"{name}latmm")
                mms.append(t)
                if dc == C - 1:
                    mm_last[lc] = t
        wst.read_by(sl, [mms[-1]])
    lat_sb = e(nc.sbuf_tensor(f"{name}_lat", [128, 2, T], F32))
    ev = []
    for lc in range(2):
        pool.wtok[lc] = mm_last[lc]
        t = kb.op("act", lambda s, lc=lc: s.copy(out=lat_sb[:, lc, :], in_=pool.db[lc][:, :T]),
                  deps=[mm_last[lc]], tag=f"{name}latev")
        pool.rtok[lc] = t
        ev.append(t)

    gamma = e(nc.sbuf_tensor(f"{name}_gam", [128, 2], F32))
    s_g = kb.dma_sem(f"{name}g")
    tg = kb.dma("sp", s_g, lambda sy: sy.dma_start(out=gamma[:], in_=gamma_dram[:]))
    latn = e(nc.sbuf_tensor(f"{name}_latn", [128, 2, T], BF16))
    rn_toks = emit_ln_fm(kb, nc, ctx, consts, pool,
                         lambda c: (lat_sb[:, c, :], [ev[c], tg]), 2, T, latn, 1e-6,
                         [], name + "rn", rms_only=True, gamma=gamma)
    s_lo = kb.dma_sem(f"{name}lo")
    lo_toks = []
    for lc in range(2):
        if lat_shared is None:
            t = kb.dma("sp", s_lo, lambda sy, lc=lc: sy.dma_start(
                out=lo3[lc, :, :], in_=latn[:, lc, :]), deps=[rn_toks[lc]],
                tag=f"{name}lod{lc}")
        else:
            t = kb.dma("sp", s_lo, lambda sy, lc=lc: sy.dma_start(
                out=lat_shared[bass.ds(sy.partition_id() % 2 * 256 + 128 * lc, 128), :],
                in_=latn[:, lc, :]), deps=[rn_toks[lc]],
                tag=f"{name}lod{lc}")
        lo_toks.append(t)
    return xo_toks + lo_toks


def emit_l1attn_core(kb, nc, ctx, consts, pool, xlnT_dram, latseq_dram, x_dram,
                     wq_dram, wkvu_dram, wo_dram, m01_dram, out_dram,
                     T=1024, L=2048, name="at", latseq_shared=None, lat_deps=(),
                     xln_deps=()):
    e = ctx.enter_context
    C = 16
    H = 16
    NT = cdiv(T, 512)
    NKB = L // 128
    x3 = x_dram.rearrange("(c p) t -> c p t", p=128)
    o3 = out_dram.rearrange("(c p) t -> c p t", p=128)

    xln = e(nc.sbuf_tensor(f"{name}_xln", [128, C, T], BF16))
    latn = e(nc.sbuf_tensor(f"{name}_latn", [128, 2, L], BF16))
    wu_sb = e(nc.sbuf_tensor(f"{name}_wu", [128, 2, 32, 128], BF16))
    s_in = kb.dma_sem(f"{name}in")
    t_xln = kb.dma("sp", s_in, lambda sy: sy.dma_start(
        out=xln[:], in_=xlnT_dram.rearrange("(c p) t -> p c t", p=128)),
        deps=list(xln_deps), tag=f"{name}xlnd")
    if latseq_shared is None:
        t_lat = kb.dma("sp", s_in, lambda sy: sy.dma_start(
            out=latn[:], in_=latseq_dram.rearrange("(c p) t -> p c t", p=128)),
            deps=list(lat_deps))
    else:
        t_la = [kb.dma("sp", s_in, lambda sy, r=r: sy.dma_start(
            out=latn[:, :, r * T:(r + 1) * T],
            in_=latseq_shared[r * 256:(r + 1) * 256, :].rearrange(
                "(c p) t -> p c t", p=128)),
            deps=list(lat_deps), tag=f"{name}lat{r}") for r in range(2)]
        t_lat = t_la[-1]
    s_wu = kb.dma_sem(f"{name}wu")
    t_wu = kb.dma("pool", s_wu, lambda g: g.dma_start(
        out=wu_sb[:], in_=wkvu_dram.rearrange("(l p) (c o) -> p l c o", p=128, o=128)))

    qs = e(nc.sbuf_tensor(f"{name}_qs", [128, H, T], BF16))
    wst = StripStream(kb, nc, ctx, f"{name}ws", [128, 512], BF16, 2, "pool")
    q_toks = [None] * H
    for og in range(4):
        mm_last = [None] * 4
        for dc in range(C):
            buf, sl, td = wst.load(lambda dc=dc, og=og: wq_dram[128 * dc:128 * dc + 128,
                                                               512 * og:512 * og + 512],
                                   tag=f"{name}qst")
            mms = []
            for b in range(4):
                for i in range(NT):
                    n = min(512, T - 512 * i)
                    dep = [td, t_xln]
                    if dc == 0 and pool.rtok[b] is not None:
                        dep += [pool.rtok[b], pool.wtok[b]]
                    t = kb.op("pe", lambda t_, dc=dc, b=b, i=i, n=n, buf=buf: t_.matmul(
                        out=pool.db[b][:, 512 * i:512 * i + n],
                        lhsT=buf[:, 128 * b:128 * b + 128],
                        rhs=xln[:, dc, 512 * i:512 * i + n],
                        start=(dc == 0), stop=(dc == C - 1)), deps=dep, tag=f"{name}qmm")
                    mms.append(t)
                    if dc == C - 1:
                        mm_last[b] = t
            wst.read_by(sl, [mms[-1]])
        for b in range(4):
            h = 4 * og + b
            pool.wtok[b] = mm_last[b]
            t = kb.op("act", lambda s, h=h, b=b: s.activation(
                out=qs[:, h, :], in_=pool.db[b][:, :T], func=AF.Copy,
                scale=0.08838834764831845), deps=[mm_last[b]], tag=f"{name}qev")
            pool.rtok[b] = t
            q_toks[h] = t

    m01 = xln
    t_m01 = kb.dma("sp", s_in, lambda sy: sy.dma_start(out=m01[:], in_=m01_dram[:]),
                   deps=q_toks, tag=f"{name}m01")

    kf = [e(nc.sbuf_tensor(f"{name}_kf{i}", [128, L], BF16)) for i in range(2)]
    vt = [e(nc.sbuf_tensor(f"{name}_vt{i}", [128, NKB, 128], BF16)) for i in range(2)]
    kf_rd = [[], []]
    vt_rd = [[], []]
    osb = e(nc.sbuf_tensor(f"{name}_osb", [128, H, T], BF16))
    pt = [e(nc.sbuf_tensor(f"{name}_pt{i}", [128, T], BF16)) for i in range(2)]
    pt_rd = [[], []]
    rowb = e(nc.sbuf_tensor(f"{name}_rowb", [1, T + 64], F32))
    sqv = e(nc.sbuf_tensor(f"{name}_sqv", [128, L], BF16))
    bcol = e(nc.sbuf_tensor(f"{name}_bcol", [128, 1], F32))
    rb = e(nc.sbuf_tensor(f"{name}_rb", [128, T], F32))
    st = {"sq_rd": [], "bcol_rd": [], "rb_rd": [], "rowb_rd": []}
    kv_toks = [None] * H
    osb_toks = [None] * H

    def emit_kv(h):
        sl = h % 2
        tk_ev = []
        nhalf = cdiv(L, 1024)
        for half in range(nhalf):
            w_half = min(1024, L - 1024 * half)
            mmk = []
            for lc in range(2):
                for i in range(cdiv(w_half, 512)):
                    n = min(512, w_half - 512 * i)
                    dep = [t_wu, t_lat]
                    if lc == 0 and i == 0:
                        dep += [pool.rtok[2], pool.wtok[2]]
                    t = kb.op("pe", lambda t_, lc=lc, i=i, n=n, h=h, half=half: t_.matmul(
                        out=pool.db[2][:, 512 * i:512 * i + n],
                        lhsT=wu_sb[:, lc, h, :],
                        rhs=latn[:, lc, 1024 * half + 512 * i:1024 * half + 512 * i + n],
                        start=(lc == 0), stop=(lc == 1)), deps=dep, tag=f"{name}kmm")
                    mmk.append(t)
            pool.wtok[2] = mmk[-1]
            te = kb.op("act", lambda s, half=half, sl=sl, w_half=w_half: s.copy(
                out=kf[sl][:, 1024 * half:1024 * half + w_half], in_=pool.db[2][:, :w_half]),
                deps=mmk + (kf_rd[sl] if half == 0 else []), tag=f"{name}kev")
            pool.rtok[2] = te
            tk_ev.append(te)
        kf_rd[sl] = []
        tv_ev = []
        prev = None
        for kb_ in range(NKB):
            mmv = []
            for lc in range(2):
                dep = [t_wu, t_lat]
                if lc == 0:
                    dep += [pool.rtok[3], pool.wtok[3]]
                t = kb.op("pe", lambda t_, lc=lc, kb_=kb_, h=h: t_.matmul(
                    out=pool.db[3][:, :128],
                    lhsT=latn[:, lc, 128 * kb_:128 * kb_ + 128],
                    rhs=wu_sb[:, lc, 16 + h, :],
                    start=(lc == 0), stop=(lc == 1)), deps=dep, tag=f"{name}vmm")
                mmv.append(t)
            pool.wtok[3] = mmv[-1]
            te = kb.op("act", lambda s, kb_=kb_, sl=sl: s.copy(
                out=vt[sl][:, kb_, :], in_=pool.db[3][:, :128]),
                deps=[mmv[-1]] + (vt_rd[sl] if kb_ == 0 else []) +
                ([prev] if prev else []), tag=f"{name}vev")
            pool.rtok[3] = te
            prev = te
            tv_ev.append(te)
        vt_rd[sl] = []
        kv_toks[h] = (tk_ev, tv_ev)

    def emit_norms(h):
        sl = h % 2
        tk_ev, _ = kv_toks[h]
        tq1 = kb.op("act", lambda s, h=h: s.activation(out=sqv[:, :T], in_=qs[:, h, :],
                                                       func=AF.Square),
                    deps=[q_toks[h]] + st["sq_rd"], tag=f"{name}nsq")
        st["sq_rd"] = []
        mmq = []
        for i in range(NT):
            n = min(512, T - 512 * i)
            dep = [tq1] + consts.tok
            if i == 0:
                dep += [pool.rtok[2], pool.wtok[2]]
            t = kb.op("pe", lambda t_, i=i, n=n: t_.matmul(
                out=pool.db[2][0:1, 512 * i:512 * i + n], lhsT=consts.ones_col_bf16[:],
                rhs=sqv[:, 512 * i:512 * i + n], start=True, stop=True),
                deps=dep, tag=f"{name}nqm")
            mmq.append(t)
        pool.wtok[2] = mmq[-1]
        t_mq = kb.op("dve", lambda v: v.reduce_max(out=rowb[0:1, T:T + 1],
                                                   in_=pool.db[2][0:1, :T], axis=AX.X),
                     deps=mmq + st["rowb_rd"], tag=f"{name}nqr")
        st["rowb_rd"] = []
        pool.rtok[2] = t_mq
        tk1 = kb.op("act", lambda s, sl=sl: s.activation(out=sqv[:, :L], in_=kf[sl][:, :],
                                                         func=AF.Square),
                    deps=tk_ev + mmq, tag=f"{name}nsk")
        kf_rd[sl].append(tk1)
        mx_toks = []
        nhalf = cdiv(L, 1024)
        for half in range(nhalf):
            w_half = min(1024, L - 1024 * half)
            mmk = []
            for i in range(cdiv(w_half, 512)):
                n = min(512, w_half - 512 * i)
                dep = [tk1] + consts.tok + [pool.rtok[2], pool.wtok[2]]
                t = kb.op("pe", lambda t_, i=i, n=n, half=half: t_.matmul(
                    out=pool.db[2][0:1, 512 * i:512 * i + n], lhsT=consts.ones_col_bf16[:],
                    rhs=sqv[:, 1024 * half + 512 * i:1024 * half + 512 * i + n],
                    start=True, stop=True), deps=dep, tag=f"{name}nkm")
                mmk.append(t)
            pool.wtok[2] = mmk[-1]
            t_mk = kb.op("dve", lambda v, half=half, w_half=w_half: v.reduce_max(
                out=rowb[0:1, T + 1 + half:T + 2 + half], in_=pool.db[2][0:1, :w_half],
                axis=AX.X), deps=mmk, tag=f"{name}nkr")
            pool.rtok[2] = t_mk
            mx_toks.append(t_mk)
        if nhalf > 1:
            t1 = kb.op("dve", lambda v: v.tensor_tensor(
                out=rowb[0:1, T + 1:T + 2], in0=rowb[0:1, T + 1:T + 2],
                in1=rowb[0:1, T + 2:T + 3], op=ALU.max), deps=mx_toks, tag=f"{name}nmx")
        else:
            t1 = mx_toks[-1]
        t2 = kb.op("dve", lambda v: v.tensor_tensor(
            out=rowb[0:1, T:T + 1], in0=rowb[0:1, T:T + 1],
            in1=rowb[0:1, T + 1:T + 2], op=ALU.mult), deps=[t1, t_mq], tag=f"{name}nmul")
        t3 = kb.op("act", lambda s: s.activation(out=rowb[0:1, T + 3:T + 4],
                                                 in_=rowb[0:1, T:T + 1], func=AF.Sqrt),
                   deps=[t2], tag=f"{name}nsqrt")
        t4 = kb.op("dve", lambda v: v.tensor_single_scalar(
            out=rowb[0:1, T + 4:T + 5], in_=rowb[0:1, T + 3:T + 4], scalar=-1.0,
            op=ALU.mult), deps=[t3], tag=f"{name}nneg")
        st["rowb_rd"].append(t4)
        dep = [t4] + consts.tok + [pool.rtok[3], pool.wtok[3]]
        t5 = kb.op("pe", lambda t_: t_.matmul(
            out=pool.db[3][:, 0:1], lhsT=consts.ones_row_f32[:],
            rhs=rowb[0:1, T + 4:T + 5], start=True, stop=True), deps=dep, tag=f"{name}nbc")
        pool.wtok[3] = t5
        t6 = kb.op("act", lambda s: s.copy(out=bcol[:], in_=pool.db[3][:, 0:1]),
                   deps=[t5] + st["bcol_rd"], tag=f"{name}nbe")
        st["bcol_rd"] = []
        pool.rtok[3] = t6
        return t6

    def emit_attn(h, t_bcol):
        sl = h % 2
        tk_ev, tv_ev = kv_toks[h]
        den_last = None
        av_last = []
        stmm_all = []
        avmm_all = []
        exp_toks = []
        prev_exp = [None, None]
        prev_mask = [None, None]
        for kb_ in range(NKB):
            psl = kb_ % 2
            dep = [tk_ev[(128 * kb_) // 1024], q_toks[h]]
            if prev_exp[psl] is not None:
                dep.append(prev_exp[psl])
            elif pool.rtok[psl] is not None:
                dep += [pool.rtok[psl], pool.wtok[psl]]
            st_mm = []
            for i in range(NT):
                n = min(512, T - 512 * i)
                t = kb.op("pe", lambda t_, kb_=kb_, i=i, n=n, sl=sl, psl=psl: t_.matmul(
                    out=pool.db[psl][:, 512 * i:512 * i + n],
                    lhsT=kf[sl][:, 128 * kb_:128 * kb_ + 128],
                    rhs=qs[:, h, 512 * i:512 * i + n], start=True, stop=True),
                    deps=dep, tag=f"{name}smm")
                st_mm.append(t)
            stmm_all.extend(st_mm)
            pool.wtok[psl] = st_mm[-1]
            dep_e = st_mm + [t_bcol] + pt_rd[psl]
            te = kb.op("act", lambda s, psl=psl: s.activation(
                out=pt[psl][:], in_=pool.db[psl][:, :T], func=AF.Exp, bias=bcol[:]),
                deps=dep_e, tag=f"{name}exp")
            pt_rd[psl] = []
            prev_exp[psl] = te
            pool.rtok[psl] = te
            exp_toks.append(te)
            tm = kb.op("dve", lambda v, kb_=kb_, psl=psl: v.tensor_tensor(
                out=pt[psl][:], in0=pt[psl][:], in1=m01[:, kb_, :], op=ALU.mult),
                deps=[te, t_m01], tag=f"{name}msk")
            for i in range(NT):
                n = min(512, T - 512 * i)
                dep_d = [tm] + consts.tok
                if kb_ == 0 and i == 0:
                    dep_d += [pool.rtok[2], pool.wtok[2]]
                t = kb.op("pe", lambda t_, i=i, n=n, kb_=kb_, psl=psl: t_.matmul(
                    out=pool.db[2][0:1, 512 * i:512 * i + n], lhsT=consts.ones_col_bf16[:],
                    rhs=pt[psl][:, 512 * i:512 * i + n], start=(kb_ == 0), stop=(kb_ == NKB - 1)),
                    deps=dep_d, tag=f"{name}dmm")
                pt_rd[psl].append(t)
                if kb_ == NKB - 1 and i == NT - 1:
                    den_last = t
            for i in range(NT):
                n = min(512, T - 512 * i)
                dep_a = [tm, tv_ev[kb_]]
                if kb_ == 0 and i == 0:
                    dep_a += [pool.rtok[3], pool.wtok[3]]
                t = kb.op("pe", lambda t_, i=i, n=n, kb_=kb_, sl=sl, psl=psl: t_.matmul(
                    out=pool.db[3][:, 512 * i:512 * i + n], lhsT=vt[sl][:, kb_, :],
                    rhs=pt[psl][:, 512 * i:512 * i + n], start=(kb_ == 0), stop=(kb_ == NKB - 1)),
                    deps=dep_a, tag=f"{name}avmm")
                pt_rd[psl].append(t)
                avmm_all.append(t)
                if kb_ == NKB - 1:
                    av_last.append(t)
        kf_rd[sl].extend(stmm_all)
        vt_rd[sl].extend(avmm_all)
        pool.wtok[2] = den_last
        pool.wtok[3] = av_last[-1]
        st["bcol_rd"].extend(exp_toks)
        t_rd = kb.op("dve", lambda v: v.reciprocal(out=rowb[0:1, :T], in_=pool.db[2][0:1, :T]),
                     deps=[den_last] + st["rowb_rd"], tag=f"{name}rden")
        st["rowb_rd"] = [t_rd]
        pool.rtok[2] = t_rd
        bt = bcast_row(kb, consts, pool, 0, rowb[0:1, :T], T, [t_rd], tag=f"{name}rbb")
        st["rowb_rd"].extend(bt)
        t_rb = kb.op("act", lambda s: s.copy(out=rb[:], in_=pool.db[0][:, :T]),
                     deps=bt + st["rb_rd"], tag=f"{name}rbe")
        st["rb_rd"] = []
        pool.rtok[0] = t_rb
        t_o = kb.op("dve", lambda v, h=h: v.tensor_tensor(
            out=osb[:, h, :], in0=pool.db[3][:, :T], in1=rb[:], op=ALU.mult),
            deps=av_last + [t_rb], tag=f"{name}oev")
        st["rb_rd"] = [t_o]
        pool.rtok[3] = t_o
        osb_toks[h] = t_o

    for h in range(H):
        emit_kv(h)
        tb = emit_norms(h)
        emit_attn(h, tb)

    xbuf = [e(nc.sbuf_tensor(f"{name}_xb{i}", [128, T], F32)) for i in range(2)]
    s_xb = [kb.dma_sem(f"{name}xb{i}") for i in range(2)]
    obuf = e(nc.sbuf_tensor(f"{name}_obuf", [128, T], F32))
    s_ob = kb.dma_sem(f"{name}ob")
    out_toks = []
    xb_rd = [[], []]
    ob_rd = None
    for og in range(4):
        mm_last = [None] * 4
        for dc in range(C):
            buf, sl, td = wst.load(lambda dc=dc, og=og: wo_dram[128 * dc:128 * dc + 128,
                                                               512 * og:512 * og + 512],
                                   tag=f"{name}ost")
            mms = []
            for b in range(4):
                for i in range(NT):
                    n = min(512, T - 512 * i)
                    dep = [td, osb_toks[dc]]
                    if dc == 0 and pool.rtok[b] is not None:
                        dep += [pool.rtok[b], pool.wtok[b]]
                    t = kb.op("pe", lambda t_, dc=dc, b=b, i=i, n=n, buf=buf: t_.matmul(
                        out=pool.db[b][:, 512 * i:512 * i + n],
                        lhsT=buf[:, 128 * b:128 * b + 128],
                        rhs=osb[:, dc, 512 * i:512 * i + n],
                        start=(dc == 0), stop=(dc == C - 1)), deps=dep, tag=f"{name}omm")
                    mms.append(t)
                    if dc == C - 1:
                        mm_last[b] = t
            wst.read_by(sl, [mms[-1]])
        for b in range(4):
            c = 4 * og + b
            pool.wtok[b] = mm_last[b]
            sl = c % 2
            tx = kb.dma("sp", s_xb[sl], lambda sy, c=c, sl=sl: sy.dma_start(
                out=xbuf[sl][:], in_=x3[c, :, :]), deps=xb_rd[sl], tag=f"{name}xd{c}")
            xb_rd[sl] = []
            t2 = kb.op("dve", lambda v, b=b, sl=sl: v.tensor_tensor(
                out=obuf[:], in0=pool.db[b][:, :T], in1=xbuf[sl][:], op=ALU.add),
                deps=[mm_last[b], tx] + ([ob_rd] if ob_rd else []), tag=f"{name}fin{c}")
            xb_rd[sl].append(t2)
            pool.rtok[b] = t2
            td2 = kb.dma("sp", s_ob, lambda g, c=c: g.dma_start(
                out=o3[c, :, :], in_=obuf[:]), deps=[t2], tag=f"{name}odma{c}")
            ob_rd = td2
            out_toks.append(td2)
    return out_toks


def emit_wtranspose(kb, nc, ctx, consts, pool, w_dram, out_dram, O, D, name,
                    scale_col=None, scale_tok=None, db=0):
    e = ctx.enter_context
    ND = D // 128
    strip = [e(nc.sbuf_tensor(f"{name}_w{i}", [128, D], BF16)) for i in range(2)]
    s_w = [kb.dma_sem(f"{name}w{i}") for i in range(2)]
    strip_rd = [[], []]
    trs = e(nc.sbuf_tensor(f"{name}_tr", [128, ND, 128], BF16))
    s_o = kb.dma_sem(f"{name}o")
    out_toks = []
    n_ot = cdiv(O, 128)
    prev_odma = None
    prev_ev = None
    for ot in range(n_ot):
        wo = min(128, O - 128 * ot)
        sl = ot % 2
        td = kb.dma("pool", s_w[sl], lambda g, ot=ot, wo=wo, sl=sl: g.dma_start(
            out=strip[sl][:wo, :], in_=w_dram[128 * ot:128 * ot + wo, :]),
            deps=strip_rd[sl], tag=f"{name}wd{ot}")
        strip_rd[sl] = []
        ev_toks = []
        for dc in range(ND):
            dep = [td] + consts.tok + [pool.rtok[db], pool.wtok[db]] + ([scale_tok] if scale_tok else [])
            tp = kb.op("pe", lambda t_, dc=dc, wo=wo, sl=sl: t_.transpose(
                out=pool.db[db][:, :64].bitcast(BF16)[:, :wo],
                in_=strip[sl][:wo, 128 * dc:128 * dc + 128],
                identity=consts.ident_bf16[:wo, :wo]), deps=dep, tag=f"{name}tp")
            pool.wtok[db] = tp
            strip_rd[sl].append(tp)
            if scale_col is not None:
                tev = kb.op("dve", lambda v, dc=dc, wo=wo: v.tensor_scalar(
                    out=trs[:, dc, :wo], in0=pool.db[db][:, :64].bitcast(BF16)[:, :wo],
                    scalar1=scale_col[:, dc:dc + 1], scalar2=None, op0=ALU.mult),
                    deps=[tp] + ([prev_odma] if prev_odma and dc == 0 else []) +
                    ([prev_ev] if prev_ev else []), tag=f"{name}te")
            else:
                tev = kb.op("act", lambda s, dc=dc, wo=wo: s.copy(
                    out=trs[:, dc, :wo], in_=pool.db[db][:, :64].bitcast(BF16)[:, :wo]),
                    deps=[tp] + ([prev_odma] if prev_odma and dc == 0 else []) +
                    ([prev_ev] if prev_ev else []), tag=f"{name}te")
            prev_ev = tev
            pool.rtok[db] = tev
            ev_toks.append(tev)
        tdo = kb.dma("sp", s_o, lambda g, ot=ot, wo=wo: g.dma_start(
            out=out_dram[:, 128 * ot:128 * ot + wo].rearrange("(dc p) o -> p dc o", p=128),
            in_=trs[:, :, :wo]), deps=ev_toks, tag=f"{name}odma{ot}")
        prev_odma = tdo
        out_toks.append(tdo)
    return out_toks


def emit_l0attn_a1(kb, nc, ctx, consts, pool, x_dram, winT_dram, cw_dram, cb_dram,
                   dtb_dram, a_dram, d_dram, y_dram, gate_dram, xbc_dram,
                   dacs_dram, dprod_dram, sout_dram, T=1024, name="ma",
                   triu_dram=None, idneg_dram=None, ltc_dram=None, sout_shared=None):
    e = ctx.enter_context
    C = 16
    TH = T + 3
    NC = T // 256
    NT = cdiv(T, 512)
    x3 = x_dram.rearrange("(c p) t -> c p t", p=128)

    xtk = e(nc.sbuf_tensor(f"{name}_xtk", [128, 2, 4352], BF16))
    _xtk_f32 = xtk[:].rearrange("p a b -> p (a b)").bitcast(F32)
    xbuf = [_xtk_f32[:, 0:TH], _xtk_f32[:, TH:2 * TH]]
    s_xb = [kb.dma_sem(f"{name}xb{i}") for i in range(2)]
    xbuf_state = [None, None]
    xbuf_rd = [[], []]

    def x_at_full(c):
        sl = c % 2
        cur = xbuf_state[sl]
        if cur is not None and cur[0] == c:
            return xbuf[sl][:], [cur[1]]
        dep = xbuf_rd[sl]
        t = kb.dma("sp", s_xb[sl], lambda sy, c=c, sl=sl: sy.dma_start(
            out=xbuf[sl][:], in_=x3[c, :, :]), deps=dep, tag=f"{name}xd{c}")
        xbuf_state[sl] = (c, t)
        xbuf_rd[sl] = []
        return xbuf[sl][:], [t]

    def x_used(c, toks):
        xbuf_rd[c % 2].extend(toks)

    xln_t = e(nc.sbuf_tensor(f"{name}_xln", [128, C, TH], BF16))
    xln = xln_t[:]
    ln_toks = emit_ln_fm(kb, nc, ctx, consts, pool,
                         lambda c: ((lambda ap, d: (ap[:, 3:TH], d))(*x_at_full(c))),
                         C, T, xln[:, :, 3:TH], 1e-5, [], name + "ln", x_done=x_used)
    lnh_toks = emit_ln_fm(kb, nc, ctx, consts, pool,
                          lambda c: ((lambda ap, d: (ap[:, 0:3], d))(*x_at_full(c))),
                          C, 3, xln[:, :, 0:3], 1e-5, [], name + "lh", x_done=x_used)
    ln_all = [None] * C
    for c in range(C):
        ln_all[c] = ln_toks[c]
    lnh_all = lnh_toks

    strips = StripStream(kb, nc, ctx, f"{name}st", [128, 512], BF16, 2, "pool")
    y_sb = e(nc.sbuf_tensor(f"{name}_ysb", [128, 2, 4096], BF16))
    halo_sb = y_sb[:].rearrange("p a b -> p (a b)")[0:3, 0:4352]
    xbc_pre = [e(nc.sbuf_tensor(f"{name}_xp{i}", [128, TH], BF16)) for i in range(2)]
    xbc_out = [e(nc.sbuf_tensor(f"{name}_xo{i}", [128, T], BF16)) for i in range(2)]
    tmp_cv = e(nc.sbuf_tensor(f"{name}_cv", [128, T], F32))
    cw_sb = e(nc.sbuf_tensor(f"{name}_cw", [128, 34, 4], F32))
    cb_sb = e(nc.sbuf_tensor(f"{name}_cb", [128, 34], F32))
    dtp_sb = e(nc.sbuf_tensor(f"{name}_dtp", [64, T], F32))
    s_c = kb.dma_sem(f"{name}c")
    t_cw = kb.dma("sp", s_c, lambda sy: sy.dma_start(out=cw_sb[:], in_=cw_dram[:]))
    t_cb = kb.dma("sp", s_c, lambda sy: sy.dma_start(out=cb_sb[:], in_=cb_dram[:]))
    s_xbc = kb.dma_sem(f"{name}xbc")
    xpre_rd = [[], []]
    xout_rd = [[], []]
    tmp_rd = []
    xbc_toks = [None] * 34
    dtp_tok = None
    halo_rd = []
    halo_all = []

    groups = [list(range(3 * g, min(3 * g + 3, 34))) for g in range(12)] + [[34]]
    for gi, grp in enumerate(groups):
        is_dtp = (grp == [34])
        col0 = 4096 + 128 * grp[0]
        width = 64 if is_dtp else 128 * len(grp)
        mm_last = [None] * len(grp)
        mm_halo = None
        for dc in range(C):
            buf, sl, td = strips.load(
                lambda dc=dc, col0=col0, width=width: winT_dram[128 * dc:128 * dc + 128,
                                                                col0:col0 + width],
                w=width, tag=f"{name}b1st")
            mms = []
            for b in range(len(grp)):
                wb = 64 if is_dtp else 128
                for tb_ in range(NT):
                    n = min(512, T - 512 * tb_)
                    dep2 = [td, ln_all[dc]]
                    if dc == 0 and pool.rtok[b] is not None:
                        dep2 += [pool.rtok[b], pool.wtok[b]]
                    t = kb.op("pe", lambda t_, dc=dc, b=b, tb_=tb_, n=n, buf=buf, wb=wb: t_.matmul(
                        out=pool.db[b][:wb, 512 * tb_:512 * tb_ + n],
                        lhsT=buf[:, 128 * b:128 * b + wb],
                        rhs=xln[:, dc, 3:3 + T][:, 512 * tb_:512 * tb_ + n],
                        start=(dc == 0), stop=(dc == C - 1)), deps=dep2, tag=f"{name}b1mm")
                    mms.append(t)
                    if dc == C - 1:
                        mm_last[b] = t
            if not is_dtp:
                dep2 = [td, lnh_all[dc]]
                if dc == 0 and pool.rtok[3] is not None:
                    dep2 += [pool.rtok[3], pool.wtok[3]]
                th = kb.op("pe", lambda t_, dc=dc, width=width, buf=buf: t_.matmul(
                    out=pool.db[3][0:3, :width], lhsT=xln[:, dc, 0:3],
                    rhs=buf[:, :width], start=(dc == 0), stop=(dc == C - 1)),
                    deps=dep2, tag=f"{name}hmm")
                mms.append(th)
                if dc == C - 1:
                    mm_halo = th
            strips.read_by(sl, [mms[-1], mms[-2] if len(mms) > 1 else mms[-1]])
        if is_dtp:
            pool.wtok[0] = mm_last[0]
            dtp_tok = kb.op("act", lambda s: s.copy(out=dtp_sb[:], in_=pool.db[0][:64, :T]),
                            deps=[mm_last[0]], tag=f"{name}dtpev")
            pool.rtok[0] = dtp_tok
            continue
        pool.wtok[3] = mm_halo
        t_he = kb.op("act", lambda s, col0=col0, width=width: s.copy(
            out=halo_sb[:, col0 - 4096:col0 - 4096 + width], in_=pool.db[3][0:3, :width]),
            deps=[mm_halo] + halo_rd, tag=f"{name}hev")
        halo_rd = []
        halo_all.append(t_he)
        pool.rtok[3] = t_he
        for b in range(len(grp)):
            blk = grp[b]
            sl2 = blk % 2
            pool.wtok[b] = mm_last[b]
            dep = [t_he] + consts.tok + [pool.rtok[3], pool.wtok[3]]
            tp = kb.op("pe", lambda t_, blk=blk: t_.transpose(
                out=pool.db[3][:, 64:66].bitcast(BF16)[:, :3],
                in_=halo_sb[:, 128 * blk:128 * blk + 128],
                identity=consts.ident_bf16[:3, :3]), deps=dep, tag=f"{name}htp")
            pool.wtok[3] = tp
            halo_rd.append(tp)
            halo_all.append(tp)
            t_hc = kb.op("act", lambda s, sl2=sl2: s.copy(
                out=xbc_pre[sl2][:, 0:3], in_=pool.db[3][:, 64:66].bitcast(BF16)[:, :3]),
                deps=[tp] + xpre_rd[sl2], tag=f"{name}hcp")
            pool.rtok[3] = t_hc
            t_mc = kb.op("act", lambda s, b=b, sl2=sl2: s.copy(
                out=xbc_pre[sl2][:, 3:TH], in_=pool.db[b][:, :T]),
                deps=[mm_last[b]] + xpre_rd[sl2], tag=f"{name}mcp")
            pool.rtok[b] = t_mc
            xpre_rd[sl2] = []
            t1 = kb.op("dve", lambda v, blk=blk, sl2=sl2: v.tensor_scalar(
                out=tmp_cv[:], in0=xbc_pre[sl2][:, 0:T], scalar1=cw_sb[:, blk, 0:1],
                scalar2=None, op0=ALU.mult), deps=[t_hc, t_mc, t_cw] + tmp_rd,
                tag=f"{name}cv0")
            for k_ in range(1, 4):
                t1 = kb.op("dve", lambda v, blk=blk, sl2=sl2, k_=k_: v.scalar_tensor_tensor(
                    out=tmp_cv[:], in0=xbc_pre[sl2][:, k_:k_ + T], scalar=cw_sb[:, blk, k_:k_ + 1],
                    in1=tmp_cv[:], op0=ALU.mult, op1=ALU.add), deps=[t1], tag=f"{name}cv{k_}")
            xpre_rd[sl2] = [t1]
            t_si = kb.op("act", lambda s, blk=blk, sl2=sl2: s.activation(
                out=xbc_out[sl2][:], in_=tmp_cv[:], func=AF.Silu, bias=cb_sb[:, blk:blk + 1]),
                deps=[t1, t_cb] + xout_rd[sl2], tag=f"{name}cvsi")
            tmp_rd = [t_si]
            td2 = kb.dma("pool", s_xbc, lambda g, blk=blk, sl2=sl2: g.dma_start(
                out=xbc_dram[128 * blk:128 * blk + 128, :], in_=xbc_out[sl2][:]),
                deps=[t_si], tag=f"{name}xbcd{blk}")
            xout_rd[sl2] = [td2]
            xbc_toks[blk] = td2

    NTB = T // 128
    xbc_ch = e(nc.sbuf_tensor(f"{name}_xch", [128, 34, 256], BF16))
    gbuf = xbc_ch[:].rearrange("p c t -> p (c t)")[:, 0:NTB * 512].rearrange(
        "p (a b) -> p a b", b=512)
    s_g = kb.dma_sem(f"{name}g")
    gate_toks = []
    gb_rd = []
    for oc in range(8):
        mm_last = [None] * NTB
        for dc in range(C):
            buf, sl, td = strips.load(
                lambda dc=dc, oc=oc: winT_dram[128 * dc:128 * dc + 128,
                                               512 * oc:512 * oc + 512],
                tag=f"{name}b2st")
            mms = []
            for tb_ in range(NTB):
                dbi, half = tb_ // 2, tb_ % 2
                dep2 = [td, ln_all[dc]]
                if dc == 0 and pool.rtok[dbi] is not None:
                    dep2 += [pool.rtok[dbi], pool.wtok[dbi]]
                t = kb.op("pe", lambda t_, dc=dc, tb_=tb_, dbi=dbi, half=half, buf=buf: t_.matmul(
                    out=pool.db[dbi][:, 512 * half:512 * half + 512],
                    lhsT=xln[:, dc, 3 + 128 * tb_:3 + 128 * tb_ + 128],
                    rhs=buf[:, :], start=(dc == 0), stop=(dc == C - 1)),
                    deps=dep2, tag=f"{name}b2mm")
                mms.append(t)
                if dc == C - 1:
                    mm_last[tb_] = t
            strips.read_by(sl, mms[-4:])
        ev = []
        for tb_ in range(NTB):
            dbi, half = tb_ // 2, tb_ % 2
            pool.wtok[dbi] = mm_last[tb_]
            t = kb.op("act", lambda s, tb_=tb_, dbi=dbi, half=half: s.copy(
                out=gbuf[:, tb_, :].squeeze(1) if False else gbuf[:, tb_, :],
                in_=pool.db[dbi][:, 512 * half:512 * half + 512]),
                deps=[mm_last[tb_]] + gb_rd, tag=f"{name}gev")
            pool.rtok[dbi] = t
            ev.append(t)
        gb_rd = []
        tdg = kb.dma("pool", s_g, lambda g, oc=oc: g.dma_start(
            out=gate_dram[:, 512 * oc:512 * oc + 512].rearrange("(tb p) o -> p tb o", p=128),
            in_=gbuf[:]), deps=ev, tag=f"{name}gd{oc}")
        gb_rd = [tdg]
        gate_toks.append(tdg)

    dtb_sb = e(nc.sbuf_tensor(f"{name}_dtb", [64, 1], F32))
    a_sb = e(nc.sbuf_tensor(f"{name}_a", [64, 1], F32))
    d_sb = e(nc.sbuf_tensor(f"{name}_d", [1, 64], F32))
    t_dtb = kb.dma("sp", s_c, lambda sy: sy.dma_start(out=dtb_sb[:], in_=dtb_dram[:]))
    t_a = kb.dma("sp", s_c, lambda sy: sy.dma_start(out=a_sb[:], in_=a_dram[:]))
    t_d = kb.dma("sp", s_c, lambda sy: sy.dma_start(out=d_sb[:], in_=d_dram[:]))
    dt_sb = e(nc.sbuf_tensor(f"{name}_dt", [64, T], F32))
    t64a = e(nc.sbuf_tensor(f"{name}_t64a", [64, T], F32))
    zt = t64a
    da_sb = dtp_sb
    t1 = kb.op("dve", lambda v: v.tensor_scalar(
        out=zt[:], in0=dtp_sb[:], scalar1=dtb_sb[:], scalar2=None, op0=ALU.add),
        deps=[dtp_tok, t_dtb], tag=f"{name}z")
    t2 = kb.op("act", lambda s: s.activation(out=dt_sb[:], in_=zt[:], func=AF.Abs),
               deps=[t1], tag=f"{name}abs")
    t3 = kb.op("act", lambda s: s.activation(out=dt_sb[:], in_=dt_sb[:], func=AF.Exp,
                                             scale=-1.0), deps=[t2], tag=f"{name}e")
    t4 = kb.op("dve", lambda v: v.tensor_single_scalar(out=dt_sb[:], in_=dt_sb[:],
                                                       scalar=1.0, op=ALU.add),
               deps=[t3], tag=f"{name}p1")
    t5 = kb.op("act", lambda s: s.activation(out=dt_sb[:], in_=dt_sb[:], func=AF.Ln),
               deps=[t4], tag=f"{name}ln1")
    t6 = kb.op("dve", lambda v: v.tensor_single_scalar(out=zt[:], in_=zt[:],
                                                       scalar=0.0, op=ALU.max),
               deps=[t5, t1], tag=f"{name}mx")
    t_dt = kb.op("dve", lambda v: v.tensor_tensor(out=dt_sb[:], in0=dt_sb[:], in1=zt[:],
                                                  op=ALU.add), deps=[t6], tag=f"{name}dt")
    t_da = kb.op("dve", lambda v: v.tensor_scalar(
        out=da_sb[:], in0=dt_sb[:], scalar1=a_sb[:], scalar2=None, op0=ALU.mult),
        deps=[t_dt, t_a], tag=f"{name}da")
    ones64 = e(nc.sbuf_tensor(f"{name}_o64", [64, 256], F32))
    t_o64 = kb.op("pool", lambda g: g.memset(ones64[:], 1.0))
    dacs_sb = e(nc.sbuf_tensor(f"{name}_dacs", [64, T], F32))
    dacs_toks = []
    prev = None
    for c in range(NC):
        t = kb.op("dve", lambda v, c=c: v.tensor_tensor_scan(
            out=dacs_sb[:, 256 * c:256 * c + 256], data0=da_sb[:, 256 * c:256 * c + 256],
            data1=ones64[:], initial=0.0, op0=ALU.add, op1=ALU.mult),
            deps=[t_da, t_o64] + ([prev] if prev else []), tag=f"{name}cs{c}")
        prev = t
        dacs_toks.append(t)
    dpr = e(nc.sbuf_tensor(f"{name}_dpr", [64, NC + 2], F32))
    lcs = dpr[:, NC:NC + 1]
    t_z = kb.op("pool", lambda g: g.memset(dpr[:, NC:NC + 2], 0.0))
    dp_toks = []
    prev = t_z
    for c in range(NC):
        ta_ = kb.op("act", lambda s, c=c: s.activation(out=dpr[:, c:c + 1], in_=lcs,
                                                       func=AF.Exp), deps=[prev], tag=f"{name}dp{c}")
        tb_ = kb.op("dve", lambda v, c=c: v.tensor_tensor(
            out=lcs, in0=lcs, in1=dacs_sb[:, 256 * c + 255:256 * c + 256], op=ALU.add),
            deps=[ta_, dacs_toks[c]], tag=f"{name}lc{c}")
        prev = tb_
        dp_toks.append(ta_)
    s_out2 = kb.dma_sem(f"{name}o2")
    t_dacs_d = kb.dma("sp", s_out2, lambda sy: sy.dma_start(out=dacs_dram[:], in_=dacs_sb[:]),
                      deps=dacs_toks, tag=f"{name}dacsd")
    t_dpr_d = kb.dma("sp", s_out2, lambda sy: sy.dma_start(out=dprod_dram[:], in_=dpr[:, :NC]),
                     deps=dp_toks + [prev], tag=f"{name}dprd")

    ncl = t64a
    tl1 = kb.op("act", lambda s: s.activation(out=ncl[:], in_=dt_sb[:], func=AF.Ln),
                deps=[t_dt, t_da] + dacs_toks, tag=f"{name}lndt")
    tl2 = kb.op("dve", lambda v: v.tensor_tensor(out=ncl[:], in0=ncl[:], in1=dacs_sb[:],
                                                 op=ALU.subtract),
                deps=[tl1] + dacs_toks, tag=f"{name}ncl")
    ncl_dram = nc.dram_tensor(f"{name}_ncld", [64, T], F32, kind="Internal").ap()
    t_ncld = kb.dma("sp", s_out2, lambda sy: sy.dma_start(out=ncl_dram[:], in_=ncl[:]),
                    deps=[tl2], tag=f"{name}ncld")

    s_all = e(nc.sbuf_tensor(f"{name}_S", [128, 64, 64], F32))
    s_bf = e(nc.sbuf_tensor(f"{name}_Sb", [128, 64, 64], BF16))
    t_s0 = kb.op("pool", lambda g: g.memset(s_all[:], 0.0))
    t_s0b = kb.op("pool", lambda g: g.memset(s_bf[:], 0.0), deps=[t_s0])
    cbts = e(nc.sbuf_tensor(f"{name}_cbts", [128, 2, 256], BF16))
    m_sb = e(nc.sbuf_tensor(f"{name}_m", [128, 2, 256], BF16))
    bdt = e(nc.sbuf_tensor(f"{name}_bd", [128, 2, 128], BF16))
    ytmp = e(nc.sbuf_tensor(f"{name}_yt", [128, 2, 64], F32))
    rowsa = e(nc.sbuf_tensor(f"{name}_ra", [1, 1024], F32))
    rowsb = e(nc.sbuf_tensor(f"{name}_rb", [1, 1024], F32))
    idneg = e(nc.sbuf_tensor(f"{name}_idn", [128, 128], BF16))
    ltc01 = e(nc.sbuf_tensor(f"{name}_ltc", [128, 128], BF16))
    s_tri2 = kb.dma_sem(f"{name}tri2")
    t_idn = kb.dma("sp", s_tri2, lambda sy: sy.dma_start(out=idneg[:], in_=idneg_dram[:]))
    t_ltc = kb.dma("sp", s_tri2, lambda sy: sy.dma_start(out=ltc01[:], in_=ltc_dram[:]))
    scl = e(nc.sbuf_tensor(f"{name}_scl", [128, 10, 64], F32))
    ones256 = e(nc.sbuf_tensor(f"{name}_o256", [1, 256], F32))
    t_o256 = kb.op("pool", lambda g: g.memset(ones256[:], 1.0))
    rows1 = e(nc.sbuf_tensor(f"{name}_r1", [1, 64], F32))
    s_xch = kb.dma_sem(f"{name}xch")
    s_y = kb.dma_sem(f"{name}y")
    s_ra = kb.dma_sem(f"{name}ra")
    s_rb2 = kb.dma_sem(f"{name}rb2")
    dep = [t_d] + consts.tok + [pool.rtok[3], pool.wtok[3]]
    tDb = kb.op("pe", lambda t_: t_.matmul(out=pool.db[3][:, :64], lhsT=consts.ones_row_f32[:],
                                           rhs=d_sb[:], start=True, stop=True),
                deps=dep, tag=f"{name}Dbc")
    pool.wtok[3] = tDb
    tDe = kb.op("act", lambda s: s.copy(out=scl[:, 5, :], in_=pool.db[3][:, :64]),
                deps=[tDb], tag=f"{name}Dev")
    pool.rtok[3] = tDe

    state = {"xch_rd": [], "xtk_rd": xbuf_rd[0] + xbuf_rd[1] + [xbuf_state[0][1], xbuf_state[1][1]],
             "scl_rd": [], "y_rd": list(halo_all), "r1_rd": [],
             "m_rd": [], "bd_rd": [], "yt_rd": [], "cbts_rd": [], "ra_rd": [],
             "rb_rd": [], "s_upd": None}
    y_dma_toks = []
    for c in range(NC):
        t_x = kb.dma("sp", s_xch, lambda sy, c=c: sy.dma_start(
            out=xbc_ch[:], in_=xbc_dram[:, 256 * c:256 * c + 256].rearrange(
                "(cc p) t -> p cc t", p=128)),
            deps=state["xch_rd"] + list(xbc_toks) + gate_toks,
            tag=f"{name}xchd{c}")
        state["xch_rd"] = []
        xtk_toks = {}
        prev_ev = None
        for cc in range(34):
            for sb in range(2):
                dep = [t_x] + consts.tok + [pool.rtok[0], pool.wtok[0]]
                tp = kb.op("pe", lambda t_, cc=cc, sb=sb: t_.transpose(
                    out=pool.db[0][:, :64].bitcast(BF16), in_=xbc_ch[:, cc, 128 * sb:128 * sb + 128],
                    identity=consts.ident_bf16[:]), deps=dep, tag=f"{name}xtp")
                pool.wtok[0] = tp
                state["xch_rd"].append(tp)
                te = kb.op("act", lambda s, cc=cc, sb=sb: s.copy(
                    out=xtk[:, sb, 128 * cc:128 * cc + 128],
                    in_=pool.db[0][:, :64].bitcast(BF16)),
                    deps=[tp] + state["xtk_rd"] + ([prev_ev] if prev_ev else []),
                    tag=f"{name}xte")
                state["xtk_rd"] = []
                pool.rtok[0] = te
                prev_ev = te
                xtk_toks[(cc, sb)] = te
        trans_toks = {}
        for (slot, srcten, srctok) in [(6, dacs_sb, dacs_toks[c]), (8, dt_sb, t_dt)]:
            for sb in range(2):
                dep = [srctok] + consts.tok + [pool.rtok[3], pool.wtok[3]]
                tp = kb.op("pe", lambda t_, c=c, sb=sb, srcten=srcten: t_.transpose(
                    out=pool.db[3][:, :64],
                    in_=srcten[:, 256 * c + 128 * sb:256 * c + 128 * sb + 128],
                    identity=consts.ident_f32[:64, :64]), deps=dep, tag=f"{name}tp{slot}")
                pool.wtok[3] = tp
                te = kb.op("act", lambda s, slot=slot, sb=sb: s.copy(
                    out=scl[:, slot + sb, :], in_=pool.db[3][:, :64]),
                    deps=[tp] + state["scl_rd"], tag=f"{name}te{slot}")
                state["scl_rd"] = []
                pool.rtok[3] = te
                trans_toks[(slot, sb)] = te
        ea_toks = []
        pe_ = None
        for sb in range(2):
            t = kb.op("act", lambda s, sb=sb: s.activation(
                out=scl[:, 0 + sb, :], in_=scl[:, 6 + sb, :], func=AF.Exp),
                deps=[trans_toks[(6, sb)]] + ([pe_] if pe_ else []), tag=f"{name}ea{sb}")
            pe_ = t
            ea_toks.append(t)
        dep = [dacs_toks[c]] + consts.tok + [pool.rtok[3], pool.wtok[3]]
        tpl = kb.op("pe", lambda t_, c=c: t_.transpose(
            out=pool.db[3][0:1, :64], in_=dacs_sb[:, 256 * c + 255:256 * c + 256],
            identity=consts.ident_f32[:64, :64]), deps=dep, tag=f"{name}lrtp")
        pool.wtok[3] = tpl
        tle = kb.op("act", lambda s: s.copy(out=rows1[:], in_=pool.db[3][0:1, :64]),
                    deps=[tpl] + state["r1_rd"], tag=f"{name}lre")
        state["r1_rd"] = []
        pool.rtok[3] = tle
        dep = [tle] + consts.tok + [pool.rtok[3], pool.wtok[3]]
        tlb = kb.op("pe", lambda t_: t_.matmul(
            out=pool.db[3][:, 512:576], lhsT=consts.ones_row_f32[:], rhs=rows1[:],
            start=True, stop=True), deps=dep, tag=f"{name}lbb")
        pool.wtok[3] = tlb
        dte_t = []
        for sb in range(2):
            t1_ = kb.op("dve", lambda v, sb=sb: v.tensor_tensor(
                out=scl[:, 2 + sb, :], in0=pool.db[3][:, 512:576], in1=scl[:, 6 + sb, :],
                op=ALU.subtract), deps=[tlb, trans_toks[(6, sb)]], tag=f"{name}dts{sb}")
            t2_ = kb.op("act", lambda s, sb=sb: s.activation(
                out=scl[:, 2 + sb, :], in_=scl[:, 2 + sb, :], func=AF.Exp),
                deps=[t1_, pe_], tag=f"{name}dte{sb}")
            t3_ = kb.op("dve", lambda v, sb=sb: v.tensor_tensor(
                out=scl[:, 2 + sb, :], in0=scl[:, 2 + sb, :], in1=scl[:, 8 + sb, :],
                op=ALU.mult), deps=[t2_, trans_toks[(8, sb)]], tag=f"{name}dtm{sb}")
            dte_t.append(t3_)
        pool.rtok[3] = dte_t[0]
        tce = kb.op("act", lambda s: s.activation(out=rows1[:], in_=rows1[:], func=AF.Exp),
                    deps=[tle, tlb], tag=f"{name}cde")
        state["r1_rd"].append(tce)
        dep = [tce] + consts.tok + [pool.rtok[3], pool.wtok[3]]
        tcb2 = kb.op("pe", lambda t_: t_.matmul(
            out=pool.db[3][:, 512:576], lhsT=consts.ones_row_f32[:], rhs=rows1[:],
            start=True, stop=True), deps=dep + dte_t, tag=f"{name}cdb")
        pool.wtok[3] = tcb2
        state["r1_rd"].append(tcb2)
        tc9 = kb.op("act", lambda s: s.copy(out=scl[:, 4, :], in_=pool.db[3][:, 512:576]),
                    deps=[tcb2], tag=f"{name}cd9")
        pool.rtok[3] = tc9
        cbt_mm = []
        for sb in range(2):
            dep = [t_x] + [pool.rtok[0], pool.wtok[0]]
            t = kb.op("pe", lambda t_, sb=sb: t_.matmul(
                out=pool.db[0][:, 512 * sb:512 * sb + 256],
                lhsT=xbc_ch[:, 32, 128 * sb:128 * sb + 128],
                rhs=xbc_ch[:, 33, :], start=True, stop=True), deps=dep, tag=f"{name}cbt{sb}")
            pool.wtok[0] = t
            cbt_mm.append(t)
        cbe = []
        for sb in range(2):
            t = kb.op("act", lambda s, sb=sb: s.copy(
                out=cbts[:, sb, :], in_=pool.db[0][:, 512 * sb:512 * sb + 256]),
                deps=[cbt_mm[sb]] + state["cbts_rd"], tag=f"{name}cbe{sb}")
            state["cbts_rd"] = []
            cbe.append(t)
        pool.rtok[0] = cbe[-1]

        if c == 0:
            sup = [t_s0b] * 64
        else:
            sup = state["s_upd"]
        new_sup = [None] * 64
        rd = state.setdefault("reg_rd", {k: [] for k in
                                         ["db1r0", "db1r1", "db2r0", "db2r1",
                                          "db3v0", "db3v1", "db3w"]})
        grp_toks = [None, None]
        y_wr_local = []
        for h in range(64):
            j2 = h % 4
            if j2 == 0:
                g0 = h
                ta_ = kb.dma("sp", s_ra, lambda sy, c=c, g0=g0: sy.dma_start(
                    out=rowsa[:].rearrange("o (h t) -> o h t", t=256),
                    in_=dacs_dram[g0:g0 + 4, 256 * c:256 * c + 256].unsqueeze(0)),
                    deps=state["ra_rd"] + [t_dacs_d], tag=f"{name}rga")
                tb2_ = kb.dma("sp", s_rb2, lambda sy, c=c, g0=g0: sy.dma_start(
                    out=rowsb[:].rearrange("o (h t) -> o h t", t=256),
                    in_=ncl_dram[g0:g0 + 4, 256 * c:256 * c + 256].unsqueeze(0)),
                    deps=state["rb_rd"] + [t_ncld], tag=f"{name}rgb")
                state["ra_rd"] = []
                state["rb_rd"] = []
                grp_toks = [ta_, tb2_]
            xh0 = 64 * (h % 2)
            cc_h = h // 2
            xh_t0 = xtk_toks[(cc_h, 0)]
            xh_t1 = xtk_toks[(cc_h, 1)]
            dep0 = [grp_toks[0], grp_toks[1], t_idn, t_ltc, t_o256] + consts.tok + rd["db1r0"]
            rd["db1r0"] = []
            e1 = kb.op("pe", lambda t_, j2=j2: t_.matmul(
                out=pool.db[1][:, 0:256], lhsT=consts.ones_row_f32[:],
                rhs=rowsa[:, 256 * j2:256 * j2 + 256], start=True, stop=False),
                deps=dep0, tag=f"{name}e1")
            e2 = kb.op("pe", lambda t_, j2=j2: t_.matmul(
                out=pool.db[1][:, 0:256], lhsT=rowsb[:, 256 * j2:256 * j2 + 128],
                rhs=ones256[:], start=False, stop=False), deps=dep0, tag=f"{name}e2")
            e3 = kb.op("pe", lambda t_: t_.matmul(
                out=pool.db[1][:, 0:128], lhsT=idneg[:], rhs=ltc01[:],
                start=False, stop=True), deps=dep0, tag=f"{name}e3")
            dep1 = [grp_toks[0], grp_toks[1], t_o256] + rd["db1r1"]
            rd["db1r1"] = []
            f1 = kb.op("pe", lambda t_, j2=j2: t_.matmul(
                out=pool.db[1][:, 512:640], lhsT=consts.ones_row_f32[:],
                rhs=rowsa[:, 256 * j2 + 128:256 * j2 + 256], start=True, stop=False),
                deps=dep1, tag=f"{name}f1")
            f2 = kb.op("pe", lambda t_, j2=j2: t_.matmul(
                out=pool.db[1][:, 512:640], lhsT=rowsb[:, 256 * j2 + 128:256 * j2 + 256],
                rhs=ones256[:, 0:128], start=False, stop=False), deps=dep1, tag=f"{name}f2")
            f3 = kb.op("pe", lambda t_: t_.matmul(
                out=pool.db[1][:, 512:640], lhsT=idneg[:], rhs=ltc01[:],
                start=False, stop=True), deps=dep1, tag=f"{name}f3")
            state["ra_rd"] += [e1, f1]
            state["rb_rd"] += [e2, f2]
            x1_ = kb.op("act", lambda s: s.activation(
                out=m_sb[:, 0, :], in_=pool.db[1][:, 0:256], func=AF.Exp),
                deps=[e1, e2, e3] + state["m_rd"], tag=f"{name}xm0")
            state["m_rd"] = []
            x2_ = kb.op("act", lambda s: s.activation(
                out=m_sb[:, 1, 128:256], in_=pool.db[1][:, 512:640], func=AF.Exp),
                deps=[f1, f2, f3, x1_], tag=f"{name}xm1")
            rd["db1r0"].append(x1_)
            rd["db1r1"].append(x2_)
            p1_ = kb.op("dve", lambda g: g.tensor_tensor(
                out=m_sb[:, 0, :], in0=m_sb[:, 0, :], in1=cbts[:, 0, :], op=ALU.mult),
                deps=[x1_] + cbe, tag=f"{name}mc0")
            p2_ = kb.op("dve", lambda g: g.tensor_tensor(
                out=m_sb[:, 1, 128:256], in0=m_sb[:, 1, 128:256], in1=cbts[:, 1, 128:256],
                op=ALU.mult), deps=[x2_, p1_] + cbe, tag=f"{name}mc1")
            b1_ = kb.op("dve", lambda v, h=h: v.tensor_scalar(
                out=bdt[:, 0, :], in0=xtk[:, 0, 4096:4224], scalar1=scl[:, 2, h:h + 1],
                scalar2=None, op0=ALU.mult), deps=[xtk_toks[(32, 0)], dte_t[0], p2_] +
                state["bd_rd"], tag=f"{name}bd0")
            state["bd_rd"] = []
            b2_ = kb.op("dve", lambda v, h=h: v.tensor_scalar(
                out=bdt[:, 1, :], in0=xtk[:, 1, 4096:4224], scalar1=scl[:, 3, h:h + 1],
                scalar2=None, op0=ALU.mult), deps=[xtk_toks[(32, 1)], dte_t[1], b1_],
                tag=f"{name}bd1")
            u0 = kb.op("pe", lambda t_, h=h, xh0=xh0, cc_h=cc_h: t_.matmul(
                out=pool.db[2][:, 0:64], lhsT=m_sb[:, 0, 0:128],
                rhs=xtk[:, 0, 128 * cc_h + xh0:128 * cc_h + xh0 + 64],
                start=True, stop=True), deps=[p1_, xh_t0] + rd["db2r0"], tag=f"{name}u0")
            rd["db2r0"] = []
            u1a = kb.op("pe", lambda t_, h=h, xh0=xh0, cc_h=cc_h: t_.matmul(
                out=pool.db[2][:, 512:576], lhsT=m_sb[:, 0, 128:256],
                rhs=xtk[:, 0, 128 * cc_h + xh0:128 * cc_h + xh0 + 64],
                start=True, stop=False), deps=[p1_, xh_t0] + rd["db2r1"], tag=f"{name}u1a")
            rd["db2r1"] = []
            u1b = kb.op("pe", lambda t_, h=h, xh0=xh0, cc_h=cc_h: t_.matmul(
                out=pool.db[2][:, 512:576], lhsT=m_sb[:, 1, 128:256],
                rhs=xtk[:, 1, 128 * cc_h + xh0:128 * cc_h + xh0 + 64],
                start=False, stop=True), deps=[p2_, xh_t1], tag=f"{name}u1b")
            state["m_rd"] += [u0, u1a, u1b]
            v0 = kb.op("pe", lambda t_, h=h: t_.matmul(
                out=pool.db[3][:, 0:64], lhsT=xbc_ch[:, 33, 0:128],
                rhs=s_bf[:, h, :], start=True, stop=True),
                deps=[t_x, sup[h]] + rd["db3v0"], tag=f"{name}v0")
            rd["db3v0"] = []
            v1 = kb.op("pe", lambda t_, h=h: t_.matmul(
                out=pool.db[3][:, 64:128], lhsT=xbc_ch[:, 33, 128:256],
                rhs=s_bf[:, h, :], start=True, stop=True),
                deps=[t_x, sup[h]] + rd["db3v1"], tag=f"{name}v1")
            rd["db3v1"] = []
            w1 = kb.op("pe", lambda t_, h=h, xh0=xh0, cc_h=cc_h: t_.matmul(
                out=pool.db[3][:, 512:576], lhsT=bdt[:, 0, :],
                rhs=xtk[:, 0, 128 * cc_h + xh0:128 * cc_h + xh0 + 64],
                start=True, stop=False), deps=[b1_, xh_t0] + rd["db3w"], tag=f"{name}w1")
            rd["db3w"] = []
            w2 = kb.op("pe", lambda t_, h=h, xh0=xh0, cc_h=cc_h: t_.matmul(
                out=pool.db[3][:, 512:576], lhsT=bdt[:, 1, :],
                rhs=xtk[:, 1, 128 * cc_h + xh0:128 * cc_h + xh0 + 64],
                start=False, stop=True), deps=[b2_, xh_t1], tag=f"{name}w2")
            state["bd_rd"] = [w1, w2]
            asm = []
            for qb, (vmm, offv, umm) in enumerate([(v0, 0, u0), (v1, 64, u1b)]):
                a1_ = kb.op("act", lambda s, h=h, qb=qb, offv=offv: s.activation(
                    out=ytmp[:, qb, :], in_=pool.db[3][:, offv:offv + 64], func=AF.Copy,
                    scale=scl[:, 0 + qb, h:h + 1]), deps=[vmm, ea_toks[qb]] +
                    state["yt_rd"], tag=f"{name}as{qb}a")
                state["yt_rd"] = []
                a2_ = kb.op("dve", lambda v, h=h, qb=qb: v.tensor_tensor(
                    out=ytmp[:, qb, :], in0=ytmp[:, qb, :],
                    in1=pool.db[2][:, 512 * qb:512 * qb + 64], op=ALU.add),
                    deps=[a1_, umm], tag=f"{name}as{qb}b")
                a3_ = kb.op("dve", lambda v, h=h, qb=qb, xh0=xh0, cc_h=cc_h: v.scalar_tensor_tensor(
                    out=y_sb[:, qb, 64 * h:64 * h + 64],
                    in0=xtk[:, qb, 128 * cc_h + xh0:128 * cc_h + xh0 + 64],
                    scalar=scl[:, 5, h:h + 1], in1=ytmp[:, qb, :], op0=ALU.mult,
                    op1=ALU.add), deps=[a2_, xh_t0 if qb == 0 else xh_t1, tDe] +
                    state["y_rd"], tag=f"{name}as{qb}c")
                state["y_rd"] = []
                asm.append((a1_, a2_, a3_))
                state["yt_rd"] = [a3_]
                y_wr_local.append(a3_)
            rd["db3v0"].append(asm[0][0])
            rd["db3v1"].append(asm[1][0])
            rd["db2r0"].append(asm[0][1])
            rd["db2r1"].append(asm[1][1])
            su0 = kb.op("dve", lambda v, h=h: v.scalar_tensor_tensor(
                out=s_all[:, h, :], in0=s_all[:, h, :], scalar=scl[:, 4, h:h + 1],
                in1=pool.db[3][:, 512:576], op0=ALU.mult, op1=ALU.add),
                deps=[w2, v0, v1, tc9, sup[h]], tag=f"{name}scan")
            su = kb.op("dve", lambda v, h=h: v.tensor_copy(
                out=s_bf[:, h, :], in_=s_all[:, h, :]), deps=[su0], tag=f"{name}scanb")
            rd["db3w"].append(su0)
            new_sup[h] = su
        state["s_upd"] = new_sup
        state["scl_rd"] = [asm[0][0], asm[1][0], su0, b1_, b2_]
        state["xtk_rd"] = [w2, asm[1][2]]
        state["xch_rd"].extend([v1, w2, su0])
        state["cbts_rd"] = [p1_, p2_]
        pool.rtok[1] = x2_
        pool.rtok[2] = asm[1][1]
        pool.rtok[3] = su0
        pool.wtok[1] = f3
        pool.wtok[2] = u1b
        pool.wtok[3] = w2
        tdy = kb.dma("sp", s_y, lambda sy, c=c: sy.dma_start(
            out=y_dram[256 * c:256 * c + 256, :].rearrange("(qb p) f -> p qb f", p=128),
            in_=y_sb[:]), deps=y_wr_local, tag=f"{name}yd{c}")
        state["y_rd"] = [tdy]
        y_dma_toks.append(tdy)

    if sout_shared is None:
        t_sd = kb.dma("sp", s_out2, lambda sy: sy.dma_start(
            out=sout_dram[:], in_=s_all[:].rearrange("p h q -> p (h q)")),
            deps=state["s_upd"], tag=f"{name}soutd")
    else:
        t_sd = kb.dma("sp", s_out2, lambda sy: sy.dma_start(
            out=sout_shared[bass.ds(sy.partition_id() % 2 * 128, 128), :],
            in_=s_all[:].rearrange("p h q -> p (h q)")),
            deps=state["s_upd"], tag=f"{name}soutd")
    return y_dma_toks + gate_toks + list(xbc_toks) + [t_dacs_d, t_dpr_d, t_sd]


def emit_l0attn_a2(kb, nc, ctx, consts, pool, x_dram, y_dram, gate_dram, xbc_dram,
                   dacs_dram, dprod_dram, sin_dram, woT_dram, nw_dram, out_dram,
                   T=1024, name="mb", sin_shared=None, sel_dram=None, sin_deps=()):
    e = ctx.enter_context
    C = 16
    NC = T // 256
    NT = cdiv(T, 512)
    x3 = x_dram.rearrange("(c p) t -> c p t", p=128)
    o3 = out_dram.rearrange("(c p) t -> c p t", p=128)

    sin_sb = e(nc.sbuf_tensor(f"{name}_sin", [128, 64, 64], BF16))
    dacs_sb = e(nc.sbuf_tensor(f"{name}_dacs", [64, T], F32))
    dpr_sb = e(nc.sbuf_tensor(f"{name}_dpr", [64, NC], F32))
    nw_sb = e(nc.sbuf_tensor(f"{name}_nw", [128, 32], F32))
    s_in = kb.dma_sem(f"{name}in")
    s_in2 = kb.dma_sem(f"{name}in2")
    if sin_shared is None:
        t_sin = kb.dma("pool", s_in2, lambda g: g.dma_start(
            out=sin_sb[:], in_=sin_dram.rearrange("p (h q) -> p h q", q=64)))
    else:
        e2 = ctx.enter_context
        sin_f32 = e2(nc.sbuf_tensor(f"{name}_sinf", [128, 4096], F32))
        sel_sb = e2(nc.sbuf_tensor(f"{name}_sel", [128, 1], F32))
        t_sf = kb.dma("sp", s_in2, lambda sy: sy.dma_start(
            out=sin_f32[:],
            in_=sin_shared[bass.ds((1 - sy.partition_id() % 2) * 128, 128), :]),
            deps=list(sin_deps), tag=f"{name}sinsh")
        t_se = kb.dma("sp", s_in2, lambda sy: sy.dma_start(out=sel_sb[:], in_=sel_dram[:]))
        t_sin = kb.op("dve", lambda v: v.tensor_scalar(
            out=sin_sb[:].rearrange("p h q -> p (h q)"), in0=sin_f32[:],
            scalar1=sel_sb[:], scalar2=None, op0=ALU.mult), deps=[t_sf, t_se],
            tag=f"{name}sinsel")
    t_dacs = kb.dma("sp", s_in, lambda sy: sy.dma_start(out=dacs_sb[:], in_=dacs_dram[:]))
    t_dpr = kb.dma("sp", s_in, lambda sy: sy.dma_start(out=dpr_sb[:], in_=dprod_dram[:]))
    t_nw = kb.dma("sp", s_in, lambda sy: sy.dma_start(out=nw_sb[:], in_=nw_dram[:]))

    ybuf = e(nc.sbuf_tensor(f"{name}_y", [128, 2, 4096], BF16))
    gbuf = e(nc.sbuf_tensor(f"{name}_g", [128, 2, 4096], BF16))
    cf = e(nc.sbuf_tensor(f"{name}_cf", [128, 256], BF16))
    scl = e(nc.sbuf_tensor(f"{name}_scl", [128, 4, 64], F32))
    ytmp = e(nc.sbuf_tensor(f"{name}_yt", [128, 64], F32))
    col = e(nc.sbuf_tensor(f"{name}_col", [128, 4], F32))
    rows1 = e(nc.sbuf_tensor(f"{name}_r1", [1, 64], F32))
    gnT = e(nc.sbuf_tensor(f"{name}_gnT", [128, 32, T], BF16))
    s_y = kb.dma_sem(f"{name}y")
    s_g2 = kb.dma_sem(f"{name}g2")
    s_cf = kb.dma_sem(f"{name}cf")
    st = {"y_rd": [], "g_rd": [], "cf_rd": [], "scl_rd": [], "r1_rd": [], "gnT_rd": [],
          "yt_rd": [], "col_rd": []}
    gnT_toks = {}
    for c in range(NC):
        t_y = kb.dma("sp", s_y, lambda sy, c=c: sy.dma_start(
            out=ybuf[:], in_=y_dram[256 * c:256 * c + 256, :].rearrange(
                "(qb p) f -> p qb f", p=128)), deps=st["y_rd"], tag=f"{name}yd{c}")
        st["y_rd"] = []
        t_cf = kb.dma("sp", s_cf, lambda sy, c=c: sy.dma_start(
            out=cf[:], in_=xbc_dram[4224:4352, 256 * c:256 * c + 256]),
            deps=st["cf_rd"], tag=f"{name}cfd{c}")
        st["cf_rd"] = []
        ea_toks = []
        for sb in range(2):
            dep = [t_dacs] + consts.tok + [pool.rtok[3], pool.wtok[3]]
            tp = kb.op("pe", lambda t_, c=c, sb=sb: t_.transpose(
                out=pool.db[3][:, :64], in_=dacs_sb[:, 256 * c + 128 * sb:256 * c + 128 * sb + 128],
                identity=consts.ident_f32[:64, :64]), deps=dep, tag=f"{name}dctp")
            pool.wtok[3] = tp
            te = kb.op("act", lambda s, sb=sb: s.activation(
                out=scl[:, sb, :], in_=pool.db[3][:, :64], func=AF.Exp),
                deps=[tp] + st["scl_rd"], tag=f"{name}ea{sb}")
            st["scl_rd"] = []
            pool.rtok[3] = te
            ea_toks.append(te)
        dep = [t_dpr] + consts.tok + [pool.rtok[3], pool.wtok[3]]
        tpd = kb.op("pe", lambda t_, c=c: t_.transpose(
            out=pool.db[3][0:1, :64], in_=dpr_sb[:, c:c + 1],
            identity=consts.ident_f32[:64, :64]), deps=dep, tag=f"{name}dptp")
        pool.wtok[3] = tpd
        tde = kb.op("act", lambda s: s.copy(out=rows1[:], in_=pool.db[3][0:1, :64]),
                    deps=[tpd] + st["r1_rd"], tag=f"{name}dpe")
        pool.rtok[3] = tde
        st["r1_rd"] = []
        dep = [tde] + consts.tok + [pool.rtok[3], pool.wtok[3]]
        tbc = kb.op("pe", lambda t_: t_.matmul(
            out=pool.db[3][:, 512:576], lhsT=consts.ones_row_f32[:], rhs=rows1[:],
            start=True, stop=True), deps=dep, tag=f"{name}dpb")
        pool.wtok[3] = tbc
        tdp = kb.op("act", lambda s: s.copy(out=scl[:, 2, :], in_=pool.db[3][:, 512:576]),
                    deps=[tbc], tag=f"{name}dpe2")
        pool.rtok[3] = tdp
        st["r1_rd"] = [tbc]
        corr_rd = []
        y_wr = []
        prev_a = []
        for h in range(64):
            for qb in range(2):
                dep = [t_cf, t_sin] + corr_rd
                corr_rd = []
                mm = kb.op("pe", lambda t_, h=h, qb=qb: t_.matmul(
                    out=pool.db[1][:, 0:64], lhsT=cf[:, 128 * qb:128 * qb + 128],
                    rhs=sin_sb[:, h, :], start=True, stop=True), deps=dep, tag=f"{name}cmm")
                a1_ = kb.op("dve", lambda v, h=h, qb=qb: v.tensor_scalar(
                    out=ytmp[:], in0=pool.db[1][:, 0:64], scalar1=scl[:, qb, h:h + 1],
                    scalar2=None, op0=ALU.mult), deps=[mm, ea_toks[qb]] + st["yt_rd"],
                    tag=f"{name}ca1")
                st["yt_rd"] = []
                corr_rd = [a1_]
                a2_ = kb.op("dve", lambda v, h=h, qb=qb: v.scalar_tensor_tensor(
                    out=ybuf[:, qb, 64 * h:64 * h + 64], in0=ytmp[:], scalar=scl[:, 2, h:h + 1],
                    in1=ybuf[:, qb, 64 * h:64 * h + 64], op0=ALU.mult, op1=ALU.add),
                    deps=[a1_, tdp, t_y] + prev_a, tag=f"{name}ca2")
                prev_a = []
                st["yt_rd"] = [a2_]
                y_wr.append(a2_)
        st["scl_rd"] = [a1_, a2_]
        pool.wtok[1] = mm
        pool.rtok[1] = a1_
        t_g = kb.dma("sp", s_g2, lambda sy, c=c: sy.dma_start(
            out=gbuf[:], in_=gate_dram[256 * c:256 * c + 256, :].rearrange(
                "(qb p) f -> p qb f", p=128)), deps=st["g_rd"], tag=f"{name}gd{c}")
        st["g_rd"] = []
        gn_toks = []
        prev_sq = None
        for qb in range(2):
            tsi = kb.op("act", lambda s, qb=qb: s.activation(
                out=gbuf[:, qb, :], in_=gbuf[:, qb, :], func=AF.Silu),
                deps=[t_g] + ([prev_sq] if prev_sq else []), tag=f"{name}si{qb}")
            tg1 = kb.op("dve", lambda v, qb=qb: v.tensor_tensor(
                out=gbuf[:, qb, :], in0=ybuf[:, qb, :], in1=gbuf[:, qb, :], op=ALU.mult),
                deps=[tsi] + y_wr, tag=f"{name}gm{qb}")
            tsq = kb.op("act", lambda s, qb=qb: s.activation(
                out=ybuf[:, qb, :], in_=gbuf[:, qb, :], func=AF.Square,
                accum_out=col[:, 0:1]), deps=[tg1] + st["col_rd"], tag=f"{name}sq{qb}")
            st["y_rd"].append(tsq)
            tms = kb.op("dve", lambda v: v.tensor_scalar(
                out=col[:, 1:2], in0=col[:, 0:1], scalar1=1.0 / 4096.0, scalar2=1e-5,
                op0=ALU.mult, op1=ALU.add), deps=[tsq], tag=f"{name}ms{qb}")
            tln = kb.op("act", lambda s: s.activation(out=col[:, 2:3], in_=col[:, 1:2],
                                                      func=AF.Ln), deps=[tms], tag=f"{name}l{qb}")
            tex = kb.op("act", lambda s: s.activation(out=col[:, 3:4], in_=col[:, 2:3],
                                                      func=AF.Exp, scale=-0.5),
                        deps=[tln], tag=f"{name}e{qb}")
            tgn = kb.op("dve", lambda v, qb=qb: v.tensor_scalar(
                out=gbuf[:, qb, :], in0=gbuf[:, qb, :], scalar1=col[:, 3:4],
                scalar2=None, op0=ALU.mult), deps=[tex, tg1], tag=f"{name}gn{qb}")
            st["col_rd"] = [tgn]
            prev_sq = tgn
            gn_toks.append(tgn)
        st["g_rd"] = []
        prev_ev = None
        for qb in range(2):
            for ic in range(32):
                dep = [gn_toks[qb]] + consts.tok + [pool.rtok[0], pool.wtok[0]]
                tp = kb.op("pe", lambda t_, qb=qb, ic=ic: t_.transpose(
                    out=pool.db[0][:, :64].bitcast(BF16), in_=gbuf[:, qb, 128 * ic:128 * ic + 128],
                    identity=consts.ident_bf16[:]), deps=dep, tag=f"{name}gtp")
                pool.wtok[0] = tp
                st["g_rd"].append(tp)
                te = kb.op("dve", lambda v, c=c, qb=qb, ic=ic: v.tensor_scalar(
                    out=gnT[:, ic, 256 * c + 128 * qb:256 * c + 128 * qb + 128],
                    in0=pool.db[0][:, :64].bitcast(BF16), scalar1=nw_sb[:, ic:ic + 1],
                    scalar2=None, op0=ALU.mult),
                    deps=[tp, t_nw] + st["gnT_rd"] + ([prev_ev] if prev_ev else []),
                    tag=f"{name}gte")
                st["gnT_rd"] = []
                pool.rtok[0] = te
                prev_ev = te
                gnT_toks[(c, ic)] = te

    wst = StripStream(kb, nc, ctx, f"{name}ws", [128, 512], BF16, 2, "pool")
    xbuf = [e(nc.sbuf_tensor(f"{name}_xb{i}", [128, T], F32)) for i in range(2)]
    s_xb = [kb.dma_sem(f"{name}xb{i}") for i in range(2)]
    obuf = e(nc.sbuf_tensor(f"{name}_ob", [128, T], F32))
    s_ob = kb.dma_sem(f"{name}ob")
    out_toks = []
    xb_rd = [[], []]
    ob_rd = None
    gnT_deps = list(gnT_toks.values())
    for og in range(4):
        mm_last = [None] * 4
        for ic in range(32):
            buf, sl, td = wst.load(lambda ic=ic, og=og: woT_dram[128 * ic:128 * ic + 128,
                                                                512 * og:512 * og + 512],
                                   tag=f"{name}ost")
            mms = []
            for b in range(4):
                for i in range(NT):
                    n = min(512, T - 512 * i)
                    dep = [td] + ([gnT_toks[(cc, ic)] for cc in range(NC)] if ic >= 0 else [])
                    if ic == 0 and pool.rtok[b] is not None:
                        dep += [pool.rtok[b], pool.wtok[b]]
                    t = kb.op("pe", lambda t_, ic=ic, b=b, i=i, n=n, buf=buf: t_.matmul(
                        out=pool.db[b][:, 512 * i:512 * i + n],
                        lhsT=buf[:, 128 * b:128 * b + 128],
                        rhs=gnT[:, ic, 512 * i:512 * i + n],
                        start=(ic == 0), stop=(ic == 31)), deps=dep, tag=f"{name}omm")
                    mms.append(t)
                    if ic == 31:
                        mm_last[b] = t
            wst.read_by(sl, [mms[-1]])
        for b in range(4):
            cc2 = 4 * og + b
            pool.wtok[b] = mm_last[b]
            sl = cc2 % 2
            tx = kb.dma("sp", s_xb[sl], lambda sy, cc2=cc2, sl=sl: sy.dma_start(
                out=xbuf[sl][:], in_=x3[cc2, :, :]), deps=xb_rd[sl], tag=f"{name}xd{cc2}")
            xb_rd[sl] = []
            t2 = kb.op("dve", lambda v, b=b, sl=sl: v.tensor_tensor(
                out=obuf[:], in0=pool.db[b][:, :T], in1=xbuf[sl][:], op=ALU.add),
                deps=[mm_last[b], tx] + ([ob_rd] if ob_rd else []), tag=f"{name}fin{cc2}")
            xb_rd[sl].append(t2)
            pool.rtok[b] = t2
            td2 = kb.dma("sp", s_ob, lambda g, cc2=cc2: g.dma_start(
                out=o3[cc2, :, :], in_=obuf[:]), deps=[t2], tag=f"{name}odma{cc2}")
            ob_rd = td2
            out_toks.append(td2)
    return out_toks


class FlagSync:

    def __init__(self, kb, nc, ctx, flg_dram, nonce_dram, nsync):
        e = ctx.enter_context
        self.kb, self.nc = kb, nc
        self._parc = {}
        self.flg = flg_dram
        self.nsync = nsync
        self.non_sb = e(nc.sbuf_tensor("fs_non", [1, nsync * 16], I32))
        self.fsb = [e(nc.sbuf_tensor(f"fs_f{i}", [1, 16], I32)) for i in range(nsync)]
        self.s_n = kb.dma_sem("fsn")
        self.t_n = kb.dma("sp", self.s_n, lambda sy: sy.dma_start(
            out=self.non_sb[:], in_=nonce_dram[:]), detach=True)
        self.s_w = kb.dma_sem("fsw")
        self.psem = [e(nc.semaphore(f"fspoll{i}")) for i in range(nsync)]

    def _par(self, sy):
        if "v" not in self._parc:
            self._parc["v"] = sy.partition_id() % 2
        return self._parc["v"]

    def write(self, s_idx, deps):
        return self.kb.dma("sp", self.s_w, lambda sy, s_idx=s_idx: sy.dma_start(
            out=self.flg[bass.ds(self._par(sy), 1), 16 * s_idx:16 * s_idx + 16],
            in_=self.non_sb[:, 16 * s_idx:16 * s_idx + 16]),
            deps=list(deps) + [self.t_n], tag=f"flgw{s_idx}", detach=True)

    def wait(self, s_idx, deps=()):
        psem = self.psem[s_idx]
        fsb = self.fsb[s_idx]

        def fn(s):
            par = s.partition_id() % 2
            nslot = 1 - par
            cnt = s.alloc_register(f"plc{s_idx}")
            fval = s.alloc_register(f"plf{s_idx}")
            nval = s.alloc_register(f"pln{s_idx}")
            cond = s.alloc_register(f"plq{s_idx}")
            s.reg_load(nval, self.non_sb[0:1, 16 * s_idx:16 * s_idx + 1])
            s.reg_mov(cnt, 0)
            s.reg_mov(fval, 0)
            s.reg_alu(cond, fval, nval, ALU.not_equal)
            with s.While(cond):
                s.dma_start(out=fsb[:], in_=self.flg[
                    bass.ds(nslot, 1), 16 * s_idx:16 * s_idx + 16]).then_inc(psem, 16)
                s.reg_add(cnt, cnt, 16)
                s.wait_ge(psem, cnt)
                s.reg_load(fval, fsb[0:1, 0:1])
                s.reg_alu(cond, fval, nval, ALU.not_equal)
            return s.engine_nop() if hasattr(s, "engine_nop") else s.nop()

        return self.kb.op("act", fn, deps=list(deps) + [self.t_n], tag=f"flgp{s_idx}",
                          detach=True)


def dequant_scratch(nc, ctx, name="dqs"):
    e = ctx.enter_context
    WC = 128
    return {"tm1": e(nc.sbuf_tensor(f"{name}_t1", [128, WC], I32)),
            "tm2": e(nc.sbuf_tensor(f"{name}_t2", [128, WC], I32)),
            "tu": e(nc.sbuf_tensor(f"{name}_tu", [128, WC], F32)),
            "tq": e(nc.sbuf_tensor(f"{name}_tq", [128, WC], F32)),
            "tsg": e(nc.sbuf_tensor(f"{name}_ts", [128, WC], I32)),
            "val": e(nc.sbuf_tensor(f"{name}_val", [128, 8 * WC], F32)),
            "vb": e(nc.sbuf_tensor(f"{name}_vb", [128, 8 * WC], BF16)),
            "vb_rd": [], "val_rd": []}


def emit_dequant_nat(kb, nc, ctx, pk_dram, scT_dram, out_sh, O_half, W,
                     name, sliced=False, detach=False, scratch=None):
    e = ctx.enter_context
    DT = detach
    IN = 8 * W
    G = IN // 128
    n_ot = O_half // 128
    WC = 128
    nwc = cdiv(W, WC)
    out_loc = nc.dram_tensor(f"{name}_loc", [O_half, IN], BF16, kind="Internal").ap()

    pk = [e(nc.sbuf_tensor(f"{name}_pk{i}", [128, W], I32)) for i in range(2)]
    s_pk = [kb.dma_sem(f"{name}pk{i}") for i in range(2)]
    sct = [e(nc.sbuf_tensor(f"{name}_sct{i}", [128, G], F32)) for i in range(2)]
    s_sct = [kb.dma_sem(f"{name}sct{i}") for i in range(2)]
    if scratch is None:
        scratch = dequant_scratch(nc, ctx, name + "s")
    tm1, tm2, tu, tq, tsg = (scratch[k] for k in ("tm1", "tm2", "tu", "tq", "tsg"))
    val, vb = scratch["val"], scratch["vb"]
    s_out = kb.dma_sem(f"{name}out")

    st = {"pk_tok": [[], []], "sct_rd": [[], []]}
    row_toks = [[] for _ in range(n_ot)]

    def emit_ot(ot):
        sl = ot % 2
        td = kb.dma("sp", s_pk[sl], lambda sy, ot=ot, sl=sl: sy.dma_start(
            out=pk[sl][:], in_=pk_dram[128 * ot:128 * ot + 128, :]),
            deps=st["pk_tok"][sl], tag=f"{name}pkd{ot}", detach=DT)
        st["pk_tok"][sl] = []
        tsc = kb.dma("pool", s_sct[sl], lambda g, ot=ot, sl=sl: g.dma_start(
            out=sct[sl][:], in_=scT_dram[128 * ot:128 * ot + 128, :]),
            deps=st["sct_rd"][sl], tag=f"{name}sctd{ot}", detach=DT)
        st["sct_rd"][sl] = []
        val_rd = scratch["val_rd"]
        vb_rd = scratch["vb_rd"]
        for wc in range(nwc):
            w0 = WC * wc
            wn = min(WC, W - w0)
            plane_toks = []
            pv = vb_rd + val_rd
            val_rd = []
            for j in range(8):
                t1 = kb.op("dve", lambda v, j=j, sl=sl, w0=w0, wn=wn: v.tensor_scalar(
                    out=tm1[:, :wn], in0=pk[sl][:, w0:w0 + wn], scalar1=4 * j, scalar2=7,
                    op0=ALU.logical_shift_right, op1=ALU.bitwise_and),
                    deps=[td] + pv, tag=f"{name}p{j}a", detach=DT)
                pv = []
                t2 = kb.op("dve", lambda v, wn=wn: v.tensor_scalar(
                    out=tm2[:, :wn], in0=tm1[:, :wn], scalar1=22, scalar2=0x3E000000,
                    op0=ALU.logical_shift_left, op1=ALU.bitwise_or), deps=[t1],
                    tag=f"{name}p{j}b", detach=DT)
                t3 = kb.op("dve", lambda v, wn=wn: v.tensor_scalar(
                    out=tu[:, :wn], in0=tm2[:, :wn].bitcast(F32), scalar1=8.0, scalar2=-1.0,
                    op0=ALU.mult, op1=ALU.add), deps=[t2], tag=f"{name}p{j}c", detach=DT)
                t4 = kb.op("act", lambda s, wn=wn: s.activation(
                    out=tq[:, :wn], in_=tm2[:, :wn].bitcast(F32), func=AF.Copy,
                    scale=4.0), deps=[t2], tag=f"{name}p{j}d", detach=DT)
                t5 = kb.op("dve", lambda v, wn=wn: v.tensor_tensor(
                    out=tq[:, :wn], in0=tu[:, :wn], in1=tq[:, :wn], op=ALU.min),
                    deps=[t3, t4], tag=f"{name}p{j}e", detach=DT)
                t6 = kb.op("dve", lambda v, j=j, sl=sl, w0=w0, wn=wn: v.tensor_scalar(
                    out=tsg[:, :wn], in0=pk[sl][:, w0:w0 + wn], scalar1=28 - 4 * j,
                    scalar2=-0x80000000, op0=ALU.logical_shift_left, op1=ALU.bitwise_and),
                    deps=[td], tag=f"{name}p{j}f", detach=DT)
                t7 = kb.op("dve", lambda v, j=j, wn=wn: v.tensor_tensor(
                    out=val[:].bitcast(I32)[:, j:8 * wn:8], in0=tq[:, :wn].bitcast(I32),
                    in1=tsg[:, :wn], op=ALU.bitwise_or), deps=[t5, t6],
                    tag=f"{name}p{j}g", detach=DT)
                plane_toks.append(t7)
            st["pk_tok"][sl] = [plane_toks[-1]]
            sc_toks = []
            g0 = (8 * w0) // 128
            for g in range(8 * wn // 128):
                t = kb.op("act", lambda s, g=g, g0=g0, sl=sl: s.activation(
                    out=vb[:, 128 * g:128 * g + 128], in_=val[:, 128 * g:128 * g + 128],
                    func=AF.Copy, scale=sct[sl][:, g0 + g:g0 + g + 1]),
                    deps=plane_toks + [tsc] + vb_rd, tag=f"{name}sc{g}", detach=DT)
                vb_rd = []
                sc_toks.append(t)
                val_rd.append(t)
                st["sct_rd"][sl].append(t)
            tdo = kb.dma("sp", s_out, lambda g, ot=ot, w0=w0, wn=wn: g.dma_start(
                out=out_loc[128 * ot:128 * ot + 128, 8 * w0:8 * w0 + 8 * wn],
                in_=vb[:, :8 * wn]), deps=sc_toks, tag=f"{name}odma{ot}_{wc}", detach=DT)
            vb_rd = [tdo]
            row_toks[ot].append(tdo)
        scratch["vb_rd"] = vb_rd
        scratch["val_rd"] = val_rd

    def emit_bulks():
        nsplit = 4 if O_half >= 4096 else 2
        Oq = O_half // nsplit
        opb = n_ot // nsplit
        bulk_toks = []
        for k in range(nsplit):
            s_bk = kb.dma_sem(f"{name}bk{k}")
            dep = [t for ot in range(k * opb, (k + 1) * opb) for t in row_toks[ot]]
            t = kb.dma("pool", s_bk, lambda gg, k=k: gg.dma_start(
                out=out_sh[bass.ds(gg.partition_id() % 2 * O_half + k * Oq, Oq), :],
                in_=out_loc[k * Oq:(k + 1) * Oq, :]), deps=dep,
                tag=f"{name}bulk{k}", detach=DT)
            bulk_toks.append(t)
        return bulk_toks

    if sliced:
        return [lambda ot=ot: emit_ot(ot) for ot in range(n_ot)], emit_bulks
    for ot in range(n_ot):
        emit_ot(ot)
    return emit_bulks()


from contextlib import ExitStack
from concourse.bass_utils import run_bass_kernel_spmd

_T = 1024
_NCORES = 8


def _build_fused():
    nc = bass.Bass("TRN2", target_bir_lowering=False,
                   detect_race_conditions=False)
    kb = KB(nc)
    F = F32
    d = {}
    d["x"] = nc.dram_tensor("x", [2048, _T + 3], F, kind="ExternalInput").ap()
    d["winT"] = nc.dram_tensor("winT", [2048, 8512], F, kind="ExternalInput").ap()
    d["cw"] = nc.dram_tensor("cw", [128, 34, 4], F, kind="ExternalInput").ap()
    d["cb"] = nc.dram_tensor("cb", [128, 34], F, kind="ExternalInput").ap()
    d["dtb"] = nc.dram_tensor("dtb", [64, 1], F, kind="ExternalInput").ap()
    d["a"] = nc.dram_tensor("a", [64, 1], F, kind="ExternalInput").ap()
    d["dd"] = nc.dram_tensor("dd", [1, 64], F, kind="ExternalInput").ap()
    d["idf"] = nc.dram_tensor("idf", [128, 128], F, kind="ExternalInput").ap()
    d["idneg"] = nc.dram_tensor("idneg", [128, 128], BF16, kind="ExternalInput").ap()
    d["ltc"] = nc.dram_tensor("ltc", [128, 128], BF16, kind="ExternalInput").ap()
    d["woT"] = nc.dram_tensor("woT", [4096, 2048], F, kind="ExternalInput").ap()
    d["nw"] = nc.dram_tensor("nw", [128, 32], F, kind="ExternalInput").ap()
    d["sel"] = nc.dram_tensor("sel", [128, 1], F, kind="ExternalInput").ap()
    d["nonce"] = nc.dram_tensor("nonce", [1, 48], I32, kind="ExternalInput").ap()
    w = {
        "router_w": nc.dram_tensor("router_w", [8, 2048], F, kind="ExternalInput").ap(),
        "w_gate": nc.dram_tensor("w_gate", [8, 2048, 1408], F, kind="ExternalInput").ap(),
        "w_up": nc.dram_tensor("w_up", [8, 2048, 1408], F, kind="ExternalInput").ap(),
        "w_down": nc.dram_tensor("w_down", [8, 1408, 2048], F, kind="ExternalInput").ap(),
        "sw_gate": nc.dram_tensor("sw_gate", [2048, 5632], F, kind="ExternalInput").ap(),
        "sw_up": nc.dram_tensor("sw_up", [2048, 5632], F, kind="ExternalInput").ap(),
        "sw_down": nc.dram_tensor("sw_down", [5632, 2048], F, kind="ExternalInput").ap(),
        "shared_gate_w": nc.dram_tensor("shared_gate_w", [16, 128], F, kind="ExternalInput").ap(),
    }
    d["wkvd"] = nc.dram_tensor("wkvd", [2048, 256], F, kind="ExternalInput").ap()
    d["gam"] = nc.dram_tensor("gam", [128, 2], F, kind="ExternalInput").ap()
    d["wq"] = nc.dram_tensor("wq", [2048, 2048], F, kind="ExternalInput").ap()
    d["wkvu"] = nc.dram_tensor("wkvu", [256, 4096], F, kind="ExternalInput").ap()
    d["wo"] = nc.dram_tensor("wo", [2048, 2048], F, kind="ExternalInput").ap()
    d["m01"] = nc.dram_tensor("m01", [128, 16, _T], BF16, kind="ExternalInput").ap()
    d["gup"] = nc.dram_tensor("gup", [5632, 256], I32, kind="ExternalInput").ap()
    d["gsc"] = nc.dram_tensor("gsc", [5632, 16], F16, kind="ExternalInput").ap()
    d["dnp"] = nc.dram_tensor("dnp", [1024, 704], I32, kind="ExternalInput").ap()
    d["dsc"] = nc.dram_tensor("dsc", [1024, 44], F16, kind="ExternalInput").ap()
    outT = nc.dram_tensor("outT", [2048, _T], F, kind="ExternalOutput").ap()
    yd = nc.dram_tensor("i_y", [_T, 4096], BF16, kind="Internal").ap()
    gd = nc.dram_tensor("i_gate", [_T, 4096], BF16, kind="Internal").ap()
    xbcd = nc.dram_tensor("i_xbc", [4352, _T], BF16, kind="Internal").ap()
    dacsd = nc.dram_tensor("i_dacs", [64, _T], F, kind="Internal").ap()
    dprodd = nc.dram_tensor("i_dprod", [64, 4], F, kind="Internal").ap()
    x1d = nc.dram_tensor("i_x1", [2048, _T], F, kind="Internal").ap()
    x2d = nc.dram_tensor("i_x2", [2048, _T], F, kind="Internal").ap()
    x3d = nc.dram_tensor("i_x3", [2048, _T], F, kind="Internal").ap()
    xlnd = nc.dram_tensor("i_xln", [2048, _T], BF16, kind="Internal").ap()
    s_sh = nc.dram_tensor("s_sh", [256, 4096], F, kind="Internal", addr_space="Shared").ap()
    lat_sh = nc.dram_tensor("lat_sh", [512, _T], BF16, kind="Internal", addr_space="Shared").ap()
    wgu_nat = nc.dram_tensor("wgu_nat", [11264, 2048], BF16, kind="Internal", addr_space="Shared").ap()
    wdn_nat = nc.dram_tensor("wdn_nat", [2048, 5632], BF16, kind="Internal", addr_space="Shared").ap()
    flgd = nc.dram_tensor("flgd", [2, 48], I32, kind="Internal", addr_space="Shared").ap()

    with ExitStack() as octx:
        pool = PsumPool(nc, octx)
        consts = Consts(kb, nc, octx, ident_dram=d["idf"])
        flags = FlagSync(kb, nc, octx, flgd, d["nonce"], 3)
        with ExitStack() as p1:
            a1 = emit_l0attn_a1(kb, nc, p1, consts, pool, d["x"], d["winT"], d["cw"],
                                d["cb"], d["dtb"], d["a"], d["dd"], yd, gd, xbcd,
                                dacsd, dprodd, None, T=_T, idneg_dram=d["idneg"],
                                ltc_dram=d["ltc"], sout_shared=s_sh)
            f0 = flags.write(0, [a1[-1]])
        kb.barrier()
        with ExitStack() as p2:
            w0 = flags.wait(0)
            a2 = emit_l0attn_a2(kb, nc, p2, consts, pool, d["x"][:, 3:_T + 3], yd, gd,
                                xbcd, dacsd, dprodd, None, d["woT"], d["nw"], x1d,
                                T=_T, sin_shared=s_sh, sel_dram=d["sel"], sin_deps=[w0])
        kb.barrier()
        with ExitStack() as p3:
            dqs = dequant_scratch(nc, p3)
            sl1, fin1 = emit_dequant_nat(kb, nc, p3, d["gup"], d["gsc"],
                                         wgu_nat, 5632, 256, "dg",
                                         sliced=True, detach=True, scratch=dqs)
            sl2, fin2 = emit_dequant_nat(kb, nc, p3, d["dnp"], d["dsc"],
                                         wdn_nat, 1024, 704, "dd",
                                         sliced=True, detach=True, scratch=dqs)
            sit = iter(sl1 + sl2)
            pcnt = [0]

            def pump():
                pcnt[0] += 1
                if pcnt[0] % 9 < 4:
                    s = next(sit, None)
                    if s is not None:
                        s()
            mf = emit_l0ffn(kb, nc, p3, consts, pool, x1d, w, x2d, T=_T, pump=pump)
            for s in sit:
                s()
            dqb = fin1() + fin2()
            f1 = flags.write(1, dqb)
            kb.op("sp", lambda sy: sy.nop(), deps=dqb, tag="dqjoin")
        kb.barrier()
        with ExitStack() as p4:
            la = emit_l1attn_lat(kb, nc, p4, consts, pool, x2d, d["wkvd"], d["gam"],
                                 xlnd, None, T=_T, lat_shared=lat_sh)
            f2 = flags.write(2, la)
        kb.barrier()
        with ExitStack() as p5:
            w2 = flags.wait(2)
            at = emit_l1attn_core(kb, nc, p5, consts, pool, xlnd, None, x2d,
                                  d["wq"], d["wkvu"], d["wo"], d["m01"], x3d,
                                  T=_T, L=2048, latseq_shared=lat_sh, lat_deps=[w2])
        kb.barrier()
        with ExitStack() as p6:
            w1 = flags.wait(1)
            d2 = emit_l1ffn(kb, nc, p6, consts, pool, x3d, wgu_nat, wdn_nat, outT,
                            T=_T, w_natural=True, w_deps=[w1])
            kb.finish(d2)
        kb.emit(octx)
    return nc


def _make_m01(q0):
    kb_i = np.arange(16)[None, :, None]
    k_i = np.arange(128)[:, None, None]
    q_i = np.arange(_T)[None, None, :]
    m = ((kb_i * 128 + k_i) <= (q0 + q_i)).astype(np.float32)
    return m.astype(ml_dtypes.bfloat16)


def kernel(**inp):
    cores = list(range(_NCORES))
    x = np.asarray(inp["L0_attn_hidden_states"])
    xt = np.ascontiguousarray(x.transpose(0, 2, 1))
    ident = np.eye(128, dtype=np.float32)
    idneg = (-1e30 * np.eye(128, dtype=np.float32)).astype(ml_dtypes.bfloat16)
    qq = np.arange(128)
    ltc = (qq[None, :] < qq[:, None]).astype(np.float32).astype(ml_dtypes.bfloat16)
    nonce = np.random.randint(1, 2 ** 31 - 1, size=(1, 48)).astype(np.int32)

    com = {
        "winT": np.ascontiguousarray(np.asarray(inp["L0_attn_in_proj_w"]).T),
        "cw": np.ascontiguousarray(np.asarray(inp["L0_attn_conv_w"]).reshape(34, 128, 4).transpose(1, 0, 2)),
        "cb": np.ascontiguousarray(np.asarray(inp["L0_attn_conv_b"]).reshape(34, 128).T),
        "dtb": np.asarray(inp["L0_attn_dt_bias"]).reshape(64, 1),
        "a": np.asarray(inp["L0_attn_A"]).reshape(64, 1),
        "dd": np.asarray(inp["L0_attn_D"]).reshape(1, 64),
        "idf": ident, "idneg": idneg, "ltc": ltc, "nonce": nonce,
        "woT": np.ascontiguousarray(np.asarray(inp["L0_attn_out_proj_w"]).T),
        "nw": np.ascontiguousarray(np.asarray(inp["L0_attn_norm_w"]).reshape(32, 128).T),
        "wkvd": np.asarray(inp["L1_attn_Wkv_down"]),
        "gam": np.ascontiguousarray(np.asarray(inp["L1_attn_kv_norm_w"]).reshape(2, 128).T),
        "wq": np.asarray(inp["L1_attn_Wq"]),
        "wkvu": np.asarray(inp["L1_attn_Wkv_up"]),
        "wo": np.asarray(inp["L1_attn_Wo"]),
    }
    for kk in ["router_w", "w_gate", "w_up", "w_down", "sw_gate", "sw_up", "sw_down"]:
        com[kk] = np.asarray(inp["L0_ffn_" + kk])
    com["shared_gate_w"] = np.asarray(inp["L0_ffn_shared_gate_w"]).reshape(16, 128)
    gup = np.asarray(inp["L1_ffn_gate_up_packed"])
    gsc = np.asarray(inp["L1_ffn_gate_up_scales"])
    dnp = np.asarray(inp["L1_ffn_down_packed"])
    dsc = np.asarray(inp["L1_ffn_down_scales"])

    nc = _build_fused()
    in_maps = []
    for k in cores:
        b, half = k // 2, k % 2
        halo = (np.zeros((2048, 3), np.float32) if half == 0 else xt[b][:, 1021:1024])
        m = dict(com)
        m["x"] = np.ascontiguousarray(
            np.concatenate([halo, xt[b][:, 1024 * half:1024 * half + 1024]], axis=1))
        m["sel"] = np.full((128, 1), float(half), np.float32)
        m["m01"] = _make_m01(half * _T)
        m["gup"] = np.ascontiguousarray(gup[half * 5632:(half + 1) * 5632])
        m["gsc"] = np.ascontiguousarray(gsc[:, half * 5632:(half + 1) * 5632].T)
        m["dnp"] = np.ascontiguousarray(dnp[half * 1024:(half + 1) * 1024])
        m["dsc"] = np.ascontiguousarray(dsc[:, half * 1024:(half + 1) * 1024].T)
        in_maps.append(m)
    res = run_bass_kernel_spmd(nc, in_maps, core_ids=cores)
    out = np.concatenate([res.results[k]["outT"].T for k in cores], axis=0)
    return np.ascontiguousarray(out.reshape(4, 2048, 2048)).astype(np.float32)
